# version 01
# baseline (speedup 1.0000x reference)
import numpy as np
import ml_dtypes
import concourse.bass as bass
import concourse.mybir as mybir
from concourse.bass_utils import run_bass_kernel_spmd

F32 = mybir.dt.float32
BF16 = mybir.dt.bfloat16
I32 = mybir.dt.int32
AF = mybir.ActivationFunctionType
ALU = mybir.AluOpType
AX = mybir.AxisListType

B, N, D, H, DH = 32, 2048, 1024, 8, 64
NCORES = 8
SEQ_PER_CORE = B // NCORES
NT = N // 128
NW = N // 512
DC = D // 128
INNER = 4096
ATTN_SCALE = float(DH) ** -0.5
RMS_EPS = float(np.finfo(np.float32).eps)
LN_EPS = 1e-5
QEPS = 1e-5

SERIAL = True
TAPS = ()
STOP_AFTER = None


class _Op:
    __slots__ = ("engine", "fn", "waits", "need_inc", "inc_amt", "eidx", "sem_val",
                 "tag", "is_dma", "semkey", "waited", "need_drain")

    def __init__(self, engine, fn, waits, inc_amt, eidx, tag, is_dma):
        self.engine = engine
        self.fn = fn
        self.waits = list(waits)
        self.need_inc = False
        self.inc_amt = inc_amt
        self.eidx = eidx
        self.sem_val = None
        self.tag = tag
        self.is_dma = is_dma
        self.semkey = engine + "_dma" if is_dma else engine
        self.waited = False
        self.need_drain = False


class Sched:
    ENGINES = ("tensor", "vector", "scalar", "sync", "gpsimd")

    def __init__(self, serial=False):
        self.ops = {e: [] for e in self.ENGINES}
        self.serial = serial
        self.order = []

    def emit(self, engine, fn, waits=(), dma=False, tag=""):
        ws = []
        for w in waits:
            if w is None or w is False:
                continue
            if not isinstance(w, _Op):
                raise TypeError(f"bad wait {w!r} at {tag}")
            if w.engine != engine:
                ws.append(w)
            elif engine == "vector":
                w.need_drain = True
        op = _Op(engine, fn, ws, 16 if dma else 1, len(self.ops[engine]), tag, dma)
        if dma:
            op.need_inc = True
        self.ops[engine].append(op)
        self.order.append(op)
        for w in op.waits:
            w.need_inc = True
            w.waited = True
        return op

    def t(self, fn, waits=(), tag=""):
        return self.emit("tensor", fn, waits, tag=tag)

    def v(self, fn, waits=(), tag=""):
        return self.emit("vector", fn, waits, tag=tag)

    def s(self, fn, waits=(), tag=""):
        return self.emit("scalar", fn, waits, tag=tag)

    def q(self, fn, waits=(), tag=""):
        return self.emit("sync", fn, waits, dma=True, tag=tag)

    def g(self, fn, waits=(), dma=True, tag=""):
        return self.emit("gpsimd", fn, waits, dma=dma, tag=tag)

    def finalize(self, nc):
        if self.serial:
            prev = None
            for op in self.order:
                if prev is not None and prev.engine != op.engine:
                    op.waits.append(prev)
                    prev.need_inc = True
                    prev.waited = True
                prev = op
        semkeys = set()
        for e in self.ENGINES:
            for op in self.ops[e]:
                semkeys.add(op.semkey)
        sems = {k: nc.alloc_semaphore(f"sem_{k}") for k in sorted(semkeys)}
        with nc.Block() as b0:
            @b0.gpsimd
            def _(g):
                for k in sorted(sems):
                    g.sem_clear(sems[k])
        nc.all_engine_barrier()
        totals = {k: 0 for k in semkeys}
        for e in self.ENGINES:
            cum = {}
            for op in self.ops[e]:
                if op.need_inc:
                    cum[op.semkey] = cum.get(op.semkey, 0) + op.inc_amt
                    op.sem_val = cum[op.semkey]
            for k, vv in cum.items():
                totals[k] = vv

        def replay(engine_obj, ename):
            for op in self.ops[ename]:
                best = {}
                for w in op.waits:
                    if w.sem_val is None:
                        raise RuntimeError(f"wait target lacks inc: {w.tag}")
                    if w.sem_val > best.get(w.semkey, -1):
                        best[w.semkey] = w.sem_val
                for we in sorted(best):
                    engine_obj.wait_ge(sems[we], best[we])
                inst = op.fn(engine_obj)
                if op.engine == "vector" and (op.need_inc or op.need_drain):
                    inst = engine_obj.drain()
                if op.need_inc:
                    if inst is None:
                        raise RuntimeError(f"op needs inc but returned None: {op.tag}")
                    inst.then_inc(sems[op.semkey], op.inc_amt)
                if op.is_dma and op.waited:
                    engine_obj.wait_ge(sems[op.semkey], op.sem_val)

        with nc.Block() as block:
            @block.tensor
            def _(t):
                replay(t, "tensor")

            @block.vector
            def _(v):
                replay(v, "vector")

            @block.scalar
            def _(s):
                replay(s, "scalar")

            @block.gpsimd
            def _(g):
                replay(g, "gpsimd")

            @block.sync
            def _(s):
                replay(s, "sync")
                for k in sorted(totals):
                    if totals[k] and k != "sync_dma":
                        s.wait_ge(sems[k], totals[k])
                if totals.get("sync_dma"):
                    s.wait_ge(sems["sync_dma"], totals["sync_dma"])


class Bank:

    def __init__(self, ap):
        self.ap = ap
        self.readers = []
        self.writer = None

    def wwaits(self):
        return list(self.readers)

    def set_writer(self, op):
        self.writer = op
        self.readers = []

    def rwaits(self):
        return [self.writer] if self.writer is not None else []

    def add_reader(self, op):
        self.readers.append(op)


class Slot:

    def __init__(self):
        self.consumers = []
        self.producer = None

    def pwaits(self):
        return list(self.consumers)

    def produced(self, op):
        self.producer = op
        self.consumers = []

    def cwaits(self):
        return [self.producer] if self.producer is not None else []

    def consumed(self, op):
        self.consumers.append(op)


def _host_consts():
    c = {}
    c["c_ident"] = np.eye(128, dtype=np.float32)
    c["c_ones"] = np.ones((128, 128), dtype=np.float32)
    masks = np.zeros((4, 128, 512), dtype=np.float32)
    jj = np.arange(128)[:, None]
    ii = np.arange(512)[None, :]
    for r, rel in enumerate((0, 128, 256, 384)):
        masks[r] = (ii >= rel + jj).astype(np.float32)
    c["c_masks"] = masks.reshape(4 * 128, 512)
    selr = np.zeros((4, 128, 128), dtype=np.float32)
    for p in range(4):
        selr[p, 32 * ((2 * p) % 4), 0:64] = 1.0
        selr[p, 32 * ((2 * p + 1) % 4), 64:128] = 1.0
    c["c_selr"] = selr.transpose(1, 0, 2).reshape(128, 4 * 128)
    selg = np.zeros((128, 4 * 128), dtype=np.float32)
    for p in range(4):
        selg[2 * p, p * 128: p * 128 + 64] = 1.0
        selg[2 * p + 1, p * 128 + 64: p * 128 + 128] = 1.0
    c["c_selg"] = selg
    return c


def build_program(taps=(), serial=False, stop_after=None):
    nc = bass.Bass("TRN2", target_bir_lowering=False)
    sch = Sched(serial=serial)
    taps = set(taps)

    x_d = nc.dram_tensor("x", [SEQ_PER_CORE, N, D], F32, kind="ExternalInput")
    wq_d = nc.dram_tensor("wq", [H, D, DH], F32, kind="ExternalInput")
    wk_d = nc.dram_tensor("wk", [H, D, DH], F32, kind="ExternalInput")
    wv_d = nc.dram_tensor("wv", [H, D, DH], F32, kind="ExternalInput")
    wo_d = nc.dram_tensor("wo", [H, DH, D], F32, kind="ExternalInput")
    wgates_d = nc.dram_tensor("wgates", [D, H], F32, kind="ExternalInput")
    rmsw_d = nc.dram_tensor("rmsw", [1, D], F32, kind="ExternalInput")
    wgate_d = nc.dram_tensor("wgate", [INNER, D], F32, kind="ExternalInput")
    wup_d = nc.dram_tensor("wup", [INNER, D], F32, kind="ExternalInput")
    wdown_d = nc.dram_tensor("wdown", [D, INNER], F32, kind="ExternalInput")
    cident_d = nc.dram_tensor("c_ident", [128, 128], F32, kind="ExternalInput")
    cones_d = nc.dram_tensor("c_ones", [128, 128], F32, kind="ExternalInput")
    cmask_d = nc.dram_tensor("c_masks", [4 * 128, 512], F32, kind="ExternalInput")
    cselr_d = nc.dram_tensor("c_selr", [128, 4 * 128], F32, kind="ExternalInput")
    cselg_d = nc.dram_tensor("c_selg", [128, 4 * 128], F32, kind="ExternalInput")

    out_d = nc.dram_tensor("out", [SEQ_PER_CORE, N, D], F32, kind="ExternalOutput")
    y1_d = nc.dram_tensor("y1_scr", [N, D], F32, kind="Internal")
    wgT_d = nc.dram_tensor("wgT_scr", [D, INNER], BF16, kind="Internal")
    scal_d = nc.dram_tensor("scal_scr", [1, 8], F32, kind="Internal")
    col_d = nc.dram_tensor("col_scr", [128, 8], F32, kind="Internal")
    wuT_d = nc.dram_tensor("wuT_scr", [D, INNER], BF16, kind="Internal")
    wdT_d = nc.dram_tensor("wdT_scr", [INNER, D], BF16, kind="Internal")

    tap_d = {}

    def mktap(name, shape, dtype=F32):
        tap_d[name] = nc.dram_tensor("tap_" + name, list(shape), dtype, kind="ExternalOutput")
        return tap_d[name]

    U8 = mybir.dt.uint8
    ARENA_B = 211200
    arena = nc.alloc_sbuf_tensor("arena", [128, ARENA_B], U8)
    _dtsz = {F32: 4, BF16: 2, I32: 4, U8: 1}
    _cur = {"static": 0, "phase": None}

    def AR(name, shape, dtype, phase="static"):
        nb = shape[1] * _dtsz[dtype]
        nb = (nb + 63) & ~63
        if phase == "static":
            off = _cur["static"]
            _cur["static"] += nb
            assert _cur["static"] <= ARENA_B, f"static overflow at {name}"
        else:
            off = _cur[phase]
            _cur[phase] += nb
            assert _cur[phase] <= ARENA_B, f"{phase} overflow at {name} ({_cur[phase]})"
        return arena[0:shape[0], off:off + shape[1] * _dtsz[dtype]].bitcast(dtype)

    ident_f = AR("ident_f", [128, 128], F32)
    ident_b = AR("ident_b", [128, 128], BF16)
    ones_f = AR("ones_f", [128, 128], F32)
    ones_b = AR("ones_b", [128, 128], BF16)
    masks_b = AR("masks_b", [128, 4 * 512], BF16)
    selr_f = AR("selr_f", [128, 4 * 128], F32)
    selg_f = AR("selg_f", [128, 4 * 128], BF16)
    rmsw_rep = AR("rmsw_rep", [128, D], F32)
    attn_scr = AR("attn_scr", [128, 192], F32)
    zrow_b = AR("zrow_b", [1, 512], BF16)
    x_tile = AR("x_tile", [128, 2 * D], F32)
    sq_scr = AR("sq_scr", [128, D], F32)
    dump_scr = AR("dump_scr", [128, D], F32)
    stat = AR("stat", [128, 64], F32)
    meanw_sb = AR("meanw_sb", [128, 32], F32)
    base = _cur["static"]
    _cur["attn"] = base
    _cur["ffn"] = base

    wq_l = AR("wq_l", [128, DC * 512], BF16, "attn")
    wk_l = AR("wk_l", [128, DC * 512], BF16, "attn")
    wv_l = AR("wv_l", [128, DC * 512], BF16, "attn")
    wgates_l = AR("wgates_l", [128, DC * 8], BF16, "attn")
    wo_l = AR("wo_l", [128, 4 * D], BF16, "attn")
    tT = AR("tT", [128, DC * N], BF16, "attn")
    qT = AR("qT", [128, 4 * N], BF16, "attn")
    kT = AR("kT", [128, 4 * N], BF16, "attn")
    v_sb = AR("v_sb", [128, NT * 512], BF16, "attn")
    gT_sb = AR("gT_sb", [128, N], BF16, "attn")
    t_tile = AR("t_tile", [128, 2 * D], BF16, "attn")
    nm_wide = AR("nm_wide", [128, 4 * 256], F32, "attn")
    corrT_sb = AR("corrT_sb", [128, 2 * 512], F32, "attn")
    pT_sb = AR("pT_sb", [128, 4 * 512], BF16, "attn")
    rl_sb = AR("rl_sb", [128, 2 * 512], F32, "attn")
    tg_sb = AR("tg_sb", [128, 2 * 512], F32, "attn")
    outT_sb = AR("outT_sb", [128, 4 * 512], BF16, "attn")
    y1_stage = AR("y1_stage", [128, 2 * D], F32, "attn")

    wgT_sb = AR("wgT_sb", [128, DC * INNER], BF16, "ffn")
    wuT_sb = AR("wuT_sb", [128, DC * INNER], BF16, "ffn")
    y1t = AR("y1t", [128, 2 * D], F32, "ffn")
    xqi = AR("xqi", [128, 512], I32, "ffn")
    xqb = AR("xqb", [128, D], BF16, "ffn")
    xqT = AR("xqT", [128, DC * 128], BF16, "ffn")
    sg_sb = AR("sg_sb", [128, 512], F32, "ffn")
    h7_sb = AR("h7_sb", [128, 512], F32, "ffn")
    xq2b = AR("xq2b", [128, 512], BF16, "ffn")
    xq2T = AR("xq2T", [128, 2 * 32 * 128], BF16, "ffn")
    wdb = AR("wdb", [128, 2 * D], BF16, "ffn")
    o_stage = AR("o_stage", [128, 2 * D], F32, "ffn")
    assert max(_cur["attn"], _cur["ffn"]) <= ARENA_B

    P = nc.alloc_psum_tensor
    psA = P("psA", [128, 2048], F32)
    psB = P("psB", [128, 2048], F32)
    banks = [Bank(psA[:, i * 512:(i + 1) * 512]) for i in range(4)] + \
            [Bank(psB[:, i * 512:(i + 1) * 512]) for i in range(4)]

    ld_ident = sch.q(lambda e: e.dma_start(ident_f[:], cident_d.ap()), tag="ident")
    ld_identb = sch.g(lambda e: e.dma_start(ident_b[:], cident_d.ap()), tag="identb")
    ld_ones = sch.q(lambda e: e.dma_start(ones_f[:], cones_d.ap()), tag="ones")
    ld_masks = sch.g(lambda e: e.dma_start(
        masks_b[:].rearrange("p (r f) -> p r f", r=4),
        cmask_d.ap().rearrange("(r p) f -> p r f", r=4)), tag="masks")
    ld_selr = sch.q(lambda e: e.dma_start(selr_f[:], cselr_d.ap()), tag="selr")
    ld_selg = sch.g(lambda e: e.dma_start(selg_f[:], cselg_d.ap()), tag="selg")
    ld_rmsw = sch.g(lambda e: e.dma_start(
        rmsw_rep[:], rmsw_d.ap().broadcast_to([128, D])), tag="rmsw")
    ld_onesb = sch.g(lambda e: e.dma_start(ones_b[:], cones_d.ap()), tag="onesb")
    ld_zrow = sch.g(lambda e: e.memset(zrow_b[:], 0.0), dma=False, tag="zrow")

    def load_attn_weights(deps, s):
        ops = []
        for dst, src_d, pariter, nm2 in ((wq_l, wq_d, True, "wq"), (wk_l, wk_d, True, "wk"),
                                         (wv_l, wv_d, False, "wv")):
            for h in range(H):
                coloff = ((h // 2) * 128 + (h % 2) * 64) if pariter else h * 64
                dst_ap = dst.rearrange("p (dc c) -> p dc c", dc=DC)[:, :, coloff:coloff + 64]
                src_ap = src_d[h].rearrange("(dc p) e -> p dc e", p=128)
                ops.append(sch.g(lambda e, d=dst_ap, s2=src_ap: e.dma_start(d, s2),
                                 waits=deps if (h == 0 and nm2 == "wq") else [],
                                 tag=f"attw{s}_{nm2}{h}"))
        ops.append(sch.g(lambda e: e.dma_start(
            wgates_l.rearrange("p (dc h) -> p dc h", dc=DC),
            wgates_d.ap().rearrange("(dc p) h -> p dc h", p=128)),
            waits=deps, tag=f"wgates{s}"))
        ops.append(sch.g(lambda e: e.dma_start(
            wo_l.rearrange("p (pc d) -> p pc d", pc=4),
            wo_d.ap().rearrange("(pc hh) e d -> (hh e) pc d", hh=2)),
            waits=deps, tag=f"wo{s}"))
        return ops

    def rsqrt_col(dep, src_col, dst_col, eps, tag, scratch_col=9):
        eps_ap = _cap(eps)
        o1 = sch.s(lambda e, eps_ap=eps_ap: e.activation(
            stat[:, scratch_col:scratch_col + 1], stat[:, src_col:src_col + 1], AF.Ln,
            bias=eps_ap), waits=[dep], tag=tag + "_ln")
        o2 = sch.s(lambda e: e.activation(
            stat[:, dst_col:dst_col + 1], stat[:, scratch_col:scratch_col + 1],
            AF.Exp, scale=-0.5), waits=[o1], tag=tag + "_ex")
        return o2

    _const_cols = {}

    def _cap(val):
        if val not in _const_cols:
            idx = 56 + len(_const_cols)
            assert idx < 64
            _const_cols[val] = idx
            sch.g(lambda e, idx=idx, val=val: e.memset(stat[:, idx:idx + 1], float(val)),
                  dma=False, tag=f"const{val}")
        idx = _const_cols[val]
        return stat[:, idx:idx + 1]

    wprep_scr = tT[:].bitcast(F32)[:, 0:INNER]
    wprep_i = qT[:].bitcast(I32)[:, 0:INNER]
    abs_scr = kT[:].bitcast(F32)[:, 0:INNER]
    wprep_q = v_sb[:, 0:INNER]
    wprep_Ts = outT_sb[:, 0:1024]

    def prep_ffn_weight(src_d, dst_T_d, rows, cols, name):
        RT = rows // 128
        sum_ops = []
        scr = Slot()
        for r in range(RT):
            ld = sch.q(lambda e, r=r: e.dma_start(wprep_scr[:, 0:cols],
                                                  src_d[r * 128:(r + 1) * 128, :]),
                       waits=scr.pwaits(), tag=f"{name}ld{r}")
            scr.produced(ld)
            so = sch.s(lambda e, r=r: e.activation(
                abs_scr[:, 0:cols], wprep_scr[:, 0:cols], AF.Abs,
                accum_out=stat[:, r:r + 1]), waits=[ld], tag=f"{name}abs{r}")
            scr.consumed(so)
            sum_ops.append(so)
        st_free = sch.s(lambda e, RT=RT: e.activation(
            abs_scr[:, 0:RT], stat[:, 0:RT], AF.Copy, accum_out=stat[:, 32:33]),
            waits=[sum_ops[-1]], tag=f"{name}stf")
        if name == "Pg" and "pstat0" in taps:
            td = mktap("pstat0", [128, 64])
            sch.q(lambda e, td=td: e.dma_start(td.ap(), stat[:]),
                  waits=[st_free], tag="tap_pstat0")
        g0 = sch.q(lambda e: e.dma_start(col_d.ap(), stat[:, 32:40]),
                   waits=[st_free], tag=f"{name}pg0")
        g1 = sch.q(lambda e: e.dma_start(sq_scr[0:1, 0:1024],
                                         col_d.ap().rearrange("p o -> (p o)").unsqueeze(0)),
                   waits=[g0], tag=f"{name}pg")
        rr = sch.v(lambda e: e.reduce_sum(
            stat[0:1, 33:34],
            sq_scr[0:1, 0:1024].rearrange("o (p c) -> o p c", c=8)[:, :, 0:1],
            axis=AX.XY), waits=[g1], tag=f"{name}rr")
        midx = {"Pg": 0, "Pu": 1, "Pd": 2}[name]
        s1 = sch.q(lambda e, midx=midx: e.dma_start(
            scal_d[0:1, midx:midx + 1], stat[0:1, 33:34]),
            waits=[rr], tag=f"{name}s1")
        bcast = sch.q(lambda e, midx=midx: e.dma_start(
            stat[:, 33:34], scal_d[0:1, midx:midx + 1].broadcast_to([128, 1])),
            waits=[s1], tag=f"{name}bc")
        c1 = sch.v(lambda e: e.tensor_scalar(
            stat[:, 41:49], stat[:, 33:34].broadcast_to([128, 8]),
            1.0 / (rows * cols), QEPS,
            op0=ALU.mult, op1=ALU.max), waits=[bcast], tag=f"{name}clip")
        c2a = sch.s(lambda e: e.activation(stat[:, 49:50], stat[:, 41:42], AF.Ln),
                    waits=[c1], tag=f"{name}rln")
        c2 = sch.s(lambda e: e.activation(stat[:, 50:51], stat[:, 49:50], AF.Exp,
                                          scale=-1.0), waits=[c2a], tag=f"{name}rex")
        if name == "Pg" and "pstat1" in taps:
            td = mktap("pstat1", [128, 64])
            sch.q(lambda e, td=td: e.dma_start(td.ap(), stat[:]), waits=[c2],
                  tag="tap_pstat1")

        last_stores = []
        tb = banks[3]
        qslot = Slot()
        for r in range(RT):
            ld = sch.q(lambda e, r=r: e.dma_start(wprep_scr[:, 0:cols],
                                                  src_d[r * 128:(r + 1) * 128, :]),
                       waits=[c2] + scr.pwaits(), tag=f"{name}ld2_{r}")
            scr.produced(ld)
            qi = sch.v(lambda e: e.tensor_scalar_mul(
                wprep_i[:, 0:cols], wprep_scr[:, 0:cols], stat[:, 50:51]),
                waits=[ld], tag=f"{name}rnd{r}")
            scr.consumed(qi)
            qc = sch.v(lambda e: e.tensor_scalar(
                wprep_q[:, 0:cols], wprep_i[:, 0:cols], -1.0, 1.0,
                op0=ALU.max, op1=ALU.min), waits=[qi] + qslot.pwaits(), tag=f"{name}clp{r}")
            qslot.produced(qc)
            CT = cols // 128
            for ct0 in range(0, CT, 8):
                nct = min(8, CT - ct0)
                tms = []
                for k in range(nct):
                    ct = ct0 + k
                    tm = sch.t(lambda e, ct=ct, k=k: e.transpose(
                        tb.ap[:].bitcast(BF16)[:, k * 128:(k + 1) * 128],
                        wprep_q[:, ct * 128:(ct + 1) * 128], ident_b[:]),
                        waits=[qc, ld_identb] + (tb.wwaits() if k == 0 else []),
                        tag=f"{name}T{r}_{ct}")
                    tms.append(tm)
                tb.set_writer(tms[-1])
                qslot.consumed(tms[-1])
                ev = sch.v(lambda e, nct=nct: e.tensor_copy(
                    wprep_Ts[:, 0:nct * 128], tb.ap[:].bitcast(BF16)[:, 0:nct * 128]),
                    waits=tb.rwaits(), tag=f"{name}Tev{r}_{ct0}")
                tb.add_reader(ev)
                stv = sch.q(lambda e, r=r, ct0=ct0, nct=nct: e.dma_start(
                    dst_T_d[ct0 * 128:ct0 * 128 + nct * 128, r * 128:(r + 1) * 128]
                        .rearrange("(n p) f -> p n f", p=128),
                    wprep_Ts[:, 0:nct * 128].rearrange("p (n f) -> p n f", f=128)),
                    waits=[ev], tag=f"{name}st{r}_{ct0}")
                last_stores.append(stv)
        return last_stores

    if stop_after not in ("consts",):
        wprep_done = []
        wprep_done += prep_ffn_weight(wgate_d, wgT_d, INNER, D, "Pg")
        if "pstat" in taps:
            td = mktap("pstat", [128, 64])
            sch.q(lambda e, td=td: e.dma_start(td.ap(), stat[:]), waits=wprep_done,
                  tag="tap_pstat")
        if stop_after != "prep1":
            wprep_done += prep_ffn_weight(wup_d, wuT_d, INNER, D, "Pu")
            if stop_after != "prep2":
                wprep_done += prep_ffn_weight(wdown_d, wdT_d, D, INNER, "Pd")
        if "wgT" in taps:
            td = mktap("wgT", [D, INNER], BF16)
            sch.q(lambda e, td=td: e.dma_start(td.ap(), wgT_d.ap()), waits=wprep_done, tag="tapwgT")
        if "wdT" in taps:
            td = mktap("wdT", [INNER, D], BF16)
            sch.q(lambda e, td=td: e.dma_start(td.ap(), wdT_d.ap()), waits=wprep_done, tag="tapwdT")

    prev_seq_done = []

    def phase_attn(s):
        nonlocal prev_seq_done
        ld_attw = load_attn_weights(list(prev_seq_done), s)
        zrl = sch.g(lambda e: e.memset(rl_sb[:], 0.0), dma=False,
                    waits=list(prev_seq_done), tag=f"zrl{s}")
        deps = list(prev_seq_done) + ld_attw + [ld_identb, ld_rmsw, zrl]

        xslot = [Slot(), Slot()]
        tslot = [Slot(), Slot()]
        a1_done = []
        tb = banks[0]
        for ti in range(NT):
            w = ti % 2
            xa = x_tile[:, w * D:(w + 1) * D]
            ta = t_tile[:, w * D:(w + 1) * D]
            ld = sch.q(lambda e, s=s, ti=ti, xa=xa: e.dma_start(
                xa, x_d[s, ti * 128:(ti + 1) * 128, :]),
                waits=deps + xslot[w].pwaits(), tag=f"x{s}_{ti}")
            xslot[w].produced(ld)
            sq = sch.s(lambda e, xa=xa: e.activation(
                sq_scr[:], xa, AF.Square, accum_out=stat[:, 0:1]),
                waits=[ld], tag=f"sq{s}_{ti}")
            st1 = sch.v(lambda e: e.tensor_scalar(
                stat[:, 1:9], stat[:, 0:1].broadcast_to([128, 8]), 1.0 / D, None,
                op0=ALU.mult), waits=[sq], tag=f"rmsm{s}_{ti}")
            rs = rsqrt_col(st1, 1, 10, RMS_EPS, f"rms{s}_{ti}")
            tt = sch.v(lambda e, xa=xa, ta=ta: e.scalar_tensor_tensor(
                ta, xa, stat[:, 10:11], rmsw_rep[:], op0=ALU.mult, op1=ALU.mult),
                waits=[rs] + tslot[w].pwaits(), tag=f"t{s}_{ti}")
            xslot[w].consumed(tt)
            tslot[w].produced(tt)
            tms = []
            for dc in range(DC):
                tm = sch.t(lambda e, dc=dc, ta=ta: e.transpose(
                    tb.ap[:].bitcast(BF16)[:, dc * 128:(dc + 1) * 128],
                    ta[:, dc * 128:(dc + 1) * 128], ident_b[:]),
                    waits=[tt] + (tb.wwaits() if dc == 0 else []), tag=f"tT{s}_{ti}_{dc}")
                tms.append(tm)
            tb.set_writer(tms[-1])
            ev = sch.v(lambda e, ti=ti: e.tensor_copy(
                tT[:].rearrange("p (dc n) -> p dc n", dc=DC)
                    [:, :, ti * 128:(ti + 1) * 128],
                tb.ap[:].bitcast(BF16)[:, 0:1024].rearrange("p (dc n) -> p dc n", dc=DC)),
                waits=tb.rwaits(), tag=f"tTev{s}_{ti}")
            tb.add_reader(ev)
            tslot[w].consumed(ev)
            a1_done.append(ev)

        if "tT" in taps and s == 0:
            td = mktap("tT", [128, DC * N], BF16)
            a1_done.append(sch.q(lambda e, td=td: e.dma_start(td.ap(), tT[:]),
                                 waits=list(a1_done), tag="tap_tT"))

        a2_done = []
        for dst, wl, nm in ((qT, wq_l, "q"), (kT, wk_l, "k")):
            for c in range(4):
                for nb in range(4):
                    b = banks[2 + (nb + c * 4) % 2]
                    mms = []
                    for dc in range(DC):
                        mm = sch.t(lambda e, c=c, nb=nb, dc=dc, wl=wl, b=b: e.matmul(
                            b.ap[:, 0:512],
                            wl[:, dc * 512 + c * 128: dc * 512 + (c + 1) * 128],
                            tT[:, dc * N + nb * 512: dc * N + (nb + 1) * 512],
                            start=(dc == 0), stop=(dc == DC - 1)),
                            waits=(a1_done + b.wwaits()) if dc == 0 else [],
                            tag=f"{nm}mm{s}_{c}_{nb}_{dc}")
                        mms.append(mm)
                    b.set_writer(mms[-1])
                    ev = sch.v(lambda e, c=c, nb=nb, dst=dst, b=b: e.tensor_copy(
                        dst[:, c * N + nb * 512: c * N + (nb + 1) * 512], b.ap[:, 0:512]),
                        waits=b.rwaits(), tag=f"{nm}ev{s}_{c}_{nb}")
                    b.add_reader(ev)
                    a2_done.append(ev)
        v_done = []
        for ti in range(NT):
            b = banks[2 + ti % 2]
            mms = []
            for dc in range(DC):
                mm = sch.t(lambda e, ti=ti, dc=dc, b=b: e.matmul(
                    b.ap[:, 0:512],
                    tT[:, dc * N + ti * 128: dc * N + (ti + 1) * 128],
                    wv_l[:, dc * 512:(dc + 1) * 512],
                    start=(dc == 0), stop=(dc == DC - 1)),
                    waits=(a1_done + b.wwaits()) if dc == 0 else [],
                    tag=f"vmm{s}_{ti}_{dc}")
                mms.append(mm)
            b.set_writer(mms[-1])
            ev = sch.v(lambda e, ti=ti, b=b: e.tensor_copy(
                v_sb[:, ti * 512:(ti + 1) * 512], b.ap[:, 0:512]),
                waits=b.rwaits(), tag=f"vev{s}_{ti}")
            b.add_reader(ev)
            v_done.append(ev)
            a2_done.append(ev)
        g_done = []
        for nb in range(4):
            b = banks[4 + nb % 2]
            mms = []
            for dc in range(DC):
                mm = sch.t(lambda e, nb=nb, dc=dc, b=b: e.matmul(
                    b.ap[0:8, 0:512],
                    wgates_l[:, dc * 8:(dc + 1) * 8],
                    tT[:, dc * N + nb * 512: dc * N + (nb + 1) * 512],
                    start=(dc == 0), stop=(dc == DC - 1)),
                    waits=(a1_done + b.wwaits()) if dc == 0 else [],
                    tag=f"gmm{s}_{nb}_{dc}")
                mms.append(mm)
            b.set_writer(mms[-1])
            sg = sch.s(lambda e, nb=nb, b=b: e.activation(
                gT_sb[0:8, nb * 512:(nb + 1) * 512], b.ap[0:8, 0:512], AF.Sigmoid),
                waits=b.rwaits(), tag=f"gsig{s}_{nb}")
            b.add_reader(sg)
            g_done.append(sg)
            a2_done.append(sg)

        if s == 0:
            for nm, buf, shape in (("qT", qT, [128, 4 * N]), ("kT", kT, [128, 4 * N]),
                                   ("v", v_sb, [128, NT * 512])):
                if nm in taps:
                    td = mktap(nm, shape, BF16)
                    a2_done.append(sch.q(lambda e, td=td, buf=buf: e.dma_start(td.ap(), buf[:]),
                                         waits=list(a2_done), tag=f"tap_{nm}"))
            if "gT" in taps:
                td = mktap("gT", [128, N], BF16)
                a2_done.append(sch.q(lambda e, td=td: e.dma_start(td.ap(), gT_sb[:]),
                                     waits=list(g_done), tag="tap_gT"))

        pT = [Slot() for _ in range(4)]
        y1slot = [Slot(), Slot()]
        outT_slot = Slot()
        rsq = Slot()
        xr = [Slot(), Slot()]
        a4_done = []
        sbk = [banks[0], banks[1]]
        lbk = [banks[2], banks[3]]
        pvb = [banks[4], banks[5], banks[6], banks[7]]
        HROW = [32 * (h % 4) for h in range(8)]

        for w in range(NW):
            nm_ready = {}
            jw = (w + 1) * 512
            for k in range(4):
                it = 4 * w + k
                for p in range(4):
                    hA, hB = 2 * p, 2 * p + 1
                    mmsA, mmsB = [], []
                    for jc5 in range(w + 1):
                        bA, bB = banks[jc5], banks[4 + jc5]
                        mmA = sch.t(lambda e, p=p, it=it, jc5=jc5, bA=bA: e.matmul(
                            bA.ap[:, 0:512],
                            qT[0:64, p * N + it * 128: p * N + (it + 1) * 128],
                            kT[0:64, p * N + jc5 * 512: p * N + (jc5 + 1) * 512],
                            start=True, stop=True, tile_position=(0, 0)),
                            waits=(a2_done if (w == 0 and k == 0 and p == 0 and jc5 == 0)
                                   else []) + bA.wwaits(),
                            tag=f"S{s}w{w}k{k}p{p}c{jc5}A")
                        bA.set_writer(mmA)
                        mmsA.append((bA, mmA))
                        mmB = sch.t(lambda e, p=p, it=it, jc5=jc5, bB=bB: e.matmul(
                            bB.ap[:, 0:512],
                            qT[64:128, p * N + it * 128: p * N + (it + 1) * 128],
                            kT[64:128, p * N + jc5 * 512: p * N + (jc5 + 1) * 512],
                            start=True, stop=True, tile_position=(64, 0)),
                            waits=bB.wwaits(), tag=f"S{s}w{w}k{k}p{p}c{jc5}B")
                        bB.set_writer(mmB)
                        mmsB.append((bB, mmB))
                    for h, reg, mml_ in ((hA, psA, mmsA), (hB, psB, mmsB)):
                        slot = k * 256 + (h // 4) * 128 + HROW[h]
                        nm = sch.v(lambda e, reg=reg, slot=slot, jw=jw: e.tensor_scalar(
                            pT_sb[:, 0:jw], reg[:, 0:jw],
                            -1.0, None, op0=ALU.mult, op1=ALU.min,
                            accum_out=nm_wide[:, slot:slot + 1]),
                            waits=[m for _, m in mml_] + sum(
                                (pT[i].pwaits() for i in range(4)), []),
                            tag=f"nm{s}w{w}k{k}h{h}")
                        for bk, _ in mml_:
                            bk.add_reader(nm)
                        nm_ready[(k, h)] = nm
            for i in range(4):
                pT[i].produced(list(nm_ready.values())[-1])
            if s == 0 and w == 0 and "pmax0" in taps:
                td = mktap("pmax0", [128, 192])
                sch.q(lambda e, td=td: e.dma_start(td.ap(), attn_scr[:]),
                      waits=list(nm_ready.values()), tag="tap_pmax0")
            corr_evs = []
            for g in range(2):
                cb = pvb[g]
                tms = []
                for k in range(4):
                    base = k * 256 + g * 128
                    deps = [nm_ready[(k, h)] for h in range(4 * g, 4 * g + 4)]
                    tm = sch.t(lambda e, base=base, k=k, cb=cb: e.transpose(
                        cb.ap[0:97, k * 128:(k + 1) * 128],
                        nm_wide[:, base:base + 97], ident_f[:]),
                        waits=deps + [ld_ident] + (cb.wwaits() if k == 0 else []),
                        tag=f"cT{s}w{w}g{g}k{k}")
                    tms.append(tm)
                cb.set_writer(tms[-1])
                ev = sch.v(lambda e, g=g, cb=cb: e.tensor_copy(
                    corrT_sb[:, g * 512:(g + 1) * 512], cb.ap[:, 0:512]),
                    waits=cb.rwaits(), tag=f"cTe{s}w{w}g{g}")
                cb.add_reader(ev)
                corr_evs.append(ev)
            if s == 0 and w == 0 and "nm0" in taps:
                td = mktap("nm0", [128, 4 * 256])
                sch.q(lambda e, td=td: e.dma_start(td.ap(), nm_wide[:]),
                      waits=list(nm_ready.values()), tag="tap_nm0")
                td2 = mktap("corrT0", [128, 1024])
                sch.q(lambda e, td2=td2: e.dma_start(td2.ap(), corrT_sb[:]),
                      waits=corr_evs, tag="tap_corrT0")
            zb_ops = []
            for bk in (lbk[0], lbk[1], pvb[0], pvb[1], pvb[2], pvb[3]):
                z = sch.t(lambda e, bk=bk: e.matmul(
                    bk.ap[:, 0:512], ones_b[0:1, :], zrow_b[:],
                    start=True, stop=True),
                    waits=[ld_zrow, ld_onesb] + bk.wwaits(), tag=f"zb{s}w{w}")
                bk.set_writer(z)
                zb_ops.append(z)
            njc = 4 * w + 4
            for jc in range(njc):
                for p in range(4):
                    hA, hB = 2 * p, 2 * p + 1
                    stA, stB = sbk[0], sbk[1]
                    mmA = sch.t(lambda e, p=p, jc=jc, w=w, stA=stA: e.matmul(
                        stA.ap[:, 0:512],
                        kT[0:64, p * N + jc * 128: p * N + (jc + 1) * 128],
                        qT[0:64, p * N + w * 512: p * N + (w + 1) * 512],
                        start=True, stop=False, tile_position=(0, 0)),
                        waits=stA.wwaits() + corr_evs, tag=f"sT{s}w{w}j{jc}p{p}A")
                    mmB = sch.t(lambda e, p=p, jc=jc, w=w, stB=stB: e.matmul(
                        stB.ap[:, 0:512],
                        kT[64:128, p * N + jc * 128: p * N + (jc + 1) * 128],
                        qT[64:128, p * N + w * 512: p * N + (w + 1) * 512],
                        start=True, stop=False, tile_position=(64, 0)),
                        waits=stB.wwaits() + corr_evs, tag=f"sT{s}w{w}j{jc}p{p}B")
                    crA = sch.t(lambda e, hA=hA, stA=stA: e.matmul(
                        stA.ap[:, 0:512],
                        ones_f[HROW[hA]:HROW[hA] + 1, 0:128],
                        corrT_sb[HROW[hA]:HROW[hA] + 1, (hA // 4) * 512:(hA // 4) * 512 + 512],
                        start=False, stop=True, tile_position=(HROW[hA], 0)),
                        waits=[mmA, ld_ones], tag=f"cr{s}w{w}j{jc}p{p}A")
                    stA.set_writer(crA)
                    crB = sch.t(lambda e, hB=hB, stB=stB: e.matmul(
                        stB.ap[:, 0:512],
                        ones_f[HROW[hB]:HROW[hB] + 1, 0:128],
                        corrT_sb[HROW[hB]:HROW[hB] + 1, (hB // 4) * 512:(hB // 4) * 512 + 512],
                        start=False, stop=True, tile_position=(HROW[hB], 0)),
                        waits=[mmB, ld_ones], tag=f"cr{s}w{w}j{jc}p{p}B")
                    stB.set_writer(crB)
                    for h, bk, sl in ((hA, stA, pT[(2 * (jc * 4 + p)) % 4]),
                                      (hB, stB, pT[(2 * (jc * 4 + p) + 1) % 4])):
                        si = pT.index(sl)
                        ex = sch.s(lambda e, bk=bk, si=si: e.activation(
                            pT_sb[:, si * 512: si * 512 + 512],
                            bk.ap[:, 0:512], AF.Exp, scale=ATTN_SCALE),
                            waits=bk.rwaits() + sl.pwaits(), tag=f"ex{s}w{w}j{jc}h{h}")
                        bk.add_reader(ex)
                        sl.produced(ex)
                        src_op = ex
                        if jc >= 4 * w:
                            r_idx = jc - 4 * w
                            mk = sch.v(lambda e, si=si, r_idx=r_idx: e.tensor_tensor(
                                pT_sb[:, si * 512: si * 512 + 512],
                                pT_sb[:, si * 512: si * 512 + 512],
                                masks_b[:, r_idx * 512:(r_idx + 1) * 512], op=ALU.mult),
                                waits=[ex, ld_masks], tag=f"mk{s}w{w}j{jc}h{h}")
                            sl.produced(mk)
                            src_op = mk
                        pb_ = pvb[p]
                        mmpv = sch.t(lambda e, h=h, jc=jc, si=si, pb_=pb_: e.matmul(
                            pb_.ap[(h % 2) * 64:(h % 2) * 64 + 64, 0:512],
                            v_sb[:, jc * 512 + h * 64: jc * 512 + (h + 1) * 64],
                            pT_sb[:, si * 512: si * 512 + 512],
                            start=False, stop=False,
                            tile_position=(0, (h % 2) * 64)),
                            waits=[src_op],
                            tag=f"pv{s}w{w}j{jc}h{h}")
                        pb_.set_writer(mmpv)
                        sl.consumed(mmpv)
                        lb = lbk[h // 4]
                        mml = sch.t(lambda e, h=h, jc=jc, si=si, lb=lb: e.matmul(
                            lb.ap[HROW[h]:HROW[h] + 1, 0:512],
                            ones_b[:, 0:1],
                            pT_sb[:, si * 512: si * 512 + 512],
                            start=False, stop=False,
                            tile_position=(0, HROW[h])),
                            waits=[src_op],
                            tag=f"l{s}w{w}j{jc}h{h}")
                        lb.set_writer(mml)
                        sl.consumed(mml)
            rl_ops = []
            for g in range(2):
                lb = lbk[g]
                for hh in range(4):
                    row = 32 * hh
                    rc = sch.v(lambda e, g=g, row=row, lb=lb: e.reciprocal(
                        rl_sb[row:row + 1, g * 512:(g + 1) * 512],
                        lb.ap[row:row + 1, 0:512]),
                        waits=lb.rwaits(), tag=f"rl{s}w{w}g{g}r{row}")
                    lb.add_reader(rc)
                    rl_ops.append(rc)
            if s == 0 and w == 0 and "rl0" in taps:
                td = mktap("rl0", [128, 1024])
                sch.q(lambda e, td=td: e.dma_start(td.ap(), rl_sb[:]),
                      waits=rl_ops, tag="tap_rl0")
            outT_evs = []
            for p in range(4):
                g = p // 2
                trl_b, tg_b = sbk[0], sbk[1]
                m1 = sch.t(lambda e, p=p, g=g, trl_b=trl_b: e.matmul(
                    trl_b.ap[:, 0:512], selr_f[:, p * 128:(p + 1) * 128],
                    rl_sb[:, g * 512:(g + 1) * 512], start=True, stop=True),
                    waits=rl_ops + [ld_selr] + trl_b.wwaits(), tag=f"trl{s}w{w}p{p}")
                trl_b.set_writer(m1)
                m2 = sch.t(lambda e, p=p, w=w, tg_b=tg_b: e.matmul(
                    tg_b.ap[:, 0:512], selg_f[0:8, p * 128:(p + 1) * 128],
                    gT_sb[0:8, w * 512:(w + 1) * 512], start=True, stop=True),
                    waits=g_done + [ld_selg] + tg_b.wwaits(), tag=f"tg{s}w{w}p{p}")
                tg_b.set_writer(m2)
                c1_ = sch.v(lambda e, tg_b=tg_b: e.tensor_copy(
                    tg_sb[:, 0:512], tg_b.ap[:, 0:512]),
                    waits=tg_b.rwaits(), tag=f"tgc{s}w{w}p{p}")
                tg_b.add_reader(c1_)
                c2_ = sch.v(lambda e, trl_b=trl_b: e.tensor_tensor(
                    tg_sb[:, 512:1024], trl_b.ap[:, 0:512], tg_sb[:, 0:512],
                    op=ALU.mult), waits=trl_b.rwaits() + [c1_], tag=f"tgm{s}w{w}p{p}")
                trl_b.add_reader(c2_)
                pb_ = pvb[p]
                ev = sch.v(lambda e, p=p, pb_=pb_: e.tensor_tensor(
                    outT_sb[:, p * 512:(p + 1) * 512], pb_.ap[:, 0:512],
                    tg_sb[:, 512:1024], op=ALU.mult),
                    waits=pb_.rwaits() + [c2_] + (outT_slot.pwaits() if p == 0 else []),
                    tag=f"oev{s}w{w}p{p}")
                pb_.add_reader(ev)
                outT_evs.append(ev)
            outT_slot.produced(outT_evs[-1])
            if s == 0 and w == 0 and "outT0" in taps:
                td = mktap("outT0", [128, 2048], BF16)
                sch.q(lambda e, td=td: e.dma_start(td.ap(), outT_sb[:]),
                      waits=outT_evs, tag="tap_outT0")
            for k in range(4):
                it = 4 * w + k
                wslot = xr[k % 2]
                xa = x_tile[:, (k % 2) * D:(k % 2 + 1) * D]
                ldx = sch.q(lambda e, s=s, it=it, xa=xa: e.dma_start(
                    xa, x_d[s, it * 128:(it + 1) * 128, :]),
                    waits=wslot.pwaits(), tag=f"xr{s}_{it}")
                wslot.produced(ldx)
                pbs = [lbk[0], lbk[1]]
                for half in range(2):
                    pbh = pbs[half]
                    mms = []
                    for p in range(4):
                        mm = sch.t(lambda e, p=p, k=k, half=half, pbh=pbh: e.matmul(
                            pbh.ap[:, 0:512],
                            outT_sb[:, p * 512 + k * 128: p * 512 + (k + 1) * 128],
                            wo_l[:, p * D + half * 512: p * D + (half + 1) * 512],
                            start=(p == 0), stop=(p == 3)),
                            waits=[outT_evs[p]] + (pbh.wwaits() if p == 0 else []),
                            tag=f"pr{s}w{w}k{k}h{half}p{p}")
                        mms.append(mm)
                    pbh.set_writer(mms[-1])
                    outT_slot.consumed(mms[-1])
                radd = []
                for half in range(2):
                    pbh = pbs[half]
                    ra = sch.v(lambda e, half=half, xa=xa, pbh=pbh: e.scalar_tensor_tensor(
                        sq_scr[:, half * 512:(half + 1) * 512],
                        xa[:, half * 512:(half + 1) * 512], 1.0, pbh.ap[:, 0:512],
                        op0=ALU.mult, op1=ALU.add),
                        waits=[ldx] + pbh.rwaits() + (rsq.pwaits() if half == 0 else []),
                        tag=f"ra{s}w{w}k{k}h{half}")
                    pbh.add_reader(ra)
                    radd.append(ra)
                wslot.consumed(radd[-1])
                rsq.produced(radd[-1])
                if s == 0 and w == 0 and k == 0 and "r0" in taps:
                    td = mktap("r0", [128, D])
                    sch.q(lambda e, td=td: e.dma_start(td.ap(), sq_scr[:, 0:D]),
                          waits=radd, tag="tap_r0")
                s1_ = sch.s(lambda e: e.activation(
                    dump_scr[:, 0:D], sq_scr[:, 0:D], AF.Copy,
                    accum_out=stat[:, 11:12]), waits=radd, tag=f"lns{s}w{w}k{k}")
                s2_ = sch.s(lambda e: e.activation(
                    dump_scr[:, 0:D], sq_scr[:, 0:D], AF.Square,
                    accum_out=stat[:, 12:13]), waits=[s1_], tag=f"lnq{s}w{w}k{k}")
                m8 = sch.v(lambda e: e.tensor_scalar(
                    stat[:, 16:24], stat[:, 11:12].broadcast_to([128, 8]), 1.0 / D,
                    None, op0=ALU.mult), waits=[s2_], tag=f"lnm{s}w{w}k{k}")
                v8 = sch.v(lambda e: e.tensor_scalar(
                    stat[:, 24:32], stat[:, 12:13].broadcast_to([128, 8]), 1.0 / D,
                    None, op0=ALU.mult), waits=[s2_], tag=f"lnv{s}w{w}k{k}")
                msq = sch.v(lambda e: e.tensor_tensor(
                    stat[:, 32:40], stat[:, 16:24], stat[:, 16:24], op=ALU.mult),
                    waits=[m8], tag=f"lnw{s}w{w}k{k}")
                var8 = sch.v(lambda e: e.tensor_tensor(
                    stat[:, 40:48], stat[:, 24:32], stat[:, 32:40], op=ALU.subtract),
                    waits=[v8, msq], tag=f"lnx{s}w{w}k{k}")
                rstd = rsqrt_col(var8, 40, 14, LN_EPS, f"lnr{s}w{w}k{k}", scratch_col=15)
                nb8 = sch.v(lambda e: e.tensor_scalar(
                    stat[:, 32:40], stat[:, 16:24], stat[:, 14:15], -1.0,
                    op0=ALU.mult, op1=ALU.mult), waits=[rstd], tag=f"lnb{s}w{w}k{k}")
                if s == 0 and w == 0 and k == 0 and "st0" in taps:
                    td = mktap("st0", [128, 64])
                    sch.q(lambda e, td=td: e.dma_start(td.ap(), stat[:]),
                          waits=[nb8], tag="tap_st0")
                ysl = y1slot[k % 2]
                ya = y1_stage[:, (k % 2) * D:(k % 2 + 1) * D]
                nrm = sch.v(lambda e, ya=ya: e.tensor_scalar(
                    ya, sq_scr[:, 0:D], stat[:, 14:15], stat[:, 32:33],
                    op0=ALU.mult, op1=ALU.add),
                    waits=[nb8] + ysl.pwaits(), tag=f"lnn{s}w{w}k{k}")
                rsq.consumed(nrm)
                ysl.produced(nrm)
                sty = sch.q(lambda e, it=it, ya=ya: e.dma_start(
                    y1_d[it * 128:(it + 1) * 128, :], ya),
                    waits=[nrm], tag=f"y1{s}_{it}")
                ysl.consumed(sty)
                a4_done.append(sty)
        if "y1" in taps and s == 0:
            td = mktap("y1", [N, D])
            a4_done.append(sch.q(lambda e, td=td: e.dma_start(td.ap(), y1_d.ap()),
                                 waits=list(a4_done), tag="tap_y1"))
        prev_seq_done = a2_done + a4_done
        return prev_seq_done

    def phase_ffn(s):
        nonlocal prev_seq_done
        deps = list(prev_seq_done)
        ffn_done = []
        ldmw = sch.q(lambda e: e.dma_start(
            meanw_sb[:, 0:8], scal_d[0:1, 0:8].broadcast_to([128, 8])),
            waits=deps, tag=f"Fmw{s}")
        cm_ops = []
        for j, cnt in ((0, INNER * D), (1, INNER * D), (2, D * INNER)):
            cm_ops.append(sch.v(lambda e, j=j, cnt=cnt: e.tensor_scalar(
                meanw_sb[:, 8 + 8 * j:16 + 8 * j],
                meanw_sb[:, j:j + 1].broadcast_to([128, 8]),
                1.0 / cnt, QEPS, op0=ALU.mult, op1=ALU.max),
                waits=[ldmw], tag=f"Fcmw{s}_{j}"))
        ldw_ops = []
        for mat_d, dst in ((wgT_d, wgT_sb), (wuT_d, wuT_sb)):
            for dc in range(DC):
                ldw_ops.append(sch.q(lambda e, mat_d=mat_d, dst=dst, dc=dc: e.dma_start(
                    dst[:, dc * INNER:(dc + 1) * INNER],
                    mat_d[dc * 128:(dc + 1) * 128, :]),
                    waits=deps if dc == 0 else [], tag=f"F{s}w{dc}"))
        y1s = [Slot(), Slot()]
        xq2Ts = Slot()
        oslot = [Slot(), Slot()]
        wdslot = [Slot(), Slot()]
        hbanks = banks[0:7]
        sbank = banks[7]
        tpb = banks[4]

        def emit_ln2(r_waits, out_ap, tag):
            s1_ = sch.s(lambda e: e.activation(
                dump_scr[:, 0:D], sq_scr[:, 0:D], AF.Copy,
                accum_out=stat[:, 11:12]), waits=r_waits, tag=tag + "s1")
            s2_ = sch.s(lambda e: e.activation(
                dump_scr[:, 0:D], sq_scr[:, 0:D], AF.Square,
                accum_out=stat[:, 12:13]), waits=[s1_], tag=tag + "s2")
            m8 = sch.v(lambda e: e.tensor_scalar(
                stat[:, 16:24], stat[:, 11:12].broadcast_to([128, 8]), 1.0 / D,
                None, op0=ALU.mult), waits=[s2_], tag=tag + "m8")
            v8 = sch.v(lambda e: e.tensor_scalar(
                stat[:, 24:32], stat[:, 12:13].broadcast_to([128, 8]), 1.0 / D,
                None, op0=ALU.mult), waits=[s2_], tag=tag + "v8")
            msq = sch.v(lambda e: e.tensor_tensor(
                stat[:, 32:40], stat[:, 16:24], stat[:, 16:24], op=ALU.mult),
                waits=[m8], tag=tag + "msq")
            var8 = sch.v(lambda e: e.tensor_tensor(
                stat[:, 40:48], stat[:, 24:32], stat[:, 32:40], op=ALU.subtract),
                waits=[v8, msq], tag=tag + "var")
            rstd = rsqrt_col(var8, 40, 14, LN_EPS, tag + "rs", scratch_col=15)
            nb8 = sch.v(lambda e: e.tensor_scalar(
                stat[:, 32:40], stat[:, 16:24], stat[:, 14:15], -1.0,
                op0=ALU.mult, op1=ALU.mult), waits=[rstd], tag=tag + "nb")
            return sch.v(lambda e, out_ap=out_ap: e.tensor_scalar(
                out_ap, sq_scr[:, 0:D], stat[:, 14:15], stat[:, 32:33],
                op0=ALU.mult, op1=ALU.add), waits=[nb8], tag=tag + "nrm")

        for grp in range(NT // 2):
            tile_quant_done = []
            for ttn in range(2):
                ti = grp * 2 + ttn
                ya = y1t[:, ttn * D:(ttn + 1) * D]
                ldy = sch.q(lambda e, ti=ti, ya=ya: e.dma_start(
                    ya, y1_d[ti * 128:(ti + 1) * 128, :]),
                    waits=deps + y1s[ttn].pwaits(), tag=f"Fy{s}_{ti}")
                y1s[ttn].produced(ldy)
                q1a = sch.v(lambda e, ya=ya: e.tensor_scalar(
                    dump_scr[:, 0:D], ya, 1.0, None, op0=ALU.mult, op1=ALU.max,
                    accum_out=stat[:, 20:21]), waits=[ldy], tag=f"Fq1a{s}_{ti}")
                q1b = sch.v(lambda e, ya=ya: e.tensor_scalar(
                    dump_scr[:, 0:D], ya, -1.0, None, op0=ALU.mult, op1=ALU.max,
                    accum_out=stat[:, 21:22]), waits=[ldy], tag=f"Fq1b{s}_{ti}")
                y1s[ttn].consumed(q1a)
                y1s[ttn].consumed(q1b)
                rm2 = sch.v(lambda e: e.tensor_scalar(
                    stat[:, 0:8], stat[:, 20:21].broadcast_to([128, 8]),
                    stat[:, 21:22], QEPS, op0=ALU.max, op1=ALU.max),
                    waits=[q1a, q1b], tag=f"Frm{s}_{ti}")
                l1 = sch.s(lambda e: e.activation(stat[:, 9:10], stat[:, 0:1], AF.Ln),
                           waits=[rm2], tag=f"Fl1{s}_{ti}")
                l2 = sch.s(lambda e: e.activation(stat[:, 10:11], stat[:, 9:10],
                                                  AF.Exp, scale=-1.0),
                           waits=[l1], tag=f"Fl2{s}_{ti}")
                sx = sch.s(lambda e: e.mul(stat[:, 9:10], stat[:, 10:11], 127.0),
                           waits=[l2], tag=f"Fsx{s}_{ti}")
                cg = sch.v(lambda e: e.tensor_scalar(
                    stat[:, 24:32], stat[:, 0:1].broadcast_to([128, 8]),
                    meanw_sb[:, 8:9], 1.0 / 127.0, op0=ALU.mult, op1=ALU.mult),
                    waits=[rm2] + cm_ops, tag=f"Fcg{s}_{ti}")
                cu = sch.v(lambda e: e.tensor_scalar(
                    stat[:, 32:40], stat[:, 0:1].broadcast_to([128, 8]),
                    meanw_sb[:, 16:17], 1.0 / 127.0, op0=ALU.mult, op1=ALU.mult),
                    waits=[rm2] + cm_ops, tag=f"Fcu{s}_{ti}")
                xq_ops = []
                for half in range(2):
                    qi = sch.v(lambda e, ya=ya, half=half: e.tensor_scalar(
                        xqi[:, 0:512], ya[:, half * 512:(half + 1) * 512],
                        stat[:, 9:10], None, op0=ALU.mult),
                        waits=[sx] + xq_ops[-1:], tag=f"Fxi{s}_{ti}_{half}")
                    y1s[ttn].consumed(qi)
                    qb = sch.v(lambda e, half=half: e.tensor_copy(
                        xqb[:, half * 512:(half + 1) * 512], xqi[:, 0:512]),
                        waits=[qi], tag=f"Fxb{s}_{ti}_{half}")
                    xq_ops.append(qb)
                tdst = tpb.ap.bitcast(BF16)
                tms = []
                for dc in range(DC):
                    tm = sch.t(lambda e, dc=dc, tdst=tdst: e.transpose(
                        tdst[:, dc * 128:(dc + 1) * 128],
                        xqb[:, dc * 128:(dc + 1) * 128], ident_b[:]),
                        waits=xq_ops + (tpb.wwaits() if dc == 0 else []),
                        tag=f"FxT{s}_{ti}_{dc}")
                    tms.append(tm)
                tpb.set_writer(tms[-1])
                xTe = sch.v(lambda e, tdst=tdst: e.tensor_copy(xqT[:, 0:1024],
                                                               tdst[:, 0:1024]),
                            waits=tpb.rwaits(), tag=f"FxTe{s}_{ti}")
                tpb.add_reader(xTe)
                hready = []
                for ob in range(8):
                    gms = []
                    for dc in range(DC):
                        gm = sch.t(lambda e, dc=dc, ob=ob: e.matmul(
                            sbank.ap[:, 0:512],
                            xqT[:, dc * 128:(dc + 1) * 128],
                            wgT_sb[:, dc * INNER + ob * 512: dc * INNER + (ob + 1) * 512],
                            start=(dc == 0), stop=(dc == DC - 1)),
                            waits=([xTe] + ldw_ops + sbank.wwaits()) if dc == 0 else [],
                            tag=f"Fg{s}_{ti}_{ob}_{dc}")
                        gms.append(gm)
                    sbank.set_writer(gms[-1])
                    sil = sch.s(lambda e: e.activation(
                        sg_sb[:, 0:512], sbank.ap[:, 0:512], AF.Silu,
                        scale=stat[:, 24:25]), waits=sbank.rwaits() + [cg],
                        tag=f"Fsl{s}_{ti}_{ob}")
                    sbank.add_reader(sil)
                    ums = []
                    for dc in range(DC):
                        um = sch.t(lambda e, dc=dc, ob=ob: e.matmul(
                            sbank.ap[:, 0:512],
                            xqT[:, dc * 128:(dc + 1) * 128],
                            wuT_sb[:, dc * INNER + ob * 512: dc * INNER + (ob + 1) * 512],
                            start=(dc == 0), stop=(dc == DC - 1)),
                            waits=sbank.wwaits() if dc == 0 else [],
                            tag=f"Fu{s}_{ti}_{ob}_{dc}")
                        ums.append(um)
                    sbank.set_writer(ums[-1])
                    if ob < 7:
                        hb = hbanks[ob]
                        hm = sch.v(lambda e, hb=hb: e.scalar_tensor_tensor(
                            hb.ap[:, 0:512], sg_sb[:, 0:512], stat[:, 32:33],
                            sbank.ap[:, 0:512], op0=ALU.mult, op1=ALU.mult),
                            waits=[sil, cu] + sbank.rwaits() + hb.wwaits() + hb.rwaits(),
                            tag=f"Fh{s}_{ti}_{ob}")
                        hb.set_writer(hm)
                        sbank.add_reader(hm)
                    else:
                        hm = sch.v(lambda e: e.scalar_tensor_tensor(
                            h7_sb[:, 0:512], sg_sb[:, 0:512], stat[:, 32:33],
                            sbank.ap[:, 0:512], op0=ALU.mult, op1=ALU.mult),
                            waits=[sil, cu] + sbank.rwaits(), tag=f"Fh{s}_{ti}_{ob}")
                        sbank.add_reader(hm)
                    hready.append(hm)
                pm_ops = []
                for ob in range(8):
                    hsrc = hbanks[ob].ap if ob < 7 else h7_sb
                    for sgn in range(2):
                        pm = sch.v(lambda e, hsrc=hsrc, ob=ob, sgn=sgn: e.tensor_scalar(
                            dump_scr[:, 0:512], hsrc[:, 0:512],
                            1.0 if sgn == 0 else -1.0, None,
                            op0=ALU.mult, op1=ALU.max,
                            accum_out=attn_scr[:, 8 * sgn + ob:8 * sgn + ob + 1]),
                            waits=[hready[ob]], tag=f"Fp{s}_{ti}_{ob}_{sgn}")
                        if ob < 7:
                            hbanks[ob].add_reader(pm)
                        pm_ops.append(pm)
                cmb = sch.v(lambda e: e.tensor_scalar(
                    dump_scr[:, 0:16], attn_scr[:, 0:16], 1.0, None,
                    op0=ALU.mult, op1=ALU.max, accum_out=stat[:, 20:21]),
                    waits=pm_ops, tag=f"Fcm{s}_{ti}")
                rm2b = sch.v(lambda e: e.tensor_scalar(
                    stat[:, 0:8], stat[:, 20:21].broadcast_to([128, 8]), QEPS, None,
                    op0=ALU.max), waits=[cmb], tag=f"Frm2{s}_{ti}")
                b1 = sch.s(lambda e: e.activation(stat[:, 10:11], stat[:, 0:1], AF.Ln),
                           waits=[rm2b], tag=f"Fb1{s}_{ti}")
                b2 = sch.s(lambda e, ttn=ttn: e.activation(
                    stat[:, 48 + 4 * ttn:49 + 4 * ttn], stat[:, 10:11], AF.Exp,
                    scale=-1.0), waits=[b1], tag=f"Fb2{s}_{ti}")
                s2f = sch.s(lambda e, ttn=ttn: e.mul(
                    stat[:, 48 + 4 * ttn:49 + 4 * ttn],
                    stat[:, 48 + 4 * ttn:49 + 4 * ttn], 127.0),
                    waits=[b2], tag=f"Fs2{s}_{ti}")
                cd1 = sch.s(lambda e, ttn=ttn: e.mul(
                    stat[:, 49 + 4 * ttn:50 + 4 * ttn], stat[:, 0:1],
                    1.0 / 127.0), waits=[rm2b], tag=f"Fcd1{s}_{ti}")
                cdn = sch.s(lambda e, ttn=ttn: e.mul(
                    stat[:, 49 + 4 * ttn:50 + 4 * ttn],
                    stat[:, 49 + 4 * ttn:50 + 4 * ttn], meanw_sb[:, 24:25]),
                    waits=[cd1] + cm_ops, tag=f"Fcd2{s}_{ti}")
                ffn_done.append(cdn)
                for ob in range(8):
                    hsrc = hbanks[ob].ap if ob < 7 else h7_sb
                    qi2 = sch.v(lambda e, hsrc=hsrc, ttn=ttn: e.tensor_scalar(
                        xqi[:, 0:512], hsrc[:, 0:512],
                        stat[:, 48 + 4 * ttn:49 + 4 * ttn], None, op0=ALU.mult),
                        waits=[s2f], tag=f"Fq2{s}_{ti}_{ob}")
                    if ob < 7:
                        hbanks[ob].add_reader(qi2)
                    qb2 = sch.v(lambda e: e.tensor_copy(xq2b[:, 0:512], xqi[:, 0:512]),
                                waits=[qi2], tag=f"Fqb2{s}_{ti}_{ob}")
                    if ob < 7:
                        tb2 = hbanks[ob]
                    else:
                        tb2 = sbank
                    tdst2 = tb2.ap.bitcast(BF16)
                    tms2 = []
                    for c4 in range(4):
                        tm2 = sch.t(lambda e, c4=c4, tdst2=tdst2: e.transpose(
                            tdst2[:, c4 * 128:(c4 + 1) * 128],
                            xq2b[:, c4 * 128:(c4 + 1) * 128], ident_b[:]),
                            waits=[qb2] + (tb2.wwaits() if c4 == 0 else []),
                            tag=f"FqT{s}_{ti}_{ob}_{c4}")
                        tms2.append(tm2)
                    tb2.set_writer(tms2[-1])
                    t2e = sch.v(lambda e, ttn=ttn, ob=ob, tdst2=tdst2: e.tensor_copy(
                        xq2T[:, ttn * 4096 + ob * 512: ttn * 4096 + (ob + 1) * 512],
                        tdst2[:, 0:512]),
                        waits=tb2.rwaits() + (xq2Ts.pwaits() if ttn == 0 and ob == 0 else []),
                        tag=f"FqTe{s}_{ti}_{ob}")
                    tb2.add_reader(t2e)
                    tile_quant_done.append(t2e)
                if s == 0 and grp == 0 and ttn == 0 and "ffn0" in taps:
                    td = mktap("f_xqb", [128, D], BF16)
                    sch.q(lambda e, td=td: e.dma_start(td.ap(), xqb[:]),
                          waits=list(tile_quant_done), tag="tap_fxqb")
                    td2 = mktap("f_xq2T", [128, 8192], BF16)
                    sch.q(lambda e, td2=td2: e.dma_start(td2.ap(), xq2T[:]),
                          waits=list(tile_quant_done), tag="tap_fxq2T")
                    td3 = mktap("f_y1t", [128, 2 * D])
                    sch.q(lambda e, td3=td3: e.dma_start(td3.ap(), y1t[:]),
                          waits=list(tile_quant_done), tag="tap_fy1t")
                    td4 = mktap("f_stat", [128, 64])
                    sch.q(lambda e, td4=td4: e.dma_start(td4.ap(), stat[:]),
                          waits=list(tile_quant_done), tag="tap_fstat")
                    td5 = mktap("f_meanw", [128, 32])
                    sch.q(lambda e, td5=td5: e.dma_start(td5.ap(), meanw_sb[:]),
                          waits=list(tile_quant_done), tag="tap_fmeanw")
            xq2Ts.produced(tile_quant_done[-1])
            if False and "ffn0" in taps:
                td = mktap("f_xqb", [128, D], BF16)
                sch.q(lambda e, td=td: e.dma_start(td.ap(), xqb[:]),
                      waits=tile_quant_done, tag="tap_fxqb")
                td2 = mktap("f_xq2T", [128, 8192], BF16)
                sch.q(lambda e, td2=td2: e.dma_start(td2.ap(), xq2T[:]),
                      waits=tile_quant_done, tag="tap_fxq2T")
                td3 = mktap("f_y1t", [128, 2 * D])
                sch.q(lambda e, td3=td3: e.dma_start(td3.ap(), y1t[:]),
                      waits=tile_quant_done, tag="tap_fy1t")
                td4 = mktap("f_stat", [128, 64])
                sch.q(lambda e, td4=td4: e.dma_start(td4.ap(), stat[:]),
                      waits=tile_quant_done, tag="tap_fstat")
            dbanks = [(banks[0], banks[1]), (banks[2], banks[3])]
            for ttn in range(2):
                for half in range(2):
                    b = dbanks[ttn][half]
            dmms = {0: [], 1: []}
            for oc in range(32):
                wsl = wdslot[oc % 2]
                ldw = sch.g(lambda e, oc=oc: e.dma_start(
                    wdb[:, (oc % 2) * D:(oc % 2 + 1) * D],
                    wdT_d[oc * 128:(oc + 1) * 128, :]),
                    waits=wsl.pwaits() + (tile_quant_done if oc < 2 else []),
                    tag=f"Fwd{s}_{grp}_{oc}")
                wsl.produced(ldw)
                for ttn in range(2):
                    for half in range(2):
                        b = dbanks[ttn][half]
                        mm = sch.t(lambda e, oc=oc, ttn=ttn, half=half, b=b: e.matmul(
                            b.ap[:, 0:512],
                            xq2T[:, ttn * 4096 + oc * 128: ttn * 4096 + (oc + 1) * 128],
                            wdb[:, (oc % 2) * D + half * 512:(oc % 2) * D + (half + 1) * 512],
                            start=(oc == 0), stop=(oc == 31)),
                            waits=[ldw] + tile_quant_done +
                                  (b.wwaits() if oc == 0 else []),
                            tag=f"Fd{s}_{grp}_{oc}_{ttn}_{half}")
                        if oc == 0:
                            b.set_writer(mm)
                        else:
                            b.writer = mm
                        dmms[ttn].append(mm)
                        wsl.consumed(mm)
                        xq2Ts.consumed(mm)
            for ttn in range(2):
                ti = grp * 2 + ttn
                ya = y1t[:, ttn * D:(ttn + 1) * D]
                ev_ops = []
                for half in range(2):
                    b = dbanks[ttn][half]
                    e1 = sch.v(lambda e, b=b, half=half, ttn=ttn: e.tensor_scalar(
                        sq_scr[:, half * 512:(half + 1) * 512], b.ap[:, 0:512],
                        stat[:, 49 + 4 * ttn:50 + 4 * ttn], None, op0=ALU.mult),
                        waits=[dmms[ttn][-1]] + rsq.pwaits() if False else
                              [dmms[ttn][-1]], tag=f"Fe{s}_{ti}_{half}")
                    b.add_reader(e1)
                    e2 = sch.v(lambda e, half=half, ya=ya: e.tensor_tensor(
                        sq_scr[:, half * 512:(half + 1) * 512],
                        sq_scr[:, half * 512:(half + 1) * 512],
                        ya[:, half * 512:(half + 1) * 512], op=ALU.add),
                        waits=[e1], tag=f"Fr{s}_{ti}_{half}")
                    y1s[ttn].consumed(e2)
                    ev_ops.append(e2)
                osl = oslot[ttn]
                oa = o_stage[:, ttn * D:(ttn + 1) * D]
                nrm2 = emit_ln2(ev_ops + osl.pwaits(), oa, f"FL{s}_{ti}")
                osl.produced(nrm2)
                sto = sch.q(lambda e, s=s, ti=ti, oa=oa: e.dma_start(
                    out_d[s, ti * 128:(ti + 1) * 128, :], oa),
                    waits=[nrm2], tag=f"out{s}_{ti}")
                osl.consumed(sto)
                ffn_done.append(sto)
        prev_seq_done = ffn_done
        return ffn_done

    for s in range(SEQ_PER_CORE if stop_after != "A2_seq0" else 1):
        if stop_after in ("consts", "prep1", "prep2", "prepall"):
            break
        phase_attn(s)
        if stop_after in ("A2", "A2_seq0"):
            break
        phase_ffn(s)

    if stop_after is not None:
        zfill = sch.g(lambda e: e.memset(sq_scr[:], 0.0), dma=False,
                      waits=prev_seq_done, tag="zf")
        for s in range(SEQ_PER_CORE):
            for ti in range(NT):
                sch.q(lambda e, s=s, ti=ti: e.dma_start(
                    out_d[s, ti * 128:(ti + 1) * 128, :], sq_scr[:]),
                    waits=[zfill], tag=f"zero{s}_{ti}")

    sch.finalize(nc)
    return nc, tap_d


def kernel(L0_attn_tokens, L0_attn_wq, L0_attn_wk, L0_attn_wv, L0_attn_wo,
           L0_attn_w_lr, L0_attn_w_target, L0_attn_w_gates, L0_attn_rms_w,
           L0_ffn_w_gate, L0_ffn_w_up, L0_ffn_w_down):
    x = np.ascontiguousarray(np.asarray(L0_attn_tokens, dtype=np.float32))
    consts = _host_consts()
    nc, tap_d = build_program(taps=TAPS, serial=SERIAL, stop_after=STOP_AFTER)
    in_maps = []
    for c in range(NCORES):
        m = {
            "x": x[c * SEQ_PER_CORE:(c + 1) * SEQ_PER_CORE],
            "wq": np.ascontiguousarray(np.asarray(L0_attn_wq, np.float32)),
            "wk": np.ascontiguousarray(np.asarray(L0_attn_wk, np.float32)),
            "wv": np.ascontiguousarray(np.asarray(L0_attn_wv, np.float32)),
            "wo": np.ascontiguousarray(np.asarray(L0_attn_wo, np.float32)),
            "wgates": np.ascontiguousarray(np.asarray(L0_attn_w_gates, np.float32)),
            "rmsw": np.ascontiguousarray(np.asarray(L0_attn_rms_w, np.float32)).reshape(1, D),
            "wgate": np.ascontiguousarray(np.asarray(L0_ffn_w_gate, np.float32)),
            "wup": np.ascontiguousarray(np.asarray(L0_ffn_w_up, np.float32)),
            "wdown": np.ascontiguousarray(np.asarray(L0_ffn_w_down, np.float32)),
        }
        m.update(consts)
        in_maps.append(m)
    res = run_bass_kernel_spmd(nc, in_maps, core_ids=list(range(NCORES)))
    kernel._last_results = res
    out = np.concatenate([np.asarray(r["out"]) for r in res.results], axis=0)
    return out


# version 04
# speedup vs baseline: 1.1966x; 1.1966x over previous
import numpy as np
import ml_dtypes
import concourse.bass as bass
import concourse.mybir as mybir
from concourse.bass_utils import run_bass_kernel_spmd

F32 = mybir.dt.float32
BF16 = mybir.dt.bfloat16
I32 = mybir.dt.int32
AF = mybir.ActivationFunctionType
ALU = mybir.AluOpType
AX = mybir.AxisListType

B, N, D, H, DH = 32, 2048, 1024, 8, 64
NCORES = 8
SEQ_PER_CORE = B // NCORES
NT = N // 128
NW = N // 512
DC = D // 128
INNER = 4096
ATTN_SCALE = float(DH) ** -0.5
RMS_EPS = float(np.finfo(np.float32).eps)
LN_EPS = 1e-5
QEPS = 1e-5

SERIAL = False
TAPS = ()
STOP_AFTER = None


class _Op:
    __slots__ = ("engine", "fn", "waits", "need_inc", "inc_amt", "eidx", "sem_val",
                 "tag", "is_dma", "semkey", "waited", "need_drain")

    def __init__(self, engine, fn, waits, inc_amt, eidx, tag, is_dma):
        self.engine = engine
        self.fn = fn
        self.waits = list(waits)
        self.need_inc = False
        self.inc_amt = inc_amt
        self.eidx = eidx
        self.sem_val = None
        self.tag = tag
        self.is_dma = is_dma
        self.semkey = engine + "_dma" if is_dma else engine
        self.waited = False
        self.need_drain = False


class Sched:
    ENGINES = ("tensor", "vector", "scalar", "sync", "gpsimd")

    def __init__(self, serial=False):
        self.ops = {e: [] for e in self.ENGINES}
        self.serial = serial
        self.order = []

    def emit(self, engine, fn, waits=(), dma=False, tag=""):
        ws = []
        for w in waits:
            if w is None or w is False:
                continue
            if not isinstance(w, _Op):
                raise TypeError(f"bad wait {w!r} at {tag}")
            if w.engine != engine:
                ws.append(w)
            elif engine == "vector":
                w.need_drain = True
        op = _Op(engine, fn, ws, 16 if dma else 1, len(self.ops[engine]), tag, dma)
        if dma:
            op.need_inc = True
        self.ops[engine].append(op)
        self.order.append(op)
        for w in op.waits:
            w.need_inc = True
            w.waited = True
        return op

    def t(self, fn, waits=(), tag=""):
        return self.emit("tensor", fn, waits, tag=tag)

    def v(self, fn, waits=(), tag=""):
        return self.emit("vector", fn, waits, tag=tag)

    def s(self, fn, waits=(), tag=""):
        return self.emit("scalar", fn, waits, tag=tag)

    def q(self, fn, waits=(), tag=""):
        return self.emit("sync", fn, waits, dma=True, tag=tag)

    def g(self, fn, waits=(), dma=True, tag=""):
        return self.emit("gpsimd", fn, waits, dma=dma, tag=tag)

    def finalize(self, nc):
        if self.serial:
            prev = None
            for op in self.order:
                if prev is not None and prev.engine != op.engine:
                    op.waits.append(prev)
                    prev.need_inc = True
                    prev.waited = True
                prev = op
        semkeys = set()
        for e in self.ENGINES:
            for op in self.ops[e]:
                semkeys.add(op.semkey)
        sems = {k: nc.alloc_semaphore(f"sem_{k}") for k in sorted(semkeys)}
        with nc.Block() as b0:
            @b0.gpsimd
            def _(g):
                for k in sorted(sems):
                    g.sem_clear(sems[k])
        nc.all_engine_barrier()
        totals = {k: 0 for k in semkeys}
        for e in self.ENGINES:
            cum = {}
            for op in self.ops[e]:
                if op.need_inc:
                    cum[op.semkey] = cum.get(op.semkey, 0) + op.inc_amt
                    op.sem_val = cum[op.semkey]
            for k, vv in cum.items():
                totals[k] = vv

        def replay(engine_obj, ename):
            for op in self.ops[ename]:
                best = {}
                for w in op.waits:
                    if w.sem_val is None:
                        raise RuntimeError(f"wait target lacks inc: {w.tag}")
                    if w.sem_val > best.get(w.semkey, -1):
                        best[w.semkey] = w.sem_val
                for we in sorted(best):
                    engine_obj.wait_ge(sems[we], best[we])
                inst = op.fn(engine_obj)
                if op.engine == "vector" and (op.need_inc or op.need_drain):
                    inst = engine_obj.drain()
                if op.need_inc:
                    if inst is None:
                        raise RuntimeError(f"op needs inc but returned None: {op.tag}")
                    inst.then_inc(sems[op.semkey], op.inc_amt)
                if op.is_dma and op.waited:
                    engine_obj.wait_ge(sems[op.semkey], op.sem_val)

        with nc.Block() as block:
            @block.tensor
            def _(t):
                replay(t, "tensor")

            @block.vector
            def _(v):
                replay(v, "vector")

            @block.scalar
            def _(s):
                replay(s, "scalar")

            @block.gpsimd
            def _(g):
                replay(g, "gpsimd")

            @block.sync
            def _(s):
                replay(s, "sync")
                for k in sorted(totals):
                    if totals[k] and k != "sync_dma":
                        s.wait_ge(sems[k], totals[k])
                if totals.get("sync_dma"):
                    s.wait_ge(sems["sync_dma"], totals["sync_dma"])


class Bank:

    def __init__(self, ap):
        self.ap = ap
        self.readers = []
        self.writer = None

    def wwaits(self):
        return list(self.readers)

    def set_writer(self, op):
        self.writer = op
        self.readers = []

    def rwaits(self):
        return [self.writer] if self.writer is not None else []

    def add_reader(self, op):
        self.readers.append(op)


class Slot:

    def __init__(self):
        self.consumers = []
        self.producer = None

    def pwaits(self):
        return list(self.consumers)

    def produced(self, op):
        self.producer = op
        self.consumers = []

    def cwaits(self):
        return [self.producer] if self.producer is not None else []

    def consumed(self, op):
        self.consumers.append(op)


def _host_consts():
    c = {}
    c["c_ident"] = np.eye(128, dtype=np.float32)
    c["c_ones"] = np.ones((128, 128), dtype=np.float32)
    masks = np.zeros((4, 128, 512), dtype=np.float32)
    jj = np.arange(128)[:, None]
    ii = np.arange(512)[None, :]
    for r, rel in enumerate((0, 128, 256, 384)):
        masks[r] = (ii >= rel + jj).astype(np.float32)
    c["c_masks"] = masks.reshape(4 * 128, 512)
    selr = np.zeros((4, 128, 128), dtype=np.float32)
    for p in range(4):
        selr[p, 32 * ((2 * p) % 4), 0:64] = 1.0
        selr[p, 32 * ((2 * p + 1) % 4), 64:128] = 1.0
    c["c_selr"] = selr.transpose(1, 0, 2).reshape(128, 4 * 128)
    selg = np.zeros((128, 4 * 128), dtype=np.float32)
    for p in range(4):
        selg[2 * p, p * 128: p * 128 + 64] = 1.0
        selg[2 * p + 1, p * 128 + 64: p * 128 + 128] = 1.0
    c["c_selg"] = selg
    return c


def build_program(taps=(), serial=False, stop_after=None):
    nc = bass.Bass("TRN2", target_bir_lowering=False)
    sch = Sched(serial=serial)
    taps = set(taps)

    x_d = nc.dram_tensor("x", [SEQ_PER_CORE, N, D], F32, kind="ExternalInput")
    wq_d = nc.dram_tensor("wq", [H, D, DH], F32, kind="ExternalInput")
    wk_d = nc.dram_tensor("wk", [H, D, DH], F32, kind="ExternalInput")
    wv_d = nc.dram_tensor("wv", [H, D, DH], F32, kind="ExternalInput")
    wo_d = nc.dram_tensor("wo", [H, DH, D], F32, kind="ExternalInput")
    wgates_d = nc.dram_tensor("wgates", [D, H], F32, kind="ExternalInput")
    rmsw_d = nc.dram_tensor("rmsw", [1, D], F32, kind="ExternalInput")
    wgate_d = nc.dram_tensor("wgate", [INNER, D], F32, kind="ExternalInput")
    wup_d = nc.dram_tensor("wup", [INNER, D], F32, kind="ExternalInput")
    wdown_d = nc.dram_tensor("wdown", [D, INNER], F32, kind="ExternalInput")
    cident_d = nc.dram_tensor("c_ident", [128, 128], F32, kind="ExternalInput")
    cones_d = nc.dram_tensor("c_ones", [128, 128], F32, kind="ExternalInput")
    cmask_d = nc.dram_tensor("c_masks", [4 * 128, 512], F32, kind="ExternalInput")
    cselr_d = nc.dram_tensor("c_selr", [128, 4 * 128], F32, kind="ExternalInput")
    cselg_d = nc.dram_tensor("c_selg", [128, 4 * 128], F32, kind="ExternalInput")

    out_d = nc.dram_tensor("out", [SEQ_PER_CORE, N, D], F32, kind="ExternalOutput")
    y1_d = nc.dram_tensor("y1_scr", [N, D], F32, kind="Internal")
    wgT_d = nc.dram_tensor("wgT_scr", [D, INNER], BF16, kind="Internal")
    scal_d = nc.dram_tensor("scal_scr", [1, 8], F32, kind="Internal")
    col_d = nc.dram_tensor("col_scr", [128, 8], F32, kind="Internal")
    wuT_d = nc.dram_tensor("wuT_scr", [D, INNER], BF16, kind="Internal")
    wdT_d = nc.dram_tensor("wdT_scr", [INNER, D], BF16, kind="Internal")

    tap_d = {}

    def mktap(name, shape, dtype=F32):
        tap_d[name] = nc.dram_tensor("tap_" + name, list(shape), dtype, kind="ExternalOutput")
        return tap_d[name]

    U8 = mybir.dt.uint8
    ARENA_B = 211200
    arena = nc.alloc_sbuf_tensor("arena", [128, ARENA_B], U8)
    _dtsz = {F32: 4, BF16: 2, I32: 4, U8: 1}
    _cur = {"static": 0, "phase": None}

    def AR(name, shape, dtype, phase="static"):
        nb = shape[1] * _dtsz[dtype]
        nb = (nb + 63) & ~63
        if phase == "static":
            off = _cur["static"]
            _cur["static"] += nb
            assert _cur["static"] <= ARENA_B, f"static overflow at {name}"
        else:
            off = _cur[phase]
            _cur[phase] += nb
            assert _cur[phase] <= ARENA_B, f"{phase} overflow at {name} ({_cur[phase]})"
        return arena[0:shape[0], off:off + shape[1] * _dtsz[dtype]].bitcast(dtype)

    ident_f = AR("ident_f", [128, 128], F32)
    ident_b = AR("ident_b", [128, 128], BF16)
    ones_f = AR("ones_f", [128, 128], F32)
    ones_b = AR("ones_b", [128, 128], BF16)
    masks_b = AR("masks_b", [128, 4 * 512], BF16)
    selr_f = AR("selr_f", [128, 4 * 128], F32)
    selg_f = AR("selg_f", [128, 4 * 128], BF16)
    rmsw_rep = AR("rmsw_rep", [128, D], F32)
    attn_scr = AR("attn_scr", [128, 192], F32)
    zrow_b = AR("zrow_b", [1, 512], BF16)
    x_tile = AR("x_tile", [128, 2 * D], F32)
    sq_scr = AR("sq_scr", [128, D], F32)
    dump_scr = AR("dump_scr", [128, D], F32)
    stat = AR("stat", [128, 64], F32)
    meanw_sb = AR("meanw_sb", [128, 32], F32)
    base = _cur["static"]
    _cur["attn"] = base
    _cur["ffn"] = base

    wq_l = AR("wq_l", [128, DC * 512], BF16, "attn")
    wk_l = AR("wk_l", [128, DC * 512], BF16, "attn")
    wv_l = AR("wv_l", [128, DC * 512], BF16, "attn")
    wgates_l = AR("wgates_l", [128, DC * 8], BF16, "attn")
    wo_l = AR("wo_l", [128, 4 * D], BF16, "attn")
    tT = AR("tT", [128, DC * N], BF16, "attn")
    qT = AR("qT", [128, 4 * N], BF16, "attn")
    kT = AR("kT", [128, 4 * N], BF16, "attn")
    v_sb = AR("v_sb", [128, NT * 512], BF16, "attn")
    gT_sb = AR("gT_sb", [128, N], BF16, "attn")
    t_tile = AR("t_tile", [128, 2 * D], BF16, "attn")
    nm_wide = AR("nm_wide", [128, 4 * 256], F32, "attn")
    corrT_sb = AR("corrT_sb", [128, 2 * 512], F32, "attn")
    pT_sb = AR("pT_sb", [128, 4 * 512], BF16, "attn")
    rl_sb = AR("rl_sb", [128, 2 * 512], F32, "attn")
    tg_sb = AR("tg_sb", [128, 2 * 512], F32, "attn")
    outT_sb = AR("outT_sb", [128, 4 * 512], BF16, "attn")
    y1_stage = AR("y1_stage", [128, 2 * D], F32, "attn")

    wgT_sb = AR("wgT_sb", [128, DC * INNER], BF16, "ffn")
    wuT_sb = AR("wuT_sb", [128, DC * INNER], BF16, "ffn")
    y1t = AR("y1t", [128, 2 * D], F32, "ffn")
    xqi = AR("xqi", [128, 512], I32, "ffn")
    xqb = AR("xqb", [128, D], BF16, "ffn")
    xqT = AR("xqT", [128, DC * 128], BF16, "ffn")
    sg_sb = AR("sg_sb", [128, 512], F32, "ffn")
    h7_sb = AR("h7_sb", [128, 512], F32, "ffn")
    xq2b = AR("xq2b", [128, 512], BF16, "ffn")
    xq2T = AR("xq2T", [128, 2 * 32 * 128], BF16, "ffn")
    wdb = AR("wdb", [128, 2 * D], BF16, "ffn")
    o_stage = AR("o_stage", [128, 2 * D], F32, "ffn")
    assert max(_cur["attn"], _cur["ffn"]) <= ARENA_B

    P = nc.alloc_psum_tensor
    psA = P("psA", [128, 2048], F32)
    psB = P("psB", [128, 2048], F32)
    banks = [Bank(psA[:, i * 512:(i + 1) * 512]) for i in range(4)] + \
            [Bank(psB[:, i * 512:(i + 1) * 512]) for i in range(4)]

    ld_ident = sch.q(lambda e: e.dma_start(ident_f[:], cident_d.ap()), tag="ident")
    ld_identb = sch.g(lambda e: e.dma_start(ident_b[:], cident_d.ap()), tag="identb")
    ld_ones = sch.q(lambda e: e.dma_start(ones_f[:], cones_d.ap()), tag="ones")
    ld_masks = sch.g(lambda e: e.dma_start(
        masks_b[:].rearrange("p (r f) -> p r f", r=4),
        cmask_d.ap().rearrange("(r p) f -> p r f", r=4)), tag="masks")
    ld_selr = sch.q(lambda e: e.dma_start(selr_f[:], cselr_d.ap()), tag="selr")
    ld_selg = sch.g(lambda e: e.dma_start(selg_f[:], cselg_d.ap()), tag="selg")
    ld_rmsw = sch.g(lambda e: e.dma_start(
        rmsw_rep[:], rmsw_d.ap().broadcast_to([128, D])), tag="rmsw")
    ld_onesb = sch.g(lambda e: e.dma_start(ones_b[:], cones_d.ap()), tag="onesb")
    ld_zrow = sch.g(lambda e: e.memset(zrow_b[:], 0.0), dma=False, tag="zrow")

    def load_attn_weights(deps, s):
        ops = []
        for dst, src_d, pariter, nm2 in ((wq_l, wq_d, True, "wq"), (wk_l, wk_d, True, "wk"),
                                         (wv_l, wv_d, False, "wv")):
            for h in range(H):
                coloff = ((h // 2) * 128 + (h % 2) * 64) if pariter else h * 64
                dst_ap = dst.rearrange("p (dc c) -> p dc c", dc=DC)[:, :, coloff:coloff + 64]
                src_ap = src_d[h].rearrange("(dc p) e -> p dc e", p=128)
                ops.append(sch.g(lambda e, d=dst_ap, s2=src_ap: e.dma_start(d, s2),
                                 waits=deps if (h == 0 and nm2 == "wq") else [],
                                 tag=f"attw{s}_{nm2}{h}"))
        ops.append(sch.g(lambda e: e.dma_start(
            wgates_l.rearrange("p (dc h) -> p dc h", dc=DC),
            wgates_d.ap().rearrange("(dc p) h -> p dc h", p=128)),
            waits=deps, tag=f"wgates{s}"))
        ops.append(sch.g(lambda e: e.dma_start(
            wo_l.rearrange("p (pc d) -> p pc d", pc=4),
            wo_d.ap().rearrange("(pc hh) e d -> (hh e) pc d", hh=2)),
            waits=deps, tag=f"wo{s}"))
        return ops

    def rsqrt_col(dep, src_col, dst_col, eps, tag, scratch_col=9):
        eps_ap = _cap(eps)
        o1 = sch.s(lambda e, eps_ap=eps_ap: e.activation(
            stat[:, scratch_col:scratch_col + 1], stat[:, src_col:src_col + 1], AF.Ln,
            bias=eps_ap), waits=[dep], tag=tag + "_ln")
        o2 = sch.s(lambda e: e.activation(
            stat[:, dst_col:dst_col + 1], stat[:, scratch_col:scratch_col + 1],
            AF.Exp, scale=-0.5), waits=[o1], tag=tag + "_ex")
        return o2

    _const_cols = {}

    def _cap(val):
        if val not in _const_cols:
            idx = 56 + len(_const_cols)
            assert idx < 64
            _const_cols[val] = idx
            sch.g(lambda e, idx=idx, val=val: e.memset(stat[:, idx:idx + 1], float(val)),
                  dma=False, tag=f"const{val}")
        idx = _const_cols[val]
        return stat[:, idx:idx + 1]

    wprep_scr = tT[:].bitcast(F32)[:, 0:INNER]
    wprep_i = qT[:].bitcast(I32)[:, 0:INNER]
    abs_scr = kT[:].bitcast(F32)[:, 0:INNER]
    wprep_q = v_sb[:, 0:INNER]
    wprep_Ts = outT_sb[:, 0:1024]

    def prep_ffn_weight(src_d, dst_T_d, rows, cols, name):
        RT = rows // 128
        sum_ops = []
        scr = Slot()
        for r in range(RT):
            ld = sch.q(lambda e, r=r: e.dma_start(wprep_scr[:, 0:cols],
                                                  src_d[r * 128:(r + 1) * 128, :]),
                       waits=scr.pwaits(), tag=f"{name}ld{r}")
            scr.produced(ld)
            so = sch.s(lambda e, r=r: e.activation(
                abs_scr[:, 0:cols], wprep_scr[:, 0:cols], AF.Abs,
                accum_out=stat[:, r:r + 1]), waits=[ld], tag=f"{name}abs{r}")
            scr.consumed(so)
            sum_ops.append(so)
        st_free = sch.s(lambda e, RT=RT: e.activation(
            abs_scr[:, 0:RT], stat[:, 0:RT], AF.Copy, accum_out=stat[:, 32:33]),
            waits=[sum_ops[-1]], tag=f"{name}stf")
        if name == "Pg" and "pstat0" in taps:
            td = mktap("pstat0", [128, 64])
            sch.q(lambda e, td=td: e.dma_start(td.ap(), stat[:]),
                  waits=[st_free], tag="tap_pstat0")
        g0 = sch.q(lambda e: e.dma_start(col_d.ap(), stat[:, 32:40]),
                   waits=[st_free], tag=f"{name}pg0")
        g1 = sch.q(lambda e: e.dma_start(sq_scr[0:1, 0:1024],
                                         col_d.ap().rearrange("p o -> (p o)").unsqueeze(0)),
                   waits=[g0], tag=f"{name}pg")
        rr = sch.v(lambda e: e.reduce_sum(
            stat[0:1, 33:34],
            sq_scr[0:1, 0:1024].rearrange("o (p c) -> o p c", c=8)[:, :, 0:1],
            axis=AX.XY), waits=[g1], tag=f"{name}rr")
        midx = {"Pg": 0, "Pu": 1, "Pd": 2}[name]
        s1 = sch.q(lambda e, midx=midx: e.dma_start(
            scal_d[0:1, midx:midx + 1], stat[0:1, 33:34]),
            waits=[rr], tag=f"{name}s1")
        bcast = sch.q(lambda e, midx=midx: e.dma_start(
            stat[:, 33:34], scal_d[0:1, midx:midx + 1].broadcast_to([128, 1])),
            waits=[s1], tag=f"{name}bc")
        c1 = sch.v(lambda e: e.tensor_scalar(
            stat[:, 41:49], stat[:, 33:34].broadcast_to([128, 8]),
            1.0 / (rows * cols), QEPS,
            op0=ALU.mult, op1=ALU.max), waits=[bcast], tag=f"{name}clip")
        c2a = sch.s(lambda e: e.activation(stat[:, 49:50], stat[:, 41:42], AF.Ln),
                    waits=[c1], tag=f"{name}rln")
        c2 = sch.s(lambda e: e.activation(stat[:, 50:51], stat[:, 49:50], AF.Exp,
                                          scale=-1.0), waits=[c2a], tag=f"{name}rex")
        if name == "Pg" and "pstat1" in taps:
            td = mktap("pstat1", [128, 64])
            sch.q(lambda e, td=td: e.dma_start(td.ap(), stat[:]), waits=[c2],
                  tag="tap_pstat1")

        last_stores = []
        tb = banks[3]
        qslot = Slot()
        for r in range(RT):
            ld = sch.q(lambda e, r=r: e.dma_start(wprep_scr[:, 0:cols],
                                                  src_d[r * 128:(r + 1) * 128, :]),
                       waits=[c2] + scr.pwaits(), tag=f"{name}ld2_{r}")
            scr.produced(ld)
            qi = sch.v(lambda e: e.tensor_scalar_mul(
                wprep_i[:, 0:cols], wprep_scr[:, 0:cols], stat[:, 50:51]),
                waits=[ld], tag=f"{name}rnd{r}")
            scr.consumed(qi)
            qc = sch.v(lambda e: e.tensor_scalar(
                wprep_q[:, 0:cols], wprep_i[:, 0:cols], -1.0, 1.0,
                op0=ALU.max, op1=ALU.min), waits=[qi] + qslot.pwaits(), tag=f"{name}clp{r}")
            qslot.produced(qc)
            CT = cols // 128
            for ct0 in range(0, CT, 8):
                nct = min(8, CT - ct0)
                tms = []
                for k in range(nct):
                    ct = ct0 + k
                    tm = sch.t(lambda e, ct=ct, k=k: e.transpose(
                        tb.ap[:].bitcast(BF16)[:, k * 128:(k + 1) * 128],
                        wprep_q[:, ct * 128:(ct + 1) * 128], ident_b[:]),
                        waits=[qc, ld_identb] + (tb.wwaits() if k == 0 else []),
                        tag=f"{name}T{r}_{ct}")
                    tms.append(tm)
                tb.set_writer(tms[-1])
                qslot.consumed(tms[-1])
                ev = sch.v(lambda e, nct=nct: e.tensor_copy(
                    wprep_Ts[:, 0:nct * 128], tb.ap[:].bitcast(BF16)[:, 0:nct * 128]),
                    waits=tb.rwaits(), tag=f"{name}Tev{r}_{ct0}")
                tb.add_reader(ev)
                stv = sch.q(lambda e, r=r, ct0=ct0, nct=nct: e.dma_start(
                    dst_T_d[ct0 * 128:ct0 * 128 + nct * 128, r * 128:(r + 1) * 128]
                        .rearrange("(n p) f -> p n f", p=128),
                    wprep_Ts[:, 0:nct * 128].rearrange("p (n f) -> p n f", f=128)),
                    waits=[ev], tag=f"{name}st{r}_{ct0}")
                last_stores.append(stv)
        return last_stores

    if stop_after not in ("consts",):
        wprep_done = []
        wprep_done += prep_ffn_weight(wgate_d, wgT_d, INNER, D, "Pg")
        if "pstat" in taps:
            td = mktap("pstat", [128, 64])
            sch.q(lambda e, td=td: e.dma_start(td.ap(), stat[:]), waits=wprep_done,
                  tag="tap_pstat")
        if stop_after != "prep1":
            wprep_done += prep_ffn_weight(wup_d, wuT_d, INNER, D, "Pu")
            if stop_after != "prep2":
                wprep_done += prep_ffn_weight(wdown_d, wdT_d, D, INNER, "Pd")
        if "wgT" in taps:
            td = mktap("wgT", [D, INNER], BF16)
            sch.q(lambda e, td=td: e.dma_start(td.ap(), wgT_d.ap()), waits=wprep_done, tag="tapwgT")
        if "wdT" in taps:
            td = mktap("wdT", [INNER, D], BF16)
            sch.q(lambda e, td=td: e.dma_start(td.ap(), wdT_d.ap()), waits=wprep_done, tag="tapwdT")

    prev_seq_done = []

    def phase_attn(s):
        nonlocal prev_seq_done
        ld_attw = load_attn_weights(list(prev_seq_done), s)
        zrl = sch.g(lambda e: e.memset(rl_sb[:], 0.0), dma=False,
                    waits=list(prev_seq_done), tag=f"zrl{s}")
        deps = list(prev_seq_done) + ld_attw + [ld_identb, ld_rmsw, zrl]

        xslot = [Slot(), Slot()]
        tslot = [Slot(), Slot()]
        a1_done = []
        tb = banks[0]
        for ti in range(NT):
            w = ti % 2
            xa = x_tile[:, w * D:(w + 1) * D]
            ta = t_tile[:, w * D:(w + 1) * D]
            ld = sch.q(lambda e, s=s, ti=ti, xa=xa: e.dma_start(
                xa, x_d[s, ti * 128:(ti + 1) * 128, :]),
                waits=deps + xslot[w].pwaits(), tag=f"x{s}_{ti}")
            xslot[w].produced(ld)
            sq = sch.s(lambda e, xa=xa: e.activation(
                sq_scr[:], xa, AF.Square, accum_out=stat[:, 0:1]),
                waits=[ld], tag=f"sq{s}_{ti}")
            st1 = sch.v(lambda e: e.tensor_scalar(
                stat[:, 1:9], stat[:, 0:1].broadcast_to([128, 8]), 1.0 / D, None,
                op0=ALU.mult), waits=[sq], tag=f"rmsm{s}_{ti}")
            rs = rsqrt_col(st1, 1, 10, RMS_EPS, f"rms{s}_{ti}")
            tt = sch.v(lambda e, xa=xa, ta=ta: e.scalar_tensor_tensor(
                ta, xa, stat[:, 10:11], rmsw_rep[:], op0=ALU.mult, op1=ALU.mult),
                waits=[rs] + tslot[w].pwaits(), tag=f"t{s}_{ti}")
            xslot[w].consumed(tt)
            tslot[w].produced(tt)
            tms = []
            for dc in range(DC):
                tm = sch.t(lambda e, dc=dc, ta=ta: e.transpose(
                    tb.ap[:].bitcast(BF16)[:, dc * 128:(dc + 1) * 128],
                    ta[:, dc * 128:(dc + 1) * 128], ident_b[:]),
                    waits=[tt] + (tb.wwaits() if dc == 0 else []), tag=f"tT{s}_{ti}_{dc}")
                tms.append(tm)
            tb.set_writer(tms[-1])
            ev = sch.v(lambda e, ti=ti: e.tensor_copy(
                tT[:].rearrange("p (dc n) -> p dc n", dc=DC)
                    [:, :, ti * 128:(ti + 1) * 128],
                tb.ap[:].bitcast(BF16)[:, 0:1024].rearrange("p (dc n) -> p dc n", dc=DC)),
                waits=tb.rwaits(), tag=f"tTev{s}_{ti}")
            tb.add_reader(ev)
            tslot[w].consumed(ev)
            a1_done.append(ev)

        if "tT" in taps and s == 0:
            td = mktap("tT", [128, DC * N], BF16)
            a1_done.append(sch.q(lambda e, td=td: e.dma_start(td.ap(), tT[:]),
                                 waits=list(a1_done), tag="tap_tT"))

        a2_done = []
        for dst, wl, nm in ((qT, wq_l, "q"), (kT, wk_l, "k")):
            for c in range(4):
                for nb in range(4):
                    b = banks[2 + (nb + c * 4) % 2]
                    mms = []
                    for dc in range(DC):
                        mm = sch.t(lambda e, c=c, nb=nb, dc=dc, wl=wl, b=b: e.matmul(
                            b.ap[:, 0:512],
                            wl[:, dc * 512 + c * 128: dc * 512 + (c + 1) * 128],
                            tT[:, dc * N + nb * 512: dc * N + (nb + 1) * 512],
                            start=(dc == 0), stop=(dc == DC - 1)),
                            waits=(a1_done + b.wwaits()) if dc == 0 else [],
                            tag=f"{nm}mm{s}_{c}_{nb}_{dc}")
                        mms.append(mm)
                    b.set_writer(mms[-1])
                    ev = sch.v(lambda e, c=c, nb=nb, dst=dst, b=b: e.tensor_copy(
                        dst[:, c * N + nb * 512: c * N + (nb + 1) * 512], b.ap[:, 0:512]),
                        waits=b.rwaits(), tag=f"{nm}ev{s}_{c}_{nb}")
                    b.add_reader(ev)
                    a2_done.append(ev)
        v_done = []
        for ti in range(NT):
            b = banks[2 + ti % 2]
            mms = []
            for dc in range(DC):
                mm = sch.t(lambda e, ti=ti, dc=dc, b=b: e.matmul(
                    b.ap[:, 0:512],
                    tT[:, dc * N + ti * 128: dc * N + (ti + 1) * 128],
                    wv_l[:, dc * 512:(dc + 1) * 512],
                    start=(dc == 0), stop=(dc == DC - 1)),
                    waits=(a1_done + b.wwaits()) if dc == 0 else [],
                    tag=f"vmm{s}_{ti}_{dc}")
                mms.append(mm)
            b.set_writer(mms[-1])
            ev = sch.v(lambda e, ti=ti, b=b: e.tensor_copy(
                v_sb[:, ti * 512:(ti + 1) * 512], b.ap[:, 0:512]),
                waits=b.rwaits(), tag=f"vev{s}_{ti}")
            b.add_reader(ev)
            v_done.append(ev)
            a2_done.append(ev)
        g_done = []
        for nb in range(4):
            b = banks[4 + nb % 2]
            mms = []
            for dc in range(DC):
                mm = sch.t(lambda e, nb=nb, dc=dc, b=b: e.matmul(
                    b.ap[0:8, 0:512],
                    wgates_l[:, dc * 8:(dc + 1) * 8],
                    tT[:, dc * N + nb * 512: dc * N + (nb + 1) * 512],
                    start=(dc == 0), stop=(dc == DC - 1)),
                    waits=(a1_done + b.wwaits()) if dc == 0 else [],
                    tag=f"gmm{s}_{nb}_{dc}")
                mms.append(mm)
            b.set_writer(mms[-1])
            sg = sch.s(lambda e, nb=nb, b=b: e.activation(
                gT_sb[0:8, nb * 512:(nb + 1) * 512], b.ap[0:8, 0:512], AF.Sigmoid),
                waits=b.rwaits(), tag=f"gsig{s}_{nb}")
            b.add_reader(sg)
            g_done.append(sg)
            a2_done.append(sg)

        if s == 0:
            for nm, buf, shape in (("qT", qT, [128, 4 * N]), ("kT", kT, [128, 4 * N]),
                                   ("v", v_sb, [128, NT * 512])):
                if nm in taps:
                    td = mktap(nm, shape, BF16)
                    a2_done.append(sch.q(lambda e, td=td, buf=buf: e.dma_start(td.ap(), buf[:]),
                                         waits=list(a2_done), tag=f"tap_{nm}"))
            if "gT" in taps:
                td = mktap("gT", [128, N], BF16)
                a2_done.append(sch.q(lambda e, td=td: e.dma_start(td.ap(), gT_sb[:]),
                                     waits=list(g_done), tag="tap_gT"))

        pT = [Slot() for _ in range(4)]
        y1slot = [Slot(), Slot()]
        outT_slot = Slot()
        rsq = Slot()
        xr = [Slot(), Slot()]
        a4_done = []
        sbk = [banks[0], banks[1]]
        lbk = [banks[2], banks[3]]
        pvb = [banks[4], banks[5], banks[6], banks[7]]
        HROW = [32 * (h % 4) for h in range(8)]

        for w in range(NW):
            nm_ready = {}
            jw = (w + 1) * 512
            for k in range(4):
                it = 4 * w + k
                for p in range(4):
                    hA, hB = 2 * p, 2 * p + 1
                    mmsA, mmsB = [], []
                    for jc5 in range(w + 1):
                        bA, bB = banks[jc5], banks[4 + jc5]
                        mmA = sch.t(lambda e, p=p, it=it, jc5=jc5, bA=bA: e.matmul(
                            bA.ap[:, 0:512],
                            qT[0:64, p * N + it * 128: p * N + (it + 1) * 128],
                            kT[0:64, p * N + jc5 * 512: p * N + (jc5 + 1) * 512],
                            start=True, stop=True, tile_position=(0, 0)),
                            waits=(a2_done if (w == 0 and k == 0 and p == 0 and jc5 == 0)
                                   else []) + bA.wwaits(),
                            tag=f"S{s}w{w}k{k}p{p}c{jc5}A")
                        bA.set_writer(mmA)
                        mmsA.append((bA, mmA))
                        mmB = sch.t(lambda e, p=p, it=it, jc5=jc5, bB=bB: e.matmul(
                            bB.ap[:, 0:512],
                            qT[64:128, p * N + it * 128: p * N + (it + 1) * 128],
                            kT[64:128, p * N + jc5 * 512: p * N + (jc5 + 1) * 512],
                            start=True, stop=True, tile_position=(64, 0)),
                            waits=bB.wwaits(), tag=f"S{s}w{w}k{k}p{p}c{jc5}B")
                        bB.set_writer(mmB)
                        mmsB.append((bB, mmB))
                    for h, reg, mml_ in ((hA, psA, mmsA), (hB, psB, mmsB)):
                        slot = k * 256 + (h // 4) * 128 + HROW[h]
                        nm = sch.v(lambda e, reg=reg, slot=slot, jw=jw: e.tensor_scalar(
                            pT_sb[:, 0:jw], reg[:, 0:jw],
                            -1.0, None, op0=ALU.mult, op1=ALU.min,
                            accum_out=nm_wide[:, slot:slot + 1]),
                            waits=[m for _, m in mml_] + sum(
                                (pT[i].pwaits() for i in range(4)), []),
                            tag=f"nm{s}w{w}k{k}h{h}")
                        for bk, _ in mml_:
                            bk.add_reader(nm)
                        nm_ready[(k, h)] = nm
            for i in range(4):
                pT[i].produced(list(nm_ready.values())[-1])
            if s == 0 and w == 0 and "pmax0" in taps:
                td = mktap("pmax0", [128, 192])
                sch.q(lambda e, td=td: e.dma_start(td.ap(), attn_scr[:]),
                      waits=list(nm_ready.values()), tag="tap_pmax0")
            corr_evs = []
            for g in range(2):
                cb = pvb[g]
                tms = []
                for k in range(4):
                    base = k * 256 + g * 128
                    deps = [nm_ready[(k, h)] for h in range(4 * g, 4 * g + 4)]
                    tm = sch.t(lambda e, base=base, k=k, cb=cb: e.transpose(
                        cb.ap[0:97, k * 128:(k + 1) * 128],
                        nm_wide[:, base:base + 97], ident_f[:]),
                        waits=deps + [ld_ident] + (cb.wwaits() if k == 0 else []),
                        tag=f"cT{s}w{w}g{g}k{k}")
                    tms.append(tm)
                cb.set_writer(tms[-1])
                ev = sch.v(lambda e, g=g, cb=cb: e.tensor_copy(
                    corrT_sb[:, g * 512:(g + 1) * 512], cb.ap[:, 0:512]),
                    waits=cb.rwaits(), tag=f"cTe{s}w{w}g{g}")
                cb.add_reader(ev)
                corr_evs.append(ev)
            if s == 0 and w == 0 and "nm0" in taps:
                td = mktap("nm0", [128, 4 * 256])
                sch.q(lambda e, td=td: e.dma_start(td.ap(), nm_wide[:]),
                      waits=list(nm_ready.values()), tag="tap_nm0")
                td2 = mktap("corrT0", [128, 1024])
                sch.q(lambda e, td2=td2: e.dma_start(td2.ap(), corrT_sb[:]),
                      waits=corr_evs, tag="tap_corrT0")
            zb_ops = []
            for bk in (lbk[0], lbk[1], pvb[0], pvb[1], pvb[2], pvb[3]):
                z = sch.t(lambda e, bk=bk: e.matmul(
                    bk.ap[:, 0:512], ones_b[0:1, :], zrow_b[:],
                    start=True, stop=True),
                    waits=[ld_zrow, ld_onesb] + bk.wwaits(), tag=f"zb{s}w{w}")
                bk.set_writer(z)
                zb_ops.append(z)
            njc = 4 * w + 4
            for jc in range(njc):
                for p in range(4):
                    hA, hB = 2 * p, 2 * p + 1
                    stA, stB = sbk[0], sbk[1]
                    mmA = sch.t(lambda e, p=p, jc=jc, w=w, stA=stA: e.matmul(
                        stA.ap[:, 0:512],
                        kT[0:64, p * N + jc * 128: p * N + (jc + 1) * 128],
                        qT[0:64, p * N + w * 512: p * N + (w + 1) * 512],
                        start=True, stop=False, tile_position=(0, 0)),
                        waits=stA.wwaits() + corr_evs, tag=f"sT{s}w{w}j{jc}p{p}A")
                    mmB = sch.t(lambda e, p=p, jc=jc, w=w, stB=stB: e.matmul(
                        stB.ap[:, 0:512],
                        kT[64:128, p * N + jc * 128: p * N + (jc + 1) * 128],
                        qT[64:128, p * N + w * 512: p * N + (w + 1) * 512],
                        start=True, stop=False, tile_position=(64, 0)),
                        waits=stB.wwaits() + corr_evs, tag=f"sT{s}w{w}j{jc}p{p}B")
                    crA = sch.t(lambda e, hA=hA, stA=stA: e.matmul(
                        stA.ap[:, 0:512],
                        ones_f[HROW[hA]:HROW[hA] + 1, 0:128],
                        corrT_sb[HROW[hA]:HROW[hA] + 1, (hA // 4) * 512:(hA // 4) * 512 + 512],
                        start=False, stop=True, tile_position=(HROW[hA], 0)),
                        waits=[mmA, ld_ones], tag=f"cr{s}w{w}j{jc}p{p}A")
                    stA.set_writer(crA)
                    crB = sch.t(lambda e, hB=hB, stB=stB: e.matmul(
                        stB.ap[:, 0:512],
                        ones_f[HROW[hB]:HROW[hB] + 1, 0:128],
                        corrT_sb[HROW[hB]:HROW[hB] + 1, (hB // 4) * 512:(hB // 4) * 512 + 512],
                        start=False, stop=True, tile_position=(HROW[hB], 0)),
                        waits=[mmB, ld_ones], tag=f"cr{s}w{w}j{jc}p{p}B")
                    stB.set_writer(crB)
                    slA = pT[(2 * (jc * 4 + p)) % 4]
                    slB = pT[(2 * (jc * 4 + p) + 1) % 4]
                    siA, siB = pT.index(slA), pT.index(slB)
                    assert siB == siA + 1 and siA % 2 == 0
                    ex2 = sch.s(lambda e, siA=siA: e.activation(
                        pT_sb[:, siA * 512: siA * 512 + 1024],
                        psA[:, 0:1024], AF.Exp, scale=ATTN_SCALE),
                        waits=stA.rwaits() + stB.rwaits() + slA.pwaits() + slB.pwaits(),
                        tag=f"ex{s}w{w}j{jc}p{p}")
                    stA.add_reader(ex2)
                    stB.add_reader(ex2)
                    slA.produced(ex2)
                    slB.produced(ex2)
                    srcs = {hA: ex2, hB: ex2}
                    if jc >= 4 * w:
                        r_idx = jc - 4 * w
                        mkA = sch.v(lambda e, siA=siA, r_idx=r_idx: e.tensor_tensor(
                            pT_sb[:, siA * 512: siA * 512 + 512],
                            pT_sb[:, siA * 512: siA * 512 + 512],
                            masks_b[:, r_idx * 512:(r_idx + 1) * 512], op=ALU.mult),
                            waits=[ex2], tag=f"mkA{s}w{w}j{jc}p{p}")
                        mkB = sch.v(lambda e, siB=siB, r_idx=r_idx: e.tensor_tensor(
                            pT_sb[:, siB * 512: siB * 512 + 512],
                            pT_sb[:, siB * 512: siB * 512 + 512],
                            masks_b[:, r_idx * 512:(r_idx + 1) * 512], op=ALU.mult),
                            waits=[ex2, ld_masks], tag=f"mkB{s}w{w}j{jc}p{p}")
                        slA.produced(mkA)
                        slB.produced(mkB)
                        srcs = {hA: mkA, hB: mkB}
                    for h, bk, sl, si in ((hA, stA, slA, siA), (hB, stB, slB, siB)):
                        src_op = srcs[h]
                        pb_ = pvb[p]
                        mmpv = sch.t(lambda e, h=h, jc=jc, si=si, pb_=pb_: e.matmul(
                            pb_.ap[(h % 2) * 64:(h % 2) * 64 + 64, 0:512],
                            v_sb[:, jc * 512 + h * 64: jc * 512 + (h + 1) * 64],
                            pT_sb[:, si * 512: si * 512 + 512],
                            start=False, stop=False,
                            tile_position=(0, (h % 2) * 64)),
                            waits=[src_op],
                            tag=f"pv{s}w{w}j{jc}h{h}")
                        pb_.set_writer(mmpv)
                        sl.consumed(mmpv)
                        lb = lbk[h // 4]
                        mml = sch.t(lambda e, h=h, jc=jc, si=si, lb=lb: e.matmul(
                            lb.ap[HROW[h]:HROW[h] + 1, 0:512],
                            ones_b[:, 0:1],
                            pT_sb[:, si * 512: si * 512 + 512],
                            start=False, stop=False,
                            tile_position=(0, HROW[h])),
                            waits=[src_op],
                            tag=f"l{s}w{w}j{jc}h{h}")
                        lb.set_writer(mml)
                        sl.consumed(mml)
            rl_ops = []
            for g in range(2):
                lb = lbk[g]
                for hh in range(4):
                    row = 32 * hh
                    rc = sch.v(lambda e, g=g, row=row, lb=lb: e.reciprocal(
                        rl_sb[row:row + 1, g * 512:(g + 1) * 512],
                        lb.ap[row:row + 1, 0:512]),
                        waits=lb.rwaits(), tag=f"rl{s}w{w}g{g}r{row}")
                    lb.add_reader(rc)
                    rl_ops.append(rc)
            if s == 0 and w == 0 and "rl0" in taps:
                td = mktap("rl0", [128, 1024])
                sch.q(lambda e, td=td: e.dma_start(td.ap(), rl_sb[:]),
                      waits=rl_ops, tag="tap_rl0")
            outT_evs = []
            for p in range(4):
                g = p // 2
                trl_b, tg_b = sbk[0], sbk[1]
                m1 = sch.t(lambda e, p=p, g=g, trl_b=trl_b: e.matmul(
                    trl_b.ap[:, 0:512], selr_f[:, p * 128:(p + 1) * 128],
                    rl_sb[:, g * 512:(g + 1) * 512], start=True, stop=True),
                    waits=rl_ops + [ld_selr] + trl_b.wwaits(), tag=f"trl{s}w{w}p{p}")
                trl_b.set_writer(m1)
                m2 = sch.t(lambda e, p=p, w=w, tg_b=tg_b: e.matmul(
                    tg_b.ap[:, 0:512], selg_f[0:8, p * 128:(p + 1) * 128],
                    gT_sb[0:8, w * 512:(w + 1) * 512], start=True, stop=True),
                    waits=g_done + [ld_selg] + tg_b.wwaits(), tag=f"tg{s}w{w}p{p}")
                tg_b.set_writer(m2)
                c1_ = sch.v(lambda e, tg_b=tg_b: e.tensor_copy(
                    tg_sb[:, 0:512], tg_b.ap[:, 0:512]),
                    waits=tg_b.rwaits(), tag=f"tgc{s}w{w}p{p}")
                tg_b.add_reader(c1_)
                c2_ = sch.v(lambda e, trl_b=trl_b: e.tensor_tensor(
                    tg_sb[:, 512:1024], trl_b.ap[:, 0:512], tg_sb[:, 0:512],
                    op=ALU.mult), waits=trl_b.rwaits() + [c1_], tag=f"tgm{s}w{w}p{p}")
                trl_b.add_reader(c2_)
                pb_ = pvb[p]
                ev = sch.v(lambda e, p=p, pb_=pb_: e.tensor_tensor(
                    outT_sb[:, p * 512:(p + 1) * 512], pb_.ap[:, 0:512],
                    tg_sb[:, 512:1024], op=ALU.mult),
                    waits=pb_.rwaits() + [c2_] + (outT_slot.pwaits() if p == 0 else []),
                    tag=f"oev{s}w{w}p{p}")
                pb_.add_reader(ev)
                outT_evs.append(ev)
            outT_slot.produced(outT_evs[-1])
            if s == 0 and w == 0 and "outT0" in taps:
                td = mktap("outT0", [128, 2048], BF16)
                sch.q(lambda e, td=td: e.dma_start(td.ap(), outT_sb[:]),
                      waits=outT_evs, tag="tap_outT0")
            for k in range(4):
                it = 4 * w + k
                wslot = xr[k % 2]
                xa = x_tile[:, (k % 2) * D:(k % 2 + 1) * D]
                ldx = sch.q(lambda e, s=s, it=it, xa=xa: e.dma_start(
                    xa, x_d[s, it * 128:(it + 1) * 128, :]),
                    waits=wslot.pwaits(), tag=f"xr{s}_{it}")
                wslot.produced(ldx)
                pbs = [lbk[0], lbk[1]]
                for half in range(2):
                    pbh = pbs[half]
                    mms = []
                    for p in range(4):
                        mm = sch.t(lambda e, p=p, k=k, half=half, pbh=pbh: e.matmul(
                            pbh.ap[:, 0:512],
                            outT_sb[:, p * 512 + k * 128: p * 512 + (k + 1) * 128],
                            wo_l[:, p * D + half * 512: p * D + (half + 1) * 512],
                            start=(p == 0), stop=(p == 3)),
                            waits=[outT_evs[p]] + (pbh.wwaits() if p == 0 else []),
                            tag=f"pr{s}w{w}k{k}h{half}p{p}")
                        mms.append(mm)
                    pbh.set_writer(mms[-1])
                    outT_slot.consumed(mms[-1])
                radd = []
                for half in range(2):
                    pbh = pbs[half]
                    ra = sch.v(lambda e, half=half, xa=xa, pbh=pbh: e.scalar_tensor_tensor(
                        sq_scr[:, half * 512:(half + 1) * 512],
                        xa[:, half * 512:(half + 1) * 512], 1.0, pbh.ap[:, 0:512],
                        op0=ALU.mult, op1=ALU.add),
                        waits=[ldx] + pbh.rwaits() + (rsq.pwaits() if half == 0 else []),
                        tag=f"ra{s}w{w}k{k}h{half}")
                    pbh.add_reader(ra)
                    radd.append(ra)
                wslot.consumed(radd[-1])
                rsq.produced(radd[-1])
                if s == 0 and w == 0 and k == 0 and "r0" in taps:
                    td = mktap("r0", [128, D])
                    sch.q(lambda e, td=td: e.dma_start(td.ap(), sq_scr[:, 0:D]),
                          waits=radd, tag="tap_r0")
                s1_ = sch.s(lambda e: e.activation(
                    dump_scr[:, 0:D], sq_scr[:, 0:D], AF.Copy,
                    accum_out=stat[:, 11:12]), waits=radd, tag=f"lns{s}w{w}k{k}")
                s2_ = sch.s(lambda e: e.activation(
                    dump_scr[:, 0:D], sq_scr[:, 0:D], AF.Square,
                    accum_out=stat[:, 12:13]), waits=[s1_], tag=f"lnq{s}w{w}k{k}")
                m8 = sch.v(lambda e: e.tensor_scalar(
                    stat[:, 16:24], stat[:, 11:12].broadcast_to([128, 8]), 1.0 / D,
                    None, op0=ALU.mult), waits=[s2_], tag=f"lnm{s}w{w}k{k}")
                v8 = sch.v(lambda e: e.tensor_scalar(
                    stat[:, 24:32], stat[:, 12:13].broadcast_to([128, 8]), 1.0 / D,
                    None, op0=ALU.mult), waits=[s2_], tag=f"lnv{s}w{w}k{k}")
                msq = sch.v(lambda e: e.tensor_tensor(
                    stat[:, 32:40], stat[:, 16:24], stat[:, 16:24], op=ALU.mult),
                    waits=[m8], tag=f"lnw{s}w{w}k{k}")
                var8 = sch.v(lambda e: e.tensor_tensor(
                    stat[:, 40:48], stat[:, 24:32], stat[:, 32:40], op=ALU.subtract),
                    waits=[v8, msq], tag=f"lnx{s}w{w}k{k}")
                rstd = rsqrt_col(var8, 40, 14, LN_EPS, f"lnr{s}w{w}k{k}", scratch_col=15)
                nb8 = sch.v(lambda e: e.tensor_scalar(
                    stat[:, 32:40], stat[:, 16:24], stat[:, 14:15], -1.0,
                    op0=ALU.mult, op1=ALU.mult), waits=[rstd], tag=f"lnb{s}w{w}k{k}")
                if s == 0 and w == 0 and k == 0 and "st0" in taps:
                    td = mktap("st0", [128, 64])
                    sch.q(lambda e, td=td: e.dma_start(td.ap(), stat[:]),
                          waits=[nb8], tag="tap_st0")
                ysl = y1slot[k % 2]
                ya = y1_stage[:, (k % 2) * D:(k % 2 + 1) * D]
                nrm = sch.v(lambda e, ya=ya: e.tensor_scalar(
                    ya, sq_scr[:, 0:D], stat[:, 14:15], stat[:, 32:33],
                    op0=ALU.mult, op1=ALU.add),
                    waits=[nb8] + ysl.pwaits(), tag=f"lnn{s}w{w}k{k}")
                rsq.consumed(nrm)
                ysl.produced(nrm)
                sty = sch.q(lambda e, it=it, ya=ya: e.dma_start(
                    y1_d[it * 128:(it + 1) * 128, :], ya),
                    waits=[nrm], tag=f"y1{s}_{it}")
                ysl.consumed(sty)
                a4_done.append(sty)
        if "y1" in taps and s == 0:
            td = mktap("y1", [N, D])
            a4_done.append(sch.q(lambda e, td=td: e.dma_start(td.ap(), y1_d.ap()),
                                 waits=list(a4_done), tag="tap_y1"))
        prev_seq_done = a2_done + a4_done
        return prev_seq_done

    def phase_ffn(s):
        nonlocal prev_seq_done
        deps = list(prev_seq_done)
        ffn_done = []
        ldmw = sch.q(lambda e: e.dma_start(
            meanw_sb[:, 0:8], scal_d[0:1, 0:8].broadcast_to([128, 8])),
            waits=deps, tag=f"Fmw{s}")
        cm_ops = []
        for j, cnt in ((0, INNER * D), (1, INNER * D), (2, D * INNER)):
            cm_ops.append(sch.v(lambda e, j=j, cnt=cnt: e.tensor_scalar(
                meanw_sb[:, 8 + 8 * j:16 + 8 * j],
                meanw_sb[:, j:j + 1].broadcast_to([128, 8]),
                1.0 / cnt, QEPS, op0=ALU.mult, op1=ALU.max),
                waits=[ldmw], tag=f"Fcmw{s}_{j}"))
        ldw_ops = []
        for mat_d, dst in ((wgT_d, wgT_sb), (wuT_d, wuT_sb)):
            for dc in range(DC):
                ldw_ops.append(sch.q(lambda e, mat_d=mat_d, dst=dst, dc=dc: e.dma_start(
                    dst[:, dc * INNER:(dc + 1) * INNER],
                    mat_d[dc * 128:(dc + 1) * 128, :]),
                    waits=deps if dc == 0 else [], tag=f"F{s}w{dc}"))
        y1s = [Slot(), Slot()]
        xq2Ts = Slot()
        oslot = [Slot(), Slot()]
        wdslot = [Slot(), Slot()]
        hbanks = banks[0:7]
        sbank = banks[7]
        tpb = banks[4]

        def emit_ln2(r_waits, out_ap, tag):
            s1_ = sch.s(lambda e: e.activation(
                dump_scr[:, 0:D], sq_scr[:, 0:D], AF.Copy,
                accum_out=stat[:, 11:12]), waits=r_waits, tag=tag + "s1")
            s2_ = sch.s(lambda e: e.activation(
                dump_scr[:, 0:D], sq_scr[:, 0:D], AF.Square,
                accum_out=stat[:, 12:13]), waits=[s1_], tag=tag + "s2")
            m8 = sch.v(lambda e: e.tensor_scalar(
                stat[:, 16:24], stat[:, 11:12].broadcast_to([128, 8]), 1.0 / D,
                None, op0=ALU.mult), waits=[s2_], tag=tag + "m8")
            v8 = sch.v(lambda e: e.tensor_scalar(
                stat[:, 24:32], stat[:, 12:13].broadcast_to([128, 8]), 1.0 / D,
                None, op0=ALU.mult), waits=[s2_], tag=tag + "v8")
            msq = sch.v(lambda e: e.tensor_tensor(
                stat[:, 32:40], stat[:, 16:24], stat[:, 16:24], op=ALU.mult),
                waits=[m8], tag=tag + "msq")
            var8 = sch.v(lambda e: e.tensor_tensor(
                stat[:, 40:48], stat[:, 24:32], stat[:, 32:40], op=ALU.subtract),
                waits=[v8, msq], tag=tag + "var")
            rstd = rsqrt_col(var8, 40, 14, LN_EPS, tag + "rs", scratch_col=15)
            nb8 = sch.v(lambda e: e.tensor_scalar(
                stat[:, 32:40], stat[:, 16:24], stat[:, 14:15], -1.0,
                op0=ALU.mult, op1=ALU.mult), waits=[rstd], tag=tag + "nb")
            return sch.v(lambda e, out_ap=out_ap: e.tensor_scalar(
                out_ap, sq_scr[:, 0:D], stat[:, 14:15], stat[:, 32:33],
                op0=ALU.mult, op1=ALU.add), waits=[nb8], tag=tag + "nrm")

        for grp in range(NT // 2):
            tile_quant_done = []
            for ttn in range(2):
                ti = grp * 2 + ttn
                ya = y1t[:, ttn * D:(ttn + 1) * D]
                ldy = sch.q(lambda e, ti=ti, ya=ya: e.dma_start(
                    ya, y1_d[ti * 128:(ti + 1) * 128, :]),
                    waits=deps + y1s[ttn].pwaits(), tag=f"Fy{s}_{ti}")
                y1s[ttn].produced(ldy)
                q1a = sch.v(lambda e, ya=ya: e.tensor_scalar(
                    dump_scr[:, 0:D], ya, 1.0, None, op0=ALU.mult, op1=ALU.max,
                    accum_out=stat[:, 20:21]), waits=[ldy], tag=f"Fq1a{s}_{ti}")
                q1b = sch.v(lambda e, ya=ya: e.tensor_scalar(
                    dump_scr[:, 0:D], ya, -1.0, None, op0=ALU.mult, op1=ALU.max,
                    accum_out=stat[:, 21:22]), waits=[ldy], tag=f"Fq1b{s}_{ti}")
                y1s[ttn].consumed(q1a)
                y1s[ttn].consumed(q1b)
                rm2 = sch.v(lambda e: e.tensor_scalar(
                    stat[:, 0:8], stat[:, 20:21].broadcast_to([128, 8]),
                    stat[:, 21:22], QEPS, op0=ALU.max, op1=ALU.max),
                    waits=[q1a, q1b], tag=f"Frm{s}_{ti}")
                l1 = sch.s(lambda e: e.activation(stat[:, 9:10], stat[:, 0:1], AF.Ln),
                           waits=[rm2], tag=f"Fl1{s}_{ti}")
                l2 = sch.s(lambda e: e.activation(stat[:, 10:11], stat[:, 9:10],
                                                  AF.Exp, scale=-1.0),
                           waits=[l1], tag=f"Fl2{s}_{ti}")
                sx = sch.s(lambda e: e.mul(stat[:, 9:10], stat[:, 10:11], 127.0),
                           waits=[l2], tag=f"Fsx{s}_{ti}")
                cg = sch.v(lambda e: e.tensor_scalar(
                    stat[:, 24:32], stat[:, 0:1].broadcast_to([128, 8]),
                    meanw_sb[:, 8:9], 1.0 / 127.0, op0=ALU.mult, op1=ALU.mult),
                    waits=[rm2] + cm_ops, tag=f"Fcg{s}_{ti}")
                cu = sch.v(lambda e: e.tensor_scalar(
                    stat[:, 32:40], stat[:, 0:1].broadcast_to([128, 8]),
                    meanw_sb[:, 16:17], 1.0 / 127.0, op0=ALU.mult, op1=ALU.mult),
                    waits=[rm2] + cm_ops, tag=f"Fcu{s}_{ti}")
                xq_ops = []
                for half in range(2):
                    qi = sch.v(lambda e, ya=ya, half=half: e.tensor_scalar(
                        xqi[:, 0:512], ya[:, half * 512:(half + 1) * 512],
                        stat[:, 9:10], None, op0=ALU.mult),
                        waits=[sx] + xq_ops[-1:], tag=f"Fxi{s}_{ti}_{half}")
                    y1s[ttn].consumed(qi)
                    qb = sch.v(lambda e, half=half: e.tensor_copy(
                        xqb[:, half * 512:(half + 1) * 512], xqi[:, 0:512]),
                        waits=[qi], tag=f"Fxb{s}_{ti}_{half}")
                    xq_ops.append(qb)
                tdst = tpb.ap.bitcast(BF16)
                tms = []
                for dc in range(DC):
                    tm = sch.t(lambda e, dc=dc, tdst=tdst: e.transpose(
                        tdst[:, dc * 128:(dc + 1) * 128],
                        xqb[:, dc * 128:(dc + 1) * 128], ident_b[:]),
                        waits=xq_ops + (tpb.wwaits() if dc == 0 else []),
                        tag=f"FxT{s}_{ti}_{dc}")
                    tms.append(tm)
                tpb.set_writer(tms[-1])
                xTe = sch.v(lambda e, tdst=tdst: e.tensor_copy(xqT[:, 0:1024],
                                                               tdst[:, 0:1024]),
                            waits=tpb.rwaits(), tag=f"FxTe{s}_{ti}")
                tpb.add_reader(xTe)
                hready = []
                for ob in range(8):
                    gms = []
                    for dc in range(DC):
                        gm = sch.t(lambda e, dc=dc, ob=ob: e.matmul(
                            sbank.ap[:, 0:512],
                            xqT[:, dc * 128:(dc + 1) * 128],
                            wgT_sb[:, dc * INNER + ob * 512: dc * INNER + (ob + 1) * 512],
                            start=(dc == 0), stop=(dc == DC - 1)),
                            waits=([xTe] + ldw_ops + sbank.wwaits()) if dc == 0 else [],
                            tag=f"Fg{s}_{ti}_{ob}_{dc}")
                        gms.append(gm)
                    sbank.set_writer(gms[-1])
                    sil = sch.s(lambda e: e.activation(
                        sg_sb[:, 0:512], sbank.ap[:, 0:512], AF.Silu,
                        scale=stat[:, 24:25]), waits=sbank.rwaits() + [cg],
                        tag=f"Fsl{s}_{ti}_{ob}")
                    sbank.add_reader(sil)
                    ums = []
                    for dc in range(DC):
                        um = sch.t(lambda e, dc=dc, ob=ob: e.matmul(
                            sbank.ap[:, 0:512],
                            xqT[:, dc * 128:(dc + 1) * 128],
                            wuT_sb[:, dc * INNER + ob * 512: dc * INNER + (ob + 1) * 512],
                            start=(dc == 0), stop=(dc == DC - 1)),
                            waits=sbank.wwaits() if dc == 0 else [],
                            tag=f"Fu{s}_{ti}_{ob}_{dc}")
                        ums.append(um)
                    sbank.set_writer(ums[-1])
                    if ob < 7:
                        hb = hbanks[ob]
                        hm = sch.v(lambda e, hb=hb: e.scalar_tensor_tensor(
                            hb.ap[:, 0:512], sg_sb[:, 0:512], stat[:, 32:33],
                            sbank.ap[:, 0:512], op0=ALU.mult, op1=ALU.mult),
                            waits=[sil, cu] + sbank.rwaits() + hb.wwaits() + hb.rwaits(),
                            tag=f"Fh{s}_{ti}_{ob}")
                        hb.set_writer(hm)
                        sbank.add_reader(hm)
                    else:
                        hm = sch.v(lambda e: e.scalar_tensor_tensor(
                            h7_sb[:, 0:512], sg_sb[:, 0:512], stat[:, 32:33],
                            sbank.ap[:, 0:512], op0=ALU.mult, op1=ALU.mult),
                            waits=[sil, cu] + sbank.rwaits(), tag=f"Fh{s}_{ti}_{ob}")
                        sbank.add_reader(hm)
                    hready.append(hm)
                pm_ops = []
                for ob in range(8):
                    hsrc = hbanks[ob].ap if ob < 7 else h7_sb
                    for sgn in range(2):
                        pm = sch.v(lambda e, hsrc=hsrc, ob=ob, sgn=sgn: e.tensor_scalar(
                            dump_scr[:, 0:512], hsrc[:, 0:512],
                            1.0 if sgn == 0 else -1.0, None,
                            op0=ALU.mult, op1=ALU.max,
                            accum_out=attn_scr[:, 8 * sgn + ob:8 * sgn + ob + 1]),
                            waits=[hready[ob]], tag=f"Fp{s}_{ti}_{ob}_{sgn}")
                        if ob < 7:
                            hbanks[ob].add_reader(pm)
                        pm_ops.append(pm)
                cmb = sch.v(lambda e: e.tensor_scalar(
                    dump_scr[:, 0:16], attn_scr[:, 0:16], 1.0, None,
                    op0=ALU.mult, op1=ALU.max, accum_out=stat[:, 20:21]),
                    waits=pm_ops, tag=f"Fcm{s}_{ti}")
                rm2b = sch.v(lambda e: e.tensor_scalar(
                    stat[:, 0:8], stat[:, 20:21].broadcast_to([128, 8]), QEPS, None,
                    op0=ALU.max), waits=[cmb], tag=f"Frm2{s}_{ti}")
                b1 = sch.s(lambda e: e.activation(stat[:, 10:11], stat[:, 0:1], AF.Ln),
                           waits=[rm2b], tag=f"Fb1{s}_{ti}")
                b2 = sch.s(lambda e, ttn=ttn: e.activation(
                    stat[:, 48 + 4 * ttn:49 + 4 * ttn], stat[:, 10:11], AF.Exp,
                    scale=-1.0), waits=[b1], tag=f"Fb2{s}_{ti}")
                s2f = sch.s(lambda e, ttn=ttn: e.mul(
                    stat[:, 48 + 4 * ttn:49 + 4 * ttn],
                    stat[:, 48 + 4 * ttn:49 + 4 * ttn], 127.0),
                    waits=[b2], tag=f"Fs2{s}_{ti}")
                cd1 = sch.s(lambda e, ttn=ttn: e.mul(
                    stat[:, 49 + 4 * ttn:50 + 4 * ttn], stat[:, 0:1],
                    1.0 / 127.0), waits=[rm2b], tag=f"Fcd1{s}_{ti}")
                cdn = sch.s(lambda e, ttn=ttn: e.mul(
                    stat[:, 49 + 4 * ttn:50 + 4 * ttn],
                    stat[:, 49 + 4 * ttn:50 + 4 * ttn], meanw_sb[:, 24:25]),
                    waits=[cd1] + cm_ops, tag=f"Fcd2{s}_{ti}")
                ffn_done.append(cdn)
                for ob in range(8):
                    hsrc = hbanks[ob].ap if ob < 7 else h7_sb
                    qi2 = sch.v(lambda e, hsrc=hsrc, ttn=ttn: e.tensor_scalar(
                        xqi[:, 0:512], hsrc[:, 0:512],
                        stat[:, 48 + 4 * ttn:49 + 4 * ttn], None, op0=ALU.mult),
                        waits=[s2f], tag=f"Fq2{s}_{ti}_{ob}")
                    if ob < 7:
                        hbanks[ob].add_reader(qi2)
                    qb2 = sch.v(lambda e: e.tensor_copy(xq2b[:, 0:512], xqi[:, 0:512]),
                                waits=[qi2], tag=f"Fqb2{s}_{ti}_{ob}")
                    if ob < 7:
                        tb2 = hbanks[ob]
                    else:
                        tb2 = sbank
                    tdst2 = tb2.ap.bitcast(BF16)
                    tms2 = []
                    for c4 in range(4):
                        tm2 = sch.t(lambda e, c4=c4, tdst2=tdst2: e.transpose(
                            tdst2[:, c4 * 128:(c4 + 1) * 128],
                            xq2b[:, c4 * 128:(c4 + 1) * 128], ident_b[:]),
                            waits=[qb2] + (tb2.wwaits() if c4 == 0 else []),
                            tag=f"FqT{s}_{ti}_{ob}_{c4}")
                        tms2.append(tm2)
                    tb2.set_writer(tms2[-1])
                    t2e = sch.v(lambda e, ttn=ttn, ob=ob, tdst2=tdst2: e.tensor_copy(
                        xq2T[:, ttn * 4096 + ob * 512: ttn * 4096 + (ob + 1) * 512],
                        tdst2[:, 0:512]),
                        waits=tb2.rwaits() + (xq2Ts.pwaits() if ttn == 0 and ob == 0 else []),
                        tag=f"FqTe{s}_{ti}_{ob}")
                    tb2.add_reader(t2e)
                    tile_quant_done.append(t2e)
                if s == 0 and grp == 0 and ttn == 0 and "ffn0" in taps:
                    td = mktap("f_xqb", [128, D], BF16)
                    sch.q(lambda e, td=td: e.dma_start(td.ap(), xqb[:]),
                          waits=list(tile_quant_done), tag="tap_fxqb")
                    td2 = mktap("f_xq2T", [128, 8192], BF16)
                    sch.q(lambda e, td2=td2: e.dma_start(td2.ap(), xq2T[:]),
                          waits=list(tile_quant_done), tag="tap_fxq2T")
                    td3 = mktap("f_y1t", [128, 2 * D])
                    sch.q(lambda e, td3=td3: e.dma_start(td3.ap(), y1t[:]),
                          waits=list(tile_quant_done), tag="tap_fy1t")
                    td4 = mktap("f_stat", [128, 64])
                    sch.q(lambda e, td4=td4: e.dma_start(td4.ap(), stat[:]),
                          waits=list(tile_quant_done), tag="tap_fstat")
                    td5 = mktap("f_meanw", [128, 32])
                    sch.q(lambda e, td5=td5: e.dma_start(td5.ap(), meanw_sb[:]),
                          waits=list(tile_quant_done), tag="tap_fmeanw")
            xq2Ts.produced(tile_quant_done[-1])
            if False and "ffn0" in taps:
                td = mktap("f_xqb", [128, D], BF16)
                sch.q(lambda e, td=td: e.dma_start(td.ap(), xqb[:]),
                      waits=tile_quant_done, tag="tap_fxqb")
                td2 = mktap("f_xq2T", [128, 8192], BF16)
                sch.q(lambda e, td2=td2: e.dma_start(td2.ap(), xq2T[:]),
                      waits=tile_quant_done, tag="tap_fxq2T")
                td3 = mktap("f_y1t", [128, 2 * D])
                sch.q(lambda e, td3=td3: e.dma_start(td3.ap(), y1t[:]),
                      waits=tile_quant_done, tag="tap_fy1t")
                td4 = mktap("f_stat", [128, 64])
                sch.q(lambda e, td4=td4: e.dma_start(td4.ap(), stat[:]),
                      waits=tile_quant_done, tag="tap_fstat")
            dbanks = [(banks[0], banks[1]), (banks[2], banks[3])]
            for ttn in range(2):
                for half in range(2):
                    b = dbanks[ttn][half]
            dmms = {0: [], 1: []}
            for oc in range(32):
                wsl = wdslot[oc % 2]
                ldw = sch.g(lambda e, oc=oc: e.dma_start(
                    wdb[:, (oc % 2) * D:(oc % 2 + 1) * D],
                    wdT_d[oc * 128:(oc + 1) * 128, :]),
                    waits=wsl.pwaits() + (tile_quant_done if oc < 2 else []),
                    tag=f"Fwd{s}_{grp}_{oc}")
                wsl.produced(ldw)
                for ttn in range(2):
                    for half in range(2):
                        b = dbanks[ttn][half]
                        mm = sch.t(lambda e, oc=oc, ttn=ttn, half=half, b=b: e.matmul(
                            b.ap[:, 0:512],
                            xq2T[:, ttn * 4096 + oc * 128: ttn * 4096 + (oc + 1) * 128],
                            wdb[:, (oc % 2) * D + half * 512:(oc % 2) * D + (half + 1) * 512],
                            start=(oc == 0), stop=(oc == 31)),
                            waits=[ldw] + tile_quant_done +
                                  (b.wwaits() if oc == 0 else []),
                            tag=f"Fd{s}_{grp}_{oc}_{ttn}_{half}")
                        if oc == 0:
                            b.set_writer(mm)
                        else:
                            b.writer = mm
                        dmms[ttn].append(mm)
                        wsl.consumed(mm)
                        xq2Ts.consumed(mm)
            for ttn in range(2):
                ti = grp * 2 + ttn
                ya = y1t[:, ttn * D:(ttn + 1) * D]
                ev_ops = []
                for half in range(2):
                    b = dbanks[ttn][half]
                    e1 = sch.v(lambda e, b=b, half=half, ttn=ttn: e.tensor_scalar(
                        sq_scr[:, half * 512:(half + 1) * 512], b.ap[:, 0:512],
                        stat[:, 49 + 4 * ttn:50 + 4 * ttn], None, op0=ALU.mult),
                        waits=[dmms[ttn][-1]] + rsq.pwaits() if False else
                              [dmms[ttn][-1]], tag=f"Fe{s}_{ti}_{half}")
                    b.add_reader(e1)
                    e2 = sch.v(lambda e, half=half, ya=ya: e.tensor_tensor(
                        sq_scr[:, half * 512:(half + 1) * 512],
                        sq_scr[:, half * 512:(half + 1) * 512],
                        ya[:, half * 512:(half + 1) * 512], op=ALU.add),
                        waits=[e1], tag=f"Fr{s}_{ti}_{half}")
                    y1s[ttn].consumed(e2)
                    ev_ops.append(e2)
                osl = oslot[ttn]
                oa = o_stage[:, ttn * D:(ttn + 1) * D]
                nrm2 = emit_ln2(ev_ops + osl.pwaits(), oa, f"FL{s}_{ti}")
                osl.produced(nrm2)
                sto = sch.q(lambda e, s=s, ti=ti, oa=oa: e.dma_start(
                    out_d[s, ti * 128:(ti + 1) * 128, :], oa),
                    waits=[nrm2], tag=f"out{s}_{ti}")
                osl.consumed(sto)
                ffn_done.append(sto)
        prev_seq_done = ffn_done
        return ffn_done

    for s in range(SEQ_PER_CORE if stop_after != "A2_seq0" else 1):
        if stop_after in ("consts", "prep1", "prep2", "prepall"):
            break
        phase_attn(s)
        if stop_after in ("A2", "A2_seq0"):
            break
        phase_ffn(s)

    if stop_after is not None:
        zfill = sch.g(lambda e: e.memset(sq_scr[:], 0.0), dma=False,
                      waits=prev_seq_done, tag="zf")
        for s in range(SEQ_PER_CORE):
            for ti in range(NT):
                sch.q(lambda e, s=s, ti=ti: e.dma_start(
                    out_d[s, ti * 128:(ti + 1) * 128, :], sq_scr[:]),
                    waits=[zfill], tag=f"zero{s}_{ti}")

    sch.finalize(nc)
    return nc, tap_d


def kernel(L0_attn_tokens, L0_attn_wq, L0_attn_wk, L0_attn_wv, L0_attn_wo,
           L0_attn_w_lr, L0_attn_w_target, L0_attn_w_gates, L0_attn_rms_w,
           L0_ffn_w_gate, L0_ffn_w_up, L0_ffn_w_down):
    x = np.ascontiguousarray(np.asarray(L0_attn_tokens, dtype=np.float32))
    consts = _host_consts()
    nc, tap_d = build_program(taps=TAPS, serial=SERIAL, stop_after=STOP_AFTER)
    in_maps = []
    for c in range(NCORES):
        m = {
            "x": x[c * SEQ_PER_CORE:(c + 1) * SEQ_PER_CORE],
            "wq": np.ascontiguousarray(np.asarray(L0_attn_wq, np.float32)),
            "wk": np.ascontiguousarray(np.asarray(L0_attn_wk, np.float32)),
            "wv": np.ascontiguousarray(np.asarray(L0_attn_wv, np.float32)),
            "wo": np.ascontiguousarray(np.asarray(L0_attn_wo, np.float32)),
            "wgates": np.ascontiguousarray(np.asarray(L0_attn_w_gates, np.float32)),
            "rmsw": np.ascontiguousarray(np.asarray(L0_attn_rms_w, np.float32)).reshape(1, D),
            "wgate": np.ascontiguousarray(np.asarray(L0_ffn_w_gate, np.float32)),
            "wup": np.ascontiguousarray(np.asarray(L0_ffn_w_up, np.float32)),
            "wdown": np.ascontiguousarray(np.asarray(L0_ffn_w_down, np.float32)),
        }
        m.update(consts)
        in_maps.append(m)
    res = run_bass_kernel_spmd(nc, in_maps, core_ids=list(range(NCORES)))
    kernel._last_results = res
    out = np.concatenate([np.asarray(r["out"]) for r in res.results], axis=0)
    return out


# version 05
# speedup vs baseline: 1.2861x; 1.0748x over previous
import numpy as np
import ml_dtypes
import concourse.bass as bass
import concourse.mybir as mybir
from concourse.bass_utils import run_bass_kernel_spmd

F32 = mybir.dt.float32
BF16 = mybir.dt.bfloat16
I32 = mybir.dt.int32
AF = mybir.ActivationFunctionType
ALU = mybir.AluOpType
AX = mybir.AxisListType

B, N, D, H, DH = 32, 2048, 1024, 8, 64
NCORES = 8
SEQ_PER_CORE = B // NCORES
NT = N // 128
NW = N // 512
DC = D // 128
INNER = 4096
ATTN_SCALE = float(DH) ** -0.5
RMS_EPS = float(np.finfo(np.float32).eps)
LN_EPS = 1e-5
QEPS = 1e-5

SERIAL = False
TAPS = ()
STOP_AFTER = None


class _Op:
    __slots__ = ("engine", "fn", "waits", "need_inc", "inc_amt", "eidx", "sem_val",
                 "tag", "is_dma", "semkey", "waited", "need_drain")

    def __init__(self, engine, fn, waits, inc_amt, eidx, tag, is_dma):
        self.engine = engine
        self.fn = fn
        self.waits = list(waits)
        self.need_inc = False
        self.inc_amt = inc_amt
        self.eidx = eidx
        self.sem_val = None
        self.tag = tag
        self.is_dma = is_dma
        self.semkey = engine + "_dma" if is_dma else engine
        self.waited = False
        self.need_drain = False


class Sched:
    ENGINES = ("tensor", "vector", "scalar", "sync", "gpsimd")

    def __init__(self, serial=False):
        self.ops = {e: [] for e in self.ENGINES}
        self.serial = serial
        self.order = []

    def emit(self, engine, fn, waits=(), dma=False, tag=""):
        ws = []
        for w in waits:
            if w is None or w is False:
                continue
            if not isinstance(w, _Op):
                raise TypeError(f"bad wait {w!r} at {tag}")
            if w.engine != engine:
                ws.append(w)
            elif engine == "vector":
                w.need_drain = True
        op = _Op(engine, fn, ws, 16 if dma else 1, len(self.ops[engine]), tag, dma)
        if dma:
            op.need_inc = True
        self.ops[engine].append(op)
        self.order.append(op)
        for w in op.waits:
            w.need_inc = True
            w.waited = True
        return op

    def t(self, fn, waits=(), tag=""):
        return self.emit("tensor", fn, waits, tag=tag)

    def v(self, fn, waits=(), tag=""):
        return self.emit("vector", fn, waits, tag=tag)

    def s(self, fn, waits=(), tag=""):
        return self.emit("scalar", fn, waits, tag=tag)

    def q(self, fn, waits=(), tag=""):
        return self.emit("sync", fn, waits, dma=True, tag=tag)

    def g(self, fn, waits=(), dma=True, tag=""):
        return self.emit("gpsimd", fn, waits, dma=dma, tag=tag)

    def finalize(self, nc):
        if self.serial:
            prev = None
            for op in self.order:
                if prev is not None and prev.engine != op.engine:
                    op.waits.append(prev)
                    prev.need_inc = True
                    prev.waited = True
                prev = op
        semkeys = set()
        for e in self.ENGINES:
            for op in self.ops[e]:
                semkeys.add(op.semkey)
        sems = {k: nc.alloc_semaphore(f"sem_{k}") for k in sorted(semkeys)}
        with nc.Block() as b0:
            @b0.gpsimd
            def _(g):
                for k in sorted(sems):
                    g.sem_clear(sems[k])
        nc.all_engine_barrier()
        totals = {k: 0 for k in semkeys}
        for e in self.ENGINES:
            cum = {}
            for op in self.ops[e]:
                if op.need_inc:
                    cum[op.semkey] = cum.get(op.semkey, 0) + op.inc_amt
                    op.sem_val = cum[op.semkey]
            for k, vv in cum.items():
                totals[k] = vv

        def replay(engine_obj, ename):
            for op in self.ops[ename]:
                best = {}
                for w in op.waits:
                    if w.sem_val is None:
                        raise RuntimeError(f"wait target lacks inc: {w.tag}")
                    if w.sem_val > best.get(w.semkey, -1):
                        best[w.semkey] = w.sem_val
                for we in sorted(best):
                    engine_obj.wait_ge(sems[we], best[we])
                inst = op.fn(engine_obj)
                if op.engine == "vector" and (op.need_inc or op.need_drain):
                    inst = engine_obj.drain()
                if op.need_inc:
                    if inst is None:
                        raise RuntimeError(f"op needs inc but returned None: {op.tag}")
                    inst.then_inc(sems[op.semkey], op.inc_amt)
                if op.is_dma and op.waited:
                    engine_obj.wait_ge(sems[op.semkey], op.sem_val)

        with nc.Block() as block:
            @block.tensor
            def _(t):
                replay(t, "tensor")

            @block.vector
            def _(v):
                replay(v, "vector")

            @block.scalar
            def _(s):
                replay(s, "scalar")

            @block.gpsimd
            def _(g):
                replay(g, "gpsimd")

            @block.sync
            def _(s):
                replay(s, "sync")
                for k in sorted(totals):
                    if totals[k] and k != "sync_dma":
                        s.wait_ge(sems[k], totals[k])
                if totals.get("sync_dma"):
                    s.wait_ge(sems["sync_dma"], totals["sync_dma"])


class Bank:

    def __init__(self, ap):
        self.ap = ap
        self.readers = []
        self.writer = None

    def wwaits(self):
        return list(self.readers)

    def set_writer(self, op):
        self.writer = op
        self.readers = []

    def rwaits(self):
        return [self.writer] if self.writer is not None else []

    def add_reader(self, op):
        self.readers.append(op)


class Slot:

    def __init__(self):
        self.consumers = []
        self.producer = None

    def pwaits(self):
        return list(self.consumers)

    def produced(self, op):
        self.producer = op
        self.consumers = []

    def cwaits(self):
        return [self.producer] if self.producer is not None else []

    def consumed(self, op):
        self.consumers.append(op)


def _host_consts():
    c = {}
    c["c_ident"] = np.eye(128, dtype=np.float32)
    c["c_ones"] = np.ones((128, 128), dtype=np.float32)
    masks = np.zeros((4, 128, 512), dtype=np.float32)
    jj = np.arange(128)[:, None]
    ii = np.arange(512)[None, :]
    for r, rel in enumerate((0, 128, 256, 384)):
        masks[r] = (ii >= rel + jj).astype(np.float32)
    c["c_masks"] = masks.reshape(4 * 128, 512)
    selr = np.zeros((4, 128, 128), dtype=np.float32)
    for p in range(4):
        selr[p, 32 * ((2 * p) % 4), 0:64] = 1.0
        selr[p, 32 * ((2 * p + 1) % 4), 64:128] = 1.0
    c["c_selr"] = selr.transpose(1, 0, 2).reshape(128, 4 * 128)
    selg = np.zeros((128, 4 * 128), dtype=np.float32)
    for p in range(4):
        selg[2 * p, p * 128: p * 128 + 64] = 1.0
        selg[2 * p + 1, p * 128 + 64: p * 128 + 128] = 1.0
    c["c_selg"] = selg
    return c


def build_program(taps=(), serial=False, stop_after=None):
    nc = bass.Bass("TRN2", target_bir_lowering=False)
    sch = Sched(serial=serial)
    taps = set(taps)

    x_d = nc.dram_tensor("x", [SEQ_PER_CORE, N, D], F32, kind="ExternalInput")
    wq_d = nc.dram_tensor("wq", [H, D, DH], F32, kind="ExternalInput")
    wk_d = nc.dram_tensor("wk", [H, D, DH], F32, kind="ExternalInput")
    wv_d = nc.dram_tensor("wv", [H, D, DH], F32, kind="ExternalInput")
    wo_d = nc.dram_tensor("wo", [H, DH, D], F32, kind="ExternalInput")
    wgates_d = nc.dram_tensor("wgates", [D, H], F32, kind="ExternalInput")
    rmsw_d = nc.dram_tensor("rmsw", [1, D], F32, kind="ExternalInput")
    wgate_d = nc.dram_tensor("wgate", [INNER, D], F32, kind="ExternalInput")
    wup_d = nc.dram_tensor("wup", [INNER, D], F32, kind="ExternalInput")
    wdown_d = nc.dram_tensor("wdown", [D, INNER], F32, kind="ExternalInput")
    cident_d = nc.dram_tensor("c_ident", [128, 128], F32, kind="ExternalInput")
    cones_d = nc.dram_tensor("c_ones", [128, 128], F32, kind="ExternalInput")
    cmask_d = nc.dram_tensor("c_masks", [4 * 128, 512], F32, kind="ExternalInput")
    cselr_d = nc.dram_tensor("c_selr", [128, 4 * 128], F32, kind="ExternalInput")
    cselg_d = nc.dram_tensor("c_selg", [128, 4 * 128], F32, kind="ExternalInput")

    out_d = nc.dram_tensor("out", [SEQ_PER_CORE, N, D], F32, kind="ExternalOutput")
    y1_d = nc.dram_tensor("y1_scr", [N, D], F32, kind="Internal")
    wgT_d = nc.dram_tensor("wgT_scr", [D, INNER], BF16, kind="Internal")
    scal_d = nc.dram_tensor("scal_scr", [1, 8], F32, kind="Internal")
    col_d = nc.dram_tensor("col_scr", [128, 8], F32, kind="Internal")
    wuT_d = nc.dram_tensor("wuT_scr", [D, INNER], BF16, kind="Internal")
    wdT_d = nc.dram_tensor("wdT_scr", [INNER, D], BF16, kind="Internal")

    tap_d = {}

    def mktap(name, shape, dtype=F32):
        tap_d[name] = nc.dram_tensor("tap_" + name, list(shape), dtype, kind="ExternalOutput")
        return tap_d[name]

    U8 = mybir.dt.uint8
    ARENA_B = 211200
    arena = nc.alloc_sbuf_tensor("arena", [128, ARENA_B], U8)
    _dtsz = {F32: 4, BF16: 2, I32: 4, U8: 1}
    _cur = {"static": 0, "phase": None}

    def AR(name, shape, dtype, phase="static"):
        nb = shape[1] * _dtsz[dtype]
        nb = (nb + 63) & ~63
        if phase == "static":
            off = _cur["static"]
            _cur["static"] += nb
            assert _cur["static"] <= ARENA_B, f"static overflow at {name}"
        else:
            off = _cur[phase]
            _cur[phase] += nb
            assert _cur[phase] <= ARENA_B, f"{phase} overflow at {name} ({_cur[phase]})"
        return arena[0:shape[0], off:off + shape[1] * _dtsz[dtype]].bitcast(dtype)

    ident_f = AR("ident_f", [128, 128], F32)
    ident_b = AR("ident_b", [128, 128], BF16)
    ones_f = AR("ones_f", [128, 128], F32)
    ones_b = AR("ones_b", [128, 128], BF16)
    masks_b = AR("masks_b", [128, 4 * 512], BF16)
    selr_f = AR("selr_f", [128, 4 * 128], F32)
    selg_f = AR("selg_f", [128, 4 * 128], BF16)
    rmsw_rep = AR("rmsw_rep", [128, D], F32)
    attn_scr = AR("attn_scr", [128, 192], F32)
    zrow_b = AR("zrow_b", [1, 512], BF16)
    x_tile = AR("x_tile", [128, 2 * D], F32)
    sq_scr = AR("sq_scr", [128, D], F32)
    dump_scr = AR("dump_scr", [128, D], F32)
    stat = AR("stat", [128, 64], F32)
    meanw_sb = AR("meanw_sb", [128, 32], F32)
    base = _cur["static"]
    _cur["attn"] = base
    _cur["ffn"] = base

    wq_l = AR("wq_l", [128, DC * 512], BF16, "attn")
    wk_l = AR("wk_l", [128, DC * 512], BF16, "attn")
    wv_l = AR("wv_l", [128, DC * 512], BF16, "attn")
    wgates_l = AR("wgates_l", [128, DC * 8], BF16, "attn")
    wo_l = AR("wo_l", [128, 4 * D], BF16, "attn")
    tT = AR("tT", [128, DC * N], BF16, "attn")
    qT = AR("qT", [128, 4 * N], BF16, "attn")
    kT = AR("kT", [128, 4 * N], BF16, "attn")
    v_sb = AR("v_sb", [128, NT * 512], BF16, "attn")
    gT_sb = AR("gT_sb", [128, N], BF16, "attn")
    t_tile = AR("t_tile", [128, 2 * D], BF16, "attn")
    nm_wide = AR("nm_wide", [128, 4 * 256], F32, "attn")
    corrT_sb = AR("corrT_sb", [128, 2 * 512], F32, "attn")
    pT_sb = AR("pT_sb", [128, 4 * 512], BF16, "attn")
    rl_sb = AR("rl_sb", [128, 2 * 512], F32, "attn")
    tg_sb = AR("tg_sb", [128, 2 * 512], F32, "attn")
    outT_sb = AR("outT_sb", [128, 4 * 512], BF16, "attn")
    y1_stage = AR("y1_stage", [128, 2 * D], F32, "attn")

    wgT_sb = AR("wgT_sb", [128, DC * INNER], BF16, "ffn")
    wuT_sb = AR("wuT_sb", [128, DC * INNER], BF16, "ffn")
    y1t = AR("y1t", [128, 2 * D], F32, "ffn")
    xqi = AR("xqi", [128, 512], I32, "ffn")
    xqb = AR("xqb", [128, D], BF16, "ffn")
    xqT = AR("xqT", [128, DC * 128], BF16, "ffn")
    sg_sb = AR("sg_sb", [128, 2 * 512], F32, "ffn")
    h6_sb = AR("h6_sb", [128, 512], F32, "ffn")
    h7_sb = AR("h7_sb", [128, 512], F32, "ffn")
    xq2b = AR("xq2b", [128, 512], BF16, "ffn")
    xq2T = AR("xq2T", [128, 2 * 32 * 128], BF16, "ffn")
    wdb = AR("wdb", [128, 2 * D], BF16, "ffn")
    o_stage = AR("o_stage", [128, D], F32, "ffn")
    assert max(_cur["attn"], _cur["ffn"]) <= ARENA_B

    P = nc.alloc_psum_tensor
    psA = P("psA", [128, 2048], F32)
    psB = P("psB", [128, 2048], F32)
    banks = [Bank(psA[:, i * 512:(i + 1) * 512]) for i in range(4)] + \
            [Bank(psB[:, i * 512:(i + 1) * 512]) for i in range(4)]

    ld_ident = sch.q(lambda e: e.dma_start(ident_f[:], cident_d.ap()), tag="ident")
    ld_identb = sch.g(lambda e: e.dma_start(ident_b[:], cident_d.ap()), tag="identb")
    ld_ones = sch.q(lambda e: e.dma_start(ones_f[:], cones_d.ap()), tag="ones")
    ld_masks = sch.g(lambda e: e.dma_start(
        masks_b[:].rearrange("p (r f) -> p r f", r=4),
        cmask_d.ap().rearrange("(r p) f -> p r f", r=4)), tag="masks")
    ld_selr = sch.q(lambda e: e.dma_start(selr_f[:], cselr_d.ap()), tag="selr")
    ld_selg = sch.g(lambda e: e.dma_start(selg_f[:], cselg_d.ap()), tag="selg")
    ld_rmsw = sch.g(lambda e: e.dma_start(
        rmsw_rep[:], rmsw_d.ap().broadcast_to([128, D])), tag="rmsw")
    ld_onesb = sch.g(lambda e: e.dma_start(ones_b[:], cones_d.ap()), tag="onesb")
    ld_zrow = sch.g(lambda e: e.memset(zrow_b[:], 0.0), dma=False, tag="zrow")

    def load_attn_weights(deps, s):
        ops = []
        for dst, src_d, pariter, nm2 in ((wq_l, wq_d, True, "wq"), (wk_l, wk_d, True, "wk"),
                                         (wv_l, wv_d, False, "wv")):
            for h in range(H):
                coloff = ((h // 2) * 128 + (h % 2) * 64) if pariter else h * 64
                dst_ap = dst.rearrange("p (dc c) -> p dc c", dc=DC)[:, :, coloff:coloff + 64]
                src_ap = src_d[h].rearrange("(dc p) e -> p dc e", p=128)
                ops.append(sch.g(lambda e, d=dst_ap, s2=src_ap: e.dma_start(d, s2),
                                 waits=deps if (h == 0 and nm2 == "wq") else [],
                                 tag=f"attw{s}_{nm2}{h}"))
        ops.append(sch.g(lambda e: e.dma_start(
            wgates_l.rearrange("p (dc h) -> p dc h", dc=DC),
            wgates_d.ap().rearrange("(dc p) h -> p dc h", p=128)),
            waits=deps, tag=f"wgates{s}"))
        ops.append(sch.g(lambda e: e.dma_start(
            wo_l.rearrange("p (pc d) -> p pc d", pc=4),
            wo_d.ap().rearrange("(pc hh) e d -> (hh e) pc d", hh=2)),
            waits=deps, tag=f"wo{s}"))
        return ops

    def rsqrt_col(dep, src_col, dst_col, eps, tag, scratch_col=9):
        eps_ap = _cap(eps)
        o1 = sch.s(lambda e, eps_ap=eps_ap: e.activation(
            stat[:, scratch_col:scratch_col + 1], stat[:, src_col:src_col + 1], AF.Ln,
            bias=eps_ap), waits=[dep], tag=tag + "_ln")
        o2 = sch.s(lambda e: e.activation(
            stat[:, dst_col:dst_col + 1], stat[:, scratch_col:scratch_col + 1],
            AF.Exp, scale=-0.5), waits=[o1], tag=tag + "_ex")
        return o2

    _const_cols = {}

    def _cap(val):
        if val not in _const_cols:
            idx = 56 + len(_const_cols)
            assert idx < 64
            _const_cols[val] = idx
            sch.g(lambda e, idx=idx, val=val: e.memset(stat[:, idx:idx + 1], float(val)),
                  dma=False, tag=f"const{val}")
        idx = _const_cols[val]
        return stat[:, idx:idx + 1]

    wprep_scr = tT[:].bitcast(F32)[:, 0:INNER]
    wprep_i = qT[:].bitcast(I32)[:, 0:INNER]
    abs_scr = kT[:].bitcast(F32)[:, 0:INNER]
    wprep_q = v_sb[:, 0:INNER]
    wprep_Ts = outT_sb[:, 0:1024]

    def prep_ffn_weight(src_d, dst_T_d, rows, cols, name):
        RT = rows // 128
        sum_ops = []
        scr = Slot()
        for r in range(RT):
            ld = sch.q(lambda e, r=r: e.dma_start(wprep_scr[:, 0:cols],
                                                  src_d[r * 128:(r + 1) * 128, :]),
                       waits=scr.pwaits(), tag=f"{name}ld{r}")
            scr.produced(ld)
            so = sch.s(lambda e, r=r: e.activation(
                abs_scr[:, 0:cols], wprep_scr[:, 0:cols], AF.Abs,
                accum_out=stat[:, r:r + 1]), waits=[ld], tag=f"{name}abs{r}")
            scr.consumed(so)
            sum_ops.append(so)
        st_free = sch.s(lambda e, RT=RT: e.activation(
            abs_scr[:, 0:RT], stat[:, 0:RT], AF.Copy, accum_out=stat[:, 32:33]),
            waits=[sum_ops[-1]], tag=f"{name}stf")
        if name == "Pg" and "pstat0" in taps:
            td = mktap("pstat0", [128, 64])
            sch.q(lambda e, td=td: e.dma_start(td.ap(), stat[:]),
                  waits=[st_free], tag="tap_pstat0")
        g0 = sch.q(lambda e: e.dma_start(col_d.ap(), stat[:, 32:40]),
                   waits=[st_free], tag=f"{name}pg0")
        g1 = sch.q(lambda e: e.dma_start(sq_scr[0:1, 0:1024],
                                         col_d.ap().rearrange("p o -> (p o)").unsqueeze(0)),
                   waits=[g0], tag=f"{name}pg")
        rr = sch.v(lambda e: e.reduce_sum(
            stat[0:1, 33:34],
            sq_scr[0:1, 0:1024].rearrange("o (p c) -> o p c", c=8)[:, :, 0:1],
            axis=AX.XY), waits=[g1], tag=f"{name}rr")
        midx = {"Pg": 0, "Pu": 1, "Pd": 2}[name]
        s1 = sch.q(lambda e, midx=midx: e.dma_start(
            scal_d[0:1, midx:midx + 1], stat[0:1, 33:34]),
            waits=[rr], tag=f"{name}s1")
        bcast = sch.q(lambda e, midx=midx: e.dma_start(
            stat[:, 33:34], scal_d[0:1, midx:midx + 1].broadcast_to([128, 1])),
            waits=[s1], tag=f"{name}bc")
        c1 = sch.v(lambda e: e.tensor_scalar(
            stat[:, 41:49], stat[:, 33:34].broadcast_to([128, 8]),
            1.0 / (rows * cols), QEPS,
            op0=ALU.mult, op1=ALU.max), waits=[bcast], tag=f"{name}clip")
        c2a = sch.s(lambda e: e.activation(stat[:, 49:50], stat[:, 41:42], AF.Ln),
                    waits=[c1], tag=f"{name}rln")
        c2 = sch.s(lambda e: e.activation(stat[:, 50:51], stat[:, 49:50], AF.Exp,
                                          scale=-1.0), waits=[c2a], tag=f"{name}rex")
        if name == "Pg" and "pstat1" in taps:
            td = mktap("pstat1", [128, 64])
            sch.q(lambda e, td=td: e.dma_start(td.ap(), stat[:]), waits=[c2],
                  tag="tap_pstat1")

        last_stores = []
        tb = banks[3]
        qslot = Slot()
        for r in range(RT):
            ld = sch.q(lambda e, r=r: e.dma_start(wprep_scr[:, 0:cols],
                                                  src_d[r * 128:(r + 1) * 128, :]),
                       waits=[c2] + scr.pwaits(), tag=f"{name}ld2_{r}")
            scr.produced(ld)
            qi = sch.v(lambda e: e.tensor_scalar_mul(
                wprep_i[:, 0:cols], wprep_scr[:, 0:cols], stat[:, 50:51]),
                waits=[ld], tag=f"{name}rnd{r}")
            scr.consumed(qi)
            qc = sch.v(lambda e: e.tensor_scalar(
                wprep_q[:, 0:cols], wprep_i[:, 0:cols], -1.0, 1.0,
                op0=ALU.max, op1=ALU.min), waits=[qi] + qslot.pwaits(), tag=f"{name}clp{r}")
            qslot.produced(qc)
            CT = cols // 128
            for ct0 in range(0, CT, 8):
                nct = min(8, CT - ct0)
                tms = []
                for k in range(nct):
                    ct = ct0 + k
                    tm = sch.t(lambda e, ct=ct, k=k: e.transpose(
                        tb.ap[:].bitcast(BF16)[:, k * 128:(k + 1) * 128],
                        wprep_q[:, ct * 128:(ct + 1) * 128], ident_b[:]),
                        waits=[qc, ld_identb] + (tb.wwaits() if k == 0 else []),
                        tag=f"{name}T{r}_{ct}")
                    tms.append(tm)
                tb.set_writer(tms[-1])
                qslot.consumed(tms[-1])
                ev = sch.v(lambda e, nct=nct: e.tensor_copy(
                    wprep_Ts[:, 0:nct * 128], tb.ap[:].bitcast(BF16)[:, 0:nct * 128]),
                    waits=tb.rwaits(), tag=f"{name}Tev{r}_{ct0}")
                tb.add_reader(ev)
                stv = sch.q(lambda e, r=r, ct0=ct0, nct=nct: e.dma_start(
                    dst_T_d[ct0 * 128:ct0 * 128 + nct * 128, r * 128:(r + 1) * 128]
                        .rearrange("(n p) f -> p n f", p=128),
                    wprep_Ts[:, 0:nct * 128].rearrange("p (n f) -> p n f", f=128)),
                    waits=[ev], tag=f"{name}st{r}_{ct0}")
                last_stores.append(stv)
        return last_stores

    if stop_after not in ("consts",):
        wprep_done = []
        wprep_done += prep_ffn_weight(wgate_d, wgT_d, INNER, D, "Pg")
        if "pstat" in taps:
            td = mktap("pstat", [128, 64])
            sch.q(lambda e, td=td: e.dma_start(td.ap(), stat[:]), waits=wprep_done,
                  tag="tap_pstat")
        if stop_after != "prep1":
            wprep_done += prep_ffn_weight(wup_d, wuT_d, INNER, D, "Pu")
            if stop_after != "prep2":
                wprep_done += prep_ffn_weight(wdown_d, wdT_d, D, INNER, "Pd")
        if "wgT" in taps:
            td = mktap("wgT", [D, INNER], BF16)
            sch.q(lambda e, td=td: e.dma_start(td.ap(), wgT_d.ap()), waits=wprep_done, tag="tapwgT")
        if "wdT" in taps:
            td = mktap("wdT", [INNER, D], BF16)
            sch.q(lambda e, td=td: e.dma_start(td.ap(), wdT_d.ap()), waits=wprep_done, tag="tapwdT")

    prev_seq_done = []

    def phase_attn(s):
        nonlocal prev_seq_done
        ld_attw = load_attn_weights(list(prev_seq_done), s)
        zrl = sch.g(lambda e: e.memset(rl_sb[:], 0.0), dma=False,
                    waits=list(prev_seq_done), tag=f"zrl{s}")
        deps = list(prev_seq_done) + ld_attw + [ld_identb, ld_rmsw, zrl]

        xslot = [Slot(), Slot()]
        tslot = [Slot(), Slot()]
        a1_done = []
        tb = banks[0]
        for ti in range(NT):
            w = ti % 2
            xa = x_tile[:, w * D:(w + 1) * D]
            ta = t_tile[:, w * D:(w + 1) * D]
            ld = sch.q(lambda e, s=s, ti=ti, xa=xa: e.dma_start(
                xa, x_d[s, ti * 128:(ti + 1) * 128, :]),
                waits=deps + xslot[w].pwaits(), tag=f"x{s}_{ti}")
            xslot[w].produced(ld)
            sq = sch.s(lambda e, xa=xa: e.activation(
                sq_scr[:], xa, AF.Square, accum_out=stat[:, 0:1]),
                waits=[ld], tag=f"sq{s}_{ti}")
            st1 = sch.v(lambda e: e.tensor_scalar(
                stat[:, 1:9], stat[:, 0:1].broadcast_to([128, 8]), 1.0 / D, None,
                op0=ALU.mult), waits=[sq], tag=f"rmsm{s}_{ti}")
            rs = rsqrt_col(st1, 1, 10, RMS_EPS, f"rms{s}_{ti}")
            tt = sch.v(lambda e, xa=xa, ta=ta: e.scalar_tensor_tensor(
                ta, xa, stat[:, 10:11], rmsw_rep[:], op0=ALU.mult, op1=ALU.mult),
                waits=[rs] + tslot[w].pwaits(), tag=f"t{s}_{ti}")
            xslot[w].consumed(tt)
            tslot[w].produced(tt)
            tms = []
            for dc in range(DC):
                tm = sch.t(lambda e, dc=dc, ta=ta: e.transpose(
                    tb.ap[:].bitcast(BF16)[:, dc * 128:(dc + 1) * 128],
                    ta[:, dc * 128:(dc + 1) * 128], ident_b[:]),
                    waits=[tt] + (tb.wwaits() if dc == 0 else []), tag=f"tT{s}_{ti}_{dc}")
                tms.append(tm)
            tb.set_writer(tms[-1])
            ev = sch.v(lambda e, ti=ti: e.tensor_copy(
                tT[:].rearrange("p (dc n) -> p dc n", dc=DC)
                    [:, :, ti * 128:(ti + 1) * 128],
                tb.ap[:].bitcast(BF16)[:, 0:1024].rearrange("p (dc n) -> p dc n", dc=DC)),
                waits=tb.rwaits(), tag=f"tTev{s}_{ti}")
            tb.add_reader(ev)
            tslot[w].consumed(ev)
            a1_done.append(ev)

        if "tT" in taps and s == 0:
            td = mktap("tT", [128, DC * N], BF16)
            a1_done.append(sch.q(lambda e, td=td: e.dma_start(td.ap(), tT[:]),
                                 waits=list(a1_done), tag="tap_tT"))

        a2_done = []
        for dst, wl, nm in ((qT, wq_l, "q"), (kT, wk_l, "k")):
            for c in range(4):
                for nb in range(4):
                    b = banks[2 + (nb + c * 4) % 2]
                    mms = []
                    for dc in range(DC):
                        mm = sch.t(lambda e, c=c, nb=nb, dc=dc, wl=wl, b=b: e.matmul(
                            b.ap[:, 0:512],
                            wl[:, dc * 512 + c * 128: dc * 512 + (c + 1) * 128],
                            tT[:, dc * N + nb * 512: dc * N + (nb + 1) * 512],
                            start=(dc == 0), stop=(dc == DC - 1)),
                            waits=(a1_done + b.wwaits()) if dc == 0 else [],
                            tag=f"{nm}mm{s}_{c}_{nb}_{dc}")
                        mms.append(mm)
                    b.set_writer(mms[-1])
                    ev = sch.v(lambda e, c=c, nb=nb, dst=dst, b=b: e.tensor_copy(
                        dst[:, c * N + nb * 512: c * N + (nb + 1) * 512], b.ap[:, 0:512]),
                        waits=b.rwaits(), tag=f"{nm}ev{s}_{c}_{nb}")
                    b.add_reader(ev)
                    a2_done.append(ev)
        v_done = []
        for ti in range(NT):
            b = banks[2 + ti % 2]
            mms = []
            for dc in range(DC):
                mm = sch.t(lambda e, ti=ti, dc=dc, b=b: e.matmul(
                    b.ap[:, 0:512],
                    tT[:, dc * N + ti * 128: dc * N + (ti + 1) * 128],
                    wv_l[:, dc * 512:(dc + 1) * 512],
                    start=(dc == 0), stop=(dc == DC - 1)),
                    waits=(a1_done + b.wwaits()) if dc == 0 else [],
                    tag=f"vmm{s}_{ti}_{dc}")
                mms.append(mm)
            b.set_writer(mms[-1])
            ev = sch.v(lambda e, ti=ti, b=b: e.tensor_copy(
                v_sb[:, ti * 512:(ti + 1) * 512], b.ap[:, 0:512]),
                waits=b.rwaits(), tag=f"vev{s}_{ti}")
            b.add_reader(ev)
            v_done.append(ev)
            a2_done.append(ev)
        g_done = []
        for nb in range(4):
            b = banks[4 + nb % 2]
            mms = []
            for dc in range(DC):
                mm = sch.t(lambda e, nb=nb, dc=dc, b=b: e.matmul(
                    b.ap[0:8, 0:512],
                    wgates_l[:, dc * 8:(dc + 1) * 8],
                    tT[:, dc * N + nb * 512: dc * N + (nb + 1) * 512],
                    start=(dc == 0), stop=(dc == DC - 1)),
                    waits=(a1_done + b.wwaits()) if dc == 0 else [],
                    tag=f"gmm{s}_{nb}_{dc}")
                mms.append(mm)
            b.set_writer(mms[-1])
            sg = sch.s(lambda e, nb=nb, b=b: e.activation(
                gT_sb[0:8, nb * 512:(nb + 1) * 512], b.ap[0:8, 0:512], AF.Sigmoid),
                waits=b.rwaits(), tag=f"gsig{s}_{nb}")
            b.add_reader(sg)
            g_done.append(sg)
            a2_done.append(sg)

        if s == 0:
            for nm, buf, shape in (("qT", qT, [128, 4 * N]), ("kT", kT, [128, 4 * N]),
                                   ("v", v_sb, [128, NT * 512])):
                if nm in taps:
                    td = mktap(nm, shape, BF16)
                    a2_done.append(sch.q(lambda e, td=td, buf=buf: e.dma_start(td.ap(), buf[:]),
                                         waits=list(a2_done), tag=f"tap_{nm}"))
            if "gT" in taps:
                td = mktap("gT", [128, N], BF16)
                a2_done.append(sch.q(lambda e, td=td: e.dma_start(td.ap(), gT_sb[:]),
                                     waits=list(g_done), tag="tap_gT"))

        pT = [Slot() for _ in range(4)]
        y1slot = [Slot(), Slot()]
        outT_slot = Slot()
        rsq = Slot()
        xr = [Slot(), Slot()]
        a4_done = []
        sbk = [banks[0], banks[1]]
        lbk = [banks[2], banks[3]]
        pvb = [banks[4], banks[5], banks[6], banks[7]]
        HROW = [32 * (h % 4) for h in range(8)]

        for w in range(NW):
            nm_ready = {}
            jw = (w + 1) * 512
            for k in range(4):
                it = 4 * w + k
                for p in range(4):
                    hA, hB = 2 * p, 2 * p + 1
                    mmsA, mmsB = [], []
                    for jc5 in range(w + 1):
                        bA, bB = banks[jc5], banks[4 + jc5]
                        mmA = sch.t(lambda e, p=p, it=it, jc5=jc5, bA=bA: e.matmul(
                            bA.ap[:, 0:512],
                            qT[0:64, p * N + it * 128: p * N + (it + 1) * 128],
                            kT[0:64, p * N + jc5 * 512: p * N + (jc5 + 1) * 512],
                            start=True, stop=True, tile_position=(0, 0)),
                            waits=(a2_done if (w == 0 and k == 0 and p == 0 and jc5 == 0)
                                   else []) + bA.wwaits(),
                            tag=f"S{s}w{w}k{k}p{p}c{jc5}A")
                        bA.set_writer(mmA)
                        mmsA.append((bA, mmA))
                        mmB = sch.t(lambda e, p=p, it=it, jc5=jc5, bB=bB: e.matmul(
                            bB.ap[:, 0:512],
                            qT[64:128, p * N + it * 128: p * N + (it + 1) * 128],
                            kT[64:128, p * N + jc5 * 512: p * N + (jc5 + 1) * 512],
                            start=True, stop=True, tile_position=(64, 0)),
                            waits=bB.wwaits(), tag=f"S{s}w{w}k{k}p{p}c{jc5}B")
                        bB.set_writer(mmB)
                        mmsB.append((bB, mmB))
                    for h, reg, mml_ in ((hA, psA, mmsA), (hB, psB, mmsB)):
                        slot = k * 256 + (h // 4) * 128 + HROW[h]
                        nm = sch.v(lambda e, reg=reg, slot=slot, jw=jw: e.tensor_scalar(
                            pT_sb[:, 0:jw], reg[:, 0:jw],
                            -1.0, None, op0=ALU.mult, op1=ALU.min,
                            accum_out=nm_wide[:, slot:slot + 1]),
                            waits=[m for _, m in mml_] + sum(
                                (pT[i].pwaits() for i in range(4)), []),
                            tag=f"nm{s}w{w}k{k}h{h}")
                        for bk, _ in mml_:
                            bk.add_reader(nm)
                        nm_ready[(k, h)] = nm
            for i in range(4):
                pT[i].produced(list(nm_ready.values())[-1])
            if s == 0 and w == 0 and "pmax0" in taps:
                td = mktap("pmax0", [128, 192])
                sch.q(lambda e, td=td: e.dma_start(td.ap(), attn_scr[:]),
                      waits=list(nm_ready.values()), tag="tap_pmax0")
            corr_evs = []
            for g in range(2):
                cb = pvb[g]
                tms = []
                for k in range(4):
                    base = k * 256 + g * 128
                    deps = [nm_ready[(k, h)] for h in range(4 * g, 4 * g + 4)]
                    tm = sch.t(lambda e, base=base, k=k, cb=cb: e.transpose(
                        cb.ap[0:97, k * 128:(k + 1) * 128],
                        nm_wide[:, base:base + 97], ident_f[:]),
                        waits=deps + [ld_ident] + (cb.wwaits() if k == 0 else []),
                        tag=f"cT{s}w{w}g{g}k{k}")
                    tms.append(tm)
                cb.set_writer(tms[-1])
                ev = sch.v(lambda e, g=g, cb=cb: e.tensor_copy(
                    corrT_sb[:, g * 512:(g + 1) * 512], cb.ap[:, 0:512]),
                    waits=cb.rwaits(), tag=f"cTe{s}w{w}g{g}")
                cb.add_reader(ev)
                corr_evs.append(ev)
            if s == 0 and w == 0 and "nm0" in taps:
                td = mktap("nm0", [128, 4 * 256])
                sch.q(lambda e, td=td: e.dma_start(td.ap(), nm_wide[:]),
                      waits=list(nm_ready.values()), tag="tap_nm0")
                td2 = mktap("corrT0", [128, 1024])
                sch.q(lambda e, td2=td2: e.dma_start(td2.ap(), corrT_sb[:]),
                      waits=corr_evs, tag="tap_corrT0")
            zb_ops = []
            for bk in (lbk[0], lbk[1], pvb[0], pvb[1], pvb[2], pvb[3]):
                z = sch.t(lambda e, bk=bk: e.matmul(
                    bk.ap[:, 0:512], ones_b[0:1, :], zrow_b[:],
                    start=True, stop=True),
                    waits=[ld_zrow, ld_onesb] + bk.wwaits(), tag=f"zb{s}w{w}")
                bk.set_writer(z)
                zb_ops.append(z)
            njc = 4 * w + 4
            for jc in range(njc):
                for p in range(4):
                    hA, hB = 2 * p, 2 * p + 1
                    stA, stB = sbk[0], sbk[1]
                    mmA = sch.t(lambda e, p=p, jc=jc, w=w, stA=stA: e.matmul(
                        stA.ap[:, 0:512],
                        kT[0:64, p * N + jc * 128: p * N + (jc + 1) * 128],
                        qT[0:64, p * N + w * 512: p * N + (w + 1) * 512],
                        start=True, stop=False, tile_position=(0, 0)),
                        waits=stA.wwaits() + corr_evs, tag=f"sT{s}w{w}j{jc}p{p}A")
                    mmB = sch.t(lambda e, p=p, jc=jc, w=w, stB=stB: e.matmul(
                        stB.ap[:, 0:512],
                        kT[64:128, p * N + jc * 128: p * N + (jc + 1) * 128],
                        qT[64:128, p * N + w * 512: p * N + (w + 1) * 512],
                        start=True, stop=False, tile_position=(64, 0)),
                        waits=stB.wwaits() + corr_evs, tag=f"sT{s}w{w}j{jc}p{p}B")
                    crA = sch.t(lambda e, hA=hA, stA=stA: e.matmul(
                        stA.ap[:, 0:512],
                        ones_f[HROW[hA]:HROW[hA] + 1, 0:128],
                        corrT_sb[HROW[hA]:HROW[hA] + 1, (hA // 4) * 512:(hA // 4) * 512 + 512],
                        start=False, stop=True, tile_position=(HROW[hA], 0)),
                        waits=[mmA, ld_ones], tag=f"cr{s}w{w}j{jc}p{p}A")
                    stA.set_writer(crA)
                    crB = sch.t(lambda e, hB=hB, stB=stB: e.matmul(
                        stB.ap[:, 0:512],
                        ones_f[HROW[hB]:HROW[hB] + 1, 0:128],
                        corrT_sb[HROW[hB]:HROW[hB] + 1, (hB // 4) * 512:(hB // 4) * 512 + 512],
                        start=False, stop=True, tile_position=(HROW[hB], 0)),
                        waits=[mmB, ld_ones], tag=f"cr{s}w{w}j{jc}p{p}B")
                    stB.set_writer(crB)
                    slA = pT[(2 * (jc * 4 + p)) % 4]
                    slB = pT[(2 * (jc * 4 + p) + 1) % 4]
                    siA, siB = pT.index(slA), pT.index(slB)
                    assert siB == siA + 1 and siA % 2 == 0
                    ex2 = sch.s(lambda e, siA=siA: e.activation(
                        pT_sb[:, siA * 512: siA * 512 + 1024],
                        psA[:, 0:1024], AF.Exp, scale=ATTN_SCALE),
                        waits=stA.rwaits() + stB.rwaits() + slA.pwaits() + slB.pwaits(),
                        tag=f"ex{s}w{w}j{jc}p{p}")
                    stA.add_reader(ex2)
                    stB.add_reader(ex2)
                    slA.produced(ex2)
                    slB.produced(ex2)
                    srcs = {hA: ex2, hB: ex2}
                    if jc >= 4 * w:
                        r_idx = jc - 4 * w
                        mkA = sch.v(lambda e, siA=siA, r_idx=r_idx: e.tensor_tensor(
                            pT_sb[:, siA * 512: siA * 512 + 512],
                            pT_sb[:, siA * 512: siA * 512 + 512],
                            masks_b[:, r_idx * 512:(r_idx + 1) * 512], op=ALU.mult),
                            waits=[ex2], tag=f"mkA{s}w{w}j{jc}p{p}")
                        mkB = sch.v(lambda e, siB=siB, r_idx=r_idx: e.tensor_tensor(
                            pT_sb[:, siB * 512: siB * 512 + 512],
                            pT_sb[:, siB * 512: siB * 512 + 512],
                            masks_b[:, r_idx * 512:(r_idx + 1) * 512], op=ALU.mult),
                            waits=[ex2, ld_masks], tag=f"mkB{s}w{w}j{jc}p{p}")
                        slA.produced(mkA)
                        slB.produced(mkB)
                        srcs = {hA: mkA, hB: mkB}
                    for h, bk, sl, si in ((hA, stA, slA, siA), (hB, stB, slB, siB)):
                        src_op = srcs[h]
                        pb_ = pvb[p]
                        mmpv = sch.t(lambda e, h=h, jc=jc, si=si, pb_=pb_: e.matmul(
                            pb_.ap[(h % 2) * 64:(h % 2) * 64 + 64, 0:512],
                            v_sb[:, jc * 512 + h * 64: jc * 512 + (h + 1) * 64],
                            pT_sb[:, si * 512: si * 512 + 512],
                            start=False, stop=False,
                            tile_position=(0, (h % 2) * 64)),
                            waits=[src_op],
                            tag=f"pv{s}w{w}j{jc}h{h}")
                        pb_.set_writer(mmpv)
                        sl.consumed(mmpv)
                        lb = lbk[h // 4]
                        mml = sch.t(lambda e, h=h, jc=jc, si=si, lb=lb: e.matmul(
                            lb.ap[HROW[h]:HROW[h] + 1, 0:512],
                            ones_b[:, 0:1],
                            pT_sb[:, si * 512: si * 512 + 512],
                            start=False, stop=False,
                            tile_position=(0, HROW[h])),
                            waits=[src_op],
                            tag=f"l{s}w{w}j{jc}h{h}")
                        lb.set_writer(mml)
                        sl.consumed(mml)
            rl_ops = []
            for g in range(2):
                lb = lbk[g]
                for hh in range(4):
                    row = 32 * hh
                    rc = sch.v(lambda e, g=g, row=row, lb=lb: e.reciprocal(
                        rl_sb[row:row + 1, g * 512:(g + 1) * 512],
                        lb.ap[row:row + 1, 0:512]),
                        waits=lb.rwaits(), tag=f"rl{s}w{w}g{g}r{row}")
                    lb.add_reader(rc)
                    rl_ops.append(rc)
            if s == 0 and w == 0 and "rl0" in taps:
                td = mktap("rl0", [128, 1024])
                sch.q(lambda e, td=td: e.dma_start(td.ap(), rl_sb[:]),
                      waits=rl_ops, tag="tap_rl0")
            outT_evs = []
            for p in range(4):
                g = p // 2
                trl_b, tg_b = sbk[0], sbk[1]
                m1 = sch.t(lambda e, p=p, g=g, trl_b=trl_b: e.matmul(
                    trl_b.ap[:, 0:512], selr_f[:, p * 128:(p + 1) * 128],
                    rl_sb[:, g * 512:(g + 1) * 512], start=True, stop=True),
                    waits=rl_ops + [ld_selr] + trl_b.wwaits(), tag=f"trl{s}w{w}p{p}")
                trl_b.set_writer(m1)
                m2 = sch.t(lambda e, p=p, w=w, tg_b=tg_b: e.matmul(
                    tg_b.ap[:, 0:512], selg_f[0:8, p * 128:(p + 1) * 128],
                    gT_sb[0:8, w * 512:(w + 1) * 512], start=True, stop=True),
                    waits=g_done + [ld_selg] + tg_b.wwaits(), tag=f"tg{s}w{w}p{p}")
                tg_b.set_writer(m2)
                c1_ = sch.v(lambda e, tg_b=tg_b: e.tensor_copy(
                    tg_sb[:, 0:512], tg_b.ap[:, 0:512]),
                    waits=tg_b.rwaits(), tag=f"tgc{s}w{w}p{p}")
                tg_b.add_reader(c1_)
                c2_ = sch.v(lambda e, trl_b=trl_b: e.tensor_tensor(
                    tg_sb[:, 512:1024], trl_b.ap[:, 0:512], tg_sb[:, 0:512],
                    op=ALU.mult), waits=trl_b.rwaits() + [c1_], tag=f"tgm{s}w{w}p{p}")
                trl_b.add_reader(c2_)
                pb_ = pvb[p]
                ev = sch.v(lambda e, p=p, pb_=pb_: e.tensor_tensor(
                    outT_sb[:, p * 512:(p + 1) * 512], pb_.ap[:, 0:512],
                    tg_sb[:, 512:1024], op=ALU.mult),
                    waits=pb_.rwaits() + [c2_] + (outT_slot.pwaits() if p == 0 else []),
                    tag=f"oev{s}w{w}p{p}")
                pb_.add_reader(ev)
                outT_evs.append(ev)
            outT_slot.produced(outT_evs[-1])
            if s == 0 and w == 0 and "outT0" in taps:
                td = mktap("outT0", [128, 2048], BF16)
                sch.q(lambda e, td=td: e.dma_start(td.ap(), outT_sb[:]),
                      waits=outT_evs, tag="tap_outT0")
            for k in range(4):
                it = 4 * w + k
                wslot = xr[k % 2]
                xa = x_tile[:, (k % 2) * D:(k % 2 + 1) * D]
                ldx = sch.q(lambda e, s=s, it=it, xa=xa: e.dma_start(
                    xa, x_d[s, it * 128:(it + 1) * 128, :]),
                    waits=wslot.pwaits(), tag=f"xr{s}_{it}")
                wslot.produced(ldx)
                pbs = [lbk[0], lbk[1]]
                for half in range(2):
                    pbh = pbs[half]
                    mms = []
                    for p in range(4):
                        mm = sch.t(lambda e, p=p, k=k, half=half, pbh=pbh: e.matmul(
                            pbh.ap[:, 0:512],
                            outT_sb[:, p * 512 + k * 128: p * 512 + (k + 1) * 128],
                            wo_l[:, p * D + half * 512: p * D + (half + 1) * 512],
                            start=(p == 0), stop=(p == 3)),
                            waits=[outT_evs[p]] + (pbh.wwaits() if p == 0 else []),
                            tag=f"pr{s}w{w}k{k}h{half}p{p}")
                        mms.append(mm)
                    pbh.set_writer(mms[-1])
                    outT_slot.consumed(mms[-1])
                radd = []
                for half in range(2):
                    pbh = pbs[half]
                    ra = sch.v(lambda e, half=half, xa=xa, pbh=pbh: e.scalar_tensor_tensor(
                        sq_scr[:, half * 512:(half + 1) * 512],
                        xa[:, half * 512:(half + 1) * 512], 1.0, pbh.ap[:, 0:512],
                        op0=ALU.mult, op1=ALU.add),
                        waits=[ldx] + pbh.rwaits() + (rsq.pwaits() if half == 0 else []),
                        tag=f"ra{s}w{w}k{k}h{half}")
                    pbh.add_reader(ra)
                    radd.append(ra)
                wslot.consumed(radd[-1])
                rsq.produced(radd[-1])
                if s == 0 and w == 0 and k == 0 and "r0" in taps:
                    td = mktap("r0", [128, D])
                    sch.q(lambda e, td=td: e.dma_start(td.ap(), sq_scr[:, 0:D]),
                          waits=radd, tag="tap_r0")
                s1_ = sch.s(lambda e: e.activation(
                    dump_scr[:, 0:D], sq_scr[:, 0:D], AF.Copy,
                    accum_out=stat[:, 11:12]), waits=radd, tag=f"lns{s}w{w}k{k}")
                s2_ = sch.s(lambda e: e.activation(
                    dump_scr[:, 0:D], sq_scr[:, 0:D], AF.Square,
                    accum_out=stat[:, 12:13]), waits=[s1_], tag=f"lnq{s}w{w}k{k}")
                m8 = sch.v(lambda e: e.tensor_scalar(
                    stat[:, 16:24], stat[:, 11:12].broadcast_to([128, 8]), 1.0 / D,
                    None, op0=ALU.mult), waits=[s2_], tag=f"lnm{s}w{w}k{k}")
                v8 = sch.v(lambda e: e.tensor_scalar(
                    stat[:, 24:32], stat[:, 12:13].broadcast_to([128, 8]), 1.0 / D,
                    None, op0=ALU.mult), waits=[s2_], tag=f"lnv{s}w{w}k{k}")
                msq = sch.v(lambda e: e.tensor_tensor(
                    stat[:, 32:40], stat[:, 16:24], stat[:, 16:24], op=ALU.mult),
                    waits=[m8], tag=f"lnw{s}w{w}k{k}")
                var8 = sch.v(lambda e: e.tensor_tensor(
                    stat[:, 40:48], stat[:, 24:32], stat[:, 32:40], op=ALU.subtract),
                    waits=[v8, msq], tag=f"lnx{s}w{w}k{k}")
                rstd = rsqrt_col(var8, 40, 14, LN_EPS, f"lnr{s}w{w}k{k}", scratch_col=15)
                nb8 = sch.v(lambda e: e.tensor_scalar(
                    stat[:, 32:40], stat[:, 16:24], stat[:, 14:15], -1.0,
                    op0=ALU.mult, op1=ALU.mult), waits=[rstd], tag=f"lnb{s}w{w}k{k}")
                if s == 0 and w == 0 and k == 0 and "st0" in taps:
                    td = mktap("st0", [128, 64])
                    sch.q(lambda e, td=td: e.dma_start(td.ap(), stat[:]),
                          waits=[nb8], tag="tap_st0")
                ysl = y1slot[k % 2]
                ya = y1_stage[:, (k % 2) * D:(k % 2 + 1) * D]
                nrm = sch.v(lambda e, ya=ya: e.tensor_scalar(
                    ya, sq_scr[:, 0:D], stat[:, 14:15], stat[:, 32:33],
                    op0=ALU.mult, op1=ALU.add),
                    waits=[nb8] + ysl.pwaits(), tag=f"lnn{s}w{w}k{k}")
                rsq.consumed(nrm)
                ysl.produced(nrm)
                sty = sch.q(lambda e, it=it, ya=ya: e.dma_start(
                    y1_d[it * 128:(it + 1) * 128, :], ya),
                    waits=[nrm], tag=f"y1{s}_{it}")
                ysl.consumed(sty)
                a4_done.append(sty)
        if "y1" in taps and s == 0:
            td = mktap("y1", [N, D])
            a4_done.append(sch.q(lambda e, td=td: e.dma_start(td.ap(), y1_d.ap()),
                                 waits=list(a4_done), tag="tap_y1"))
        prev_seq_done = a2_done + a4_done
        return prev_seq_done

    def phase_ffn(s):
        nonlocal prev_seq_done
        deps = list(prev_seq_done)
        ffn_done = []
        ldmw = sch.q(lambda e: e.dma_start(
            meanw_sb[:, 0:8], scal_d[0:1, 0:8].broadcast_to([128, 8])),
            waits=deps, tag=f"Fmw{s}")
        cm_ops = []
        for j, cnt in ((0, INNER * D), (1, INNER * D), (2, D * INNER)):
            cm_ops.append(sch.v(lambda e, j=j, cnt=cnt: e.tensor_scalar(
                meanw_sb[:, 8 + 8 * j:16 + 8 * j],
                meanw_sb[:, j:j + 1].broadcast_to([128, 8]),
                1.0 / cnt, QEPS, op0=ALU.mult, op1=ALU.max),
                waits=[ldmw], tag=f"Fcmw{s}_{j}"))
        ldw_ops = []
        for mat_d, dst in ((wgT_d, wgT_sb), (wuT_d, wuT_sb)):
            for dc in range(DC):
                ldw_ops.append(sch.q(lambda e, mat_d=mat_d, dst=dst, dc=dc: e.dma_start(
                    dst[:, dc * INNER:(dc + 1) * INNER],
                    mat_d[dc * 128:(dc + 1) * 128, :]),
                    waits=deps if dc == 0 else [], tag=f"F{s}w{dc}"))
        y1s = [Slot(), Slot()]
        xq2Ts = Slot()
        oslot = [Slot(), Slot()]
        wdslot = [Slot(), Slot()]
        hbanks = banks[0:7]
        sbank = banks[7]
        tpb = banks[4]

        def emit_ln2(r_waits, out_ap, tag):
            s1_ = sch.s(lambda e: e.activation(
                dump_scr[:, 0:D], sq_scr[:, 0:D], AF.Copy,
                accum_out=stat[:, 11:12]), waits=r_waits, tag=tag + "s1")
            s2_ = sch.s(lambda e: e.activation(
                dump_scr[:, 0:D], sq_scr[:, 0:D], AF.Square,
                accum_out=stat[:, 12:13]), waits=[s1_], tag=tag + "s2")
            m8 = sch.v(lambda e: e.tensor_scalar(
                stat[:, 16:24], stat[:, 11:12].broadcast_to([128, 8]), 1.0 / D,
                None, op0=ALU.mult), waits=[s2_], tag=tag + "m8")
            v8 = sch.v(lambda e: e.tensor_scalar(
                stat[:, 24:32], stat[:, 12:13].broadcast_to([128, 8]), 1.0 / D,
                None, op0=ALU.mult), waits=[s2_], tag=tag + "v8")
            msq = sch.v(lambda e: e.tensor_tensor(
                stat[:, 32:40], stat[:, 16:24], stat[:, 16:24], op=ALU.mult),
                waits=[m8], tag=tag + "msq")
            var8 = sch.v(lambda e: e.tensor_tensor(
                stat[:, 40:48], stat[:, 24:32], stat[:, 32:40], op=ALU.subtract),
                waits=[v8, msq], tag=tag + "var")
            rstd = rsqrt_col(var8, 40, 14, LN_EPS, tag + "rs", scratch_col=15)
            nb8 = sch.v(lambda e: e.tensor_scalar(
                stat[:, 32:40], stat[:, 16:24], stat[:, 14:15], -1.0,
                op0=ALU.mult, op1=ALU.mult), waits=[rstd], tag=tag + "nb")
            return sch.v(lambda e, out_ap=out_ap: e.tensor_scalar(
                out_ap, sq_scr[:, 0:D], stat[:, 14:15], stat[:, 32:33],
                op0=ALU.mult, op1=ALU.add), waits=[nb8], tag=tag + "nrm")

        for grp in range(NT // 2):
            tile_quant_done = []
            for ttn in range(2):
                ti = grp * 2 + ttn
                ya = y1t[:, ttn * D:(ttn + 1) * D]
                ldy = sch.q(lambda e, ti=ti, ya=ya: e.dma_start(
                    ya, y1_d[ti * 128:(ti + 1) * 128, :]),
                    waits=deps + y1s[ttn].pwaits(), tag=f"Fy{s}_{ti}")
                y1s[ttn].produced(ldy)
                q1a = sch.v(lambda e, ya=ya: e.tensor_scalar(
                    dump_scr[:, 0:D], ya, 1.0, None, op0=ALU.mult, op1=ALU.max,
                    accum_out=stat[:, 20:21]), waits=[ldy], tag=f"Fq1a{s}_{ti}")
                q1b = sch.v(lambda e, ya=ya: e.tensor_scalar(
                    dump_scr[:, 0:D], ya, -1.0, None, op0=ALU.mult, op1=ALU.max,
                    accum_out=stat[:, 21:22]), waits=[ldy], tag=f"Fq1b{s}_{ti}")
                y1s[ttn].consumed(q1a)
                y1s[ttn].consumed(q1b)
                rm2 = sch.v(lambda e: e.tensor_scalar(
                    stat[:, 0:8], stat[:, 20:21].broadcast_to([128, 8]),
                    stat[:, 21:22], QEPS, op0=ALU.max, op1=ALU.max),
                    waits=[q1a, q1b], tag=f"Frm{s}_{ti}")
                l1 = sch.s(lambda e: e.activation(stat[:, 9:10], stat[:, 0:1], AF.Ln),
                           waits=[rm2], tag=f"Fl1{s}_{ti}")
                l2 = sch.s(lambda e: e.activation(stat[:, 10:11], stat[:, 9:10],
                                                  AF.Exp, scale=-1.0),
                           waits=[l1], tag=f"Fl2{s}_{ti}")
                sx = sch.s(lambda e: e.mul(stat[:, 9:10], stat[:, 10:11], 127.0),
                           waits=[l2], tag=f"Fsx{s}_{ti}")
                cg = sch.v(lambda e: e.tensor_scalar(
                    stat[:, 24:32], stat[:, 0:1].broadcast_to([128, 8]),
                    meanw_sb[:, 8:9], 1.0 / 127.0, op0=ALU.mult, op1=ALU.mult),
                    waits=[rm2] + cm_ops, tag=f"Fcg{s}_{ti}")
                cu = sch.v(lambda e: e.tensor_scalar(
                    stat[:, 32:40], stat[:, 0:1].broadcast_to([128, 8]),
                    meanw_sb[:, 16:17], 1.0 / 127.0, op0=ALU.mult, op1=ALU.mult),
                    waits=[rm2] + cm_ops, tag=f"Fcu{s}_{ti}")
                xq_ops = []
                for half in range(2):
                    qi = sch.v(lambda e, ya=ya, half=half: e.tensor_scalar(
                        xqi[:, 0:512], ya[:, half * 512:(half + 1) * 512],
                        stat[:, 9:10], None, op0=ALU.mult),
                        waits=[sx] + xq_ops[-1:], tag=f"Fxi{s}_{ti}_{half}")
                    y1s[ttn].consumed(qi)
                    qb = sch.v(lambda e, half=half: e.tensor_copy(
                        xqb[:, half * 512:(half + 1) * 512], xqi[:, 0:512]),
                        waits=[qi], tag=f"Fxb{s}_{ti}_{half}")
                    xq_ops.append(qb)
                tdst = tpb.ap.bitcast(BF16)
                tms = []
                for dc in range(DC):
                    tm = sch.t(lambda e, dc=dc, tdst=tdst: e.transpose(
                        tdst[:, dc * 128:(dc + 1) * 128],
                        xqb[:, dc * 128:(dc + 1) * 128], ident_b[:]),
                        waits=xq_ops + (tpb.wwaits() if dc == 0 else []),
                        tag=f"FxT{s}_{ti}_{dc}")
                    tms.append(tm)
                tpb.set_writer(tms[-1])
                xTe = sch.v(lambda e, tdst=tdst: e.tensor_copy(xqT[:, 0:1024],
                                                               tdst[:, 0:1024]),
                            waits=tpb.rwaits(), tag=f"FxTe{s}_{ti}")
                tpb.add_reader(xTe)
                hready = []
                trb = [banks[6], banks[7]]

                def emit_g(ob):
                    tb_ = trb[ob % 2]
                    gms = []
                    for dc in range(DC):
                        gm = sch.t(lambda e, dc=dc, ob=ob, tb_=tb_: e.matmul(
                            tb_.ap[:, 0:512],
                            xqT[:, dc * 128:(dc + 1) * 128],
                            wgT_sb[:, dc * INNER + ob * 512: dc * INNER + (ob + 1) * 512],
                            start=(dc == 0), stop=(dc == DC - 1)),
                            waits=([xTe] + ldw_ops + tb_.wwaits()) if dc == 0 else [],
                            tag=f"Fg{s}_{ti}_{ob}_{dc}")
                        gms.append(gm)
                    tb_.set_writer(gms[-1])
                    sil = sch.s(lambda e, ob=ob, tb_=tb_: e.activation(
                        sg_sb[:, (ob % 2) * 512:(ob % 2) * 512 + 512],
                        tb_.ap[:, 0:512], AF.Silu,
                        scale=stat[:, 24:25]), waits=tb_.rwaits() + [cg],
                        tag=f"Fsl{s}_{ti}_{ob}")
                    tb_.add_reader(sil)
                    return sil

                def emit_u(ob, sil):
                    tb_ = trb[ob % 2]
                    ums = []
                    for dc in range(DC):
                        um = sch.t(lambda e, dc=dc, ob=ob, tb_=tb_: e.matmul(
                            tb_.ap[:, 0:512],
                            xqT[:, dc * 128:(dc + 1) * 128],
                            wuT_sb[:, dc * INNER + ob * 512: dc * INNER + (ob + 1) * 512],
                            start=(dc == 0), stop=(dc == DC - 1)),
                            waits=tb_.wwaits() if dc == 0 else [],
                            tag=f"Fu{s}_{ti}_{ob}_{dc}")
                        ums.append(um)
                    tb_.set_writer(ums[-1])
                    if ob < 6:
                        hdst = hbanks[ob]
                        hm = sch.v(lambda e, hdst=hdst, ob=ob, tb_=tb_: e.scalar_tensor_tensor(
                            hdst.ap[:, 0:512],
                            sg_sb[:, (ob % 2) * 512:(ob % 2) * 512 + 512],
                            stat[:, 32:33], tb_.ap[:, 0:512],
                            op0=ALU.mult, op1=ALU.mult),
                            waits=[sil, cu] + tb_.rwaits() + hdst.wwaits() + hdst.rwaits(),
                            tag=f"Fh{s}_{ti}_{ob}")
                        hdst.set_writer(hm)
                        tb_.add_reader(hm)
                    else:
                        hsb = h6_sb if ob == 6 else h7_sb
                        hm = sch.v(lambda e, hsb=hsb, ob=ob, tb_=tb_: e.scalar_tensor_tensor(
                            hsb[:, 0:512],
                            sg_sb[:, (ob % 2) * 512:(ob % 2) * 512 + 512],
                            stat[:, 32:33], tb_.ap[:, 0:512],
                            op0=ALU.mult, op1=ALU.mult),
                            waits=[sil, cu] + tb_.rwaits(), tag=f"Fh{s}_{ti}_{ob}")
                        tb_.add_reader(hm)
                    return hm

                sils = {}
                for kp in range(0, 8, 2):
                    sils[kp] = emit_g(kp)
                    sils[kp + 1] = emit_g(kp + 1)
                    hready.append(emit_u(kp, sils[kp]))
                    hready.append(emit_u(kp + 1, sils[kp + 1]))
                pm_ops = []
                def hsrc_of(ob):
                    if ob < 6:
                        return hbanks[ob].ap
                    return h6_sb if ob == 6 else h7_sb
                for ob in range(8):
                    hsrc = hsrc_of(ob)
                    for sgn in range(2):
                        pm = sch.v(lambda e, hsrc=hsrc, ob=ob, sgn=sgn: e.tensor_scalar(
                            dump_scr[:, 0:512], hsrc[:, 0:512],
                            1.0 if sgn == 0 else -1.0, None,
                            op0=ALU.mult, op1=ALU.max,
                            accum_out=attn_scr[:, 8 * sgn + ob:8 * sgn + ob + 1]),
                            waits=[hready[ob]], tag=f"Fp{s}_{ti}_{ob}_{sgn}")
                        if ob < 6:
                            hbanks[ob].add_reader(pm)
                        pm_ops.append(pm)
                cmb = sch.v(lambda e: e.tensor_scalar(
                    dump_scr[:, 0:16], attn_scr[:, 0:16], 1.0, None,
                    op0=ALU.mult, op1=ALU.max, accum_out=stat[:, 20:21]),
                    waits=pm_ops, tag=f"Fcm{s}_{ti}")
                rm2b = sch.v(lambda e: e.tensor_scalar(
                    stat[:, 0:8], stat[:, 20:21].broadcast_to([128, 8]), QEPS, None,
                    op0=ALU.max), waits=[cmb], tag=f"Frm2{s}_{ti}")
                b1 = sch.s(lambda e: e.activation(stat[:, 10:11], stat[:, 0:1], AF.Ln),
                           waits=[rm2b], tag=f"Fb1{s}_{ti}")
                b2 = sch.s(lambda e, ttn=ttn: e.activation(
                    stat[:, 48 + 4 * ttn:49 + 4 * ttn], stat[:, 10:11], AF.Exp,
                    scale=-1.0), waits=[b1], tag=f"Fb2{s}_{ti}")
                s2f = sch.s(lambda e, ttn=ttn: e.mul(
                    stat[:, 48 + 4 * ttn:49 + 4 * ttn],
                    stat[:, 48 + 4 * ttn:49 + 4 * ttn], 127.0),
                    waits=[b2], tag=f"Fs2{s}_{ti}")
                cd1 = sch.s(lambda e, ttn=ttn: e.mul(
                    stat[:, 49 + 4 * ttn:50 + 4 * ttn], stat[:, 0:1],
                    1.0 / 127.0), waits=[rm2b], tag=f"Fcd1{s}_{ti}")
                cdn = sch.s(lambda e, ttn=ttn: e.mul(
                    stat[:, 49 + 4 * ttn:50 + 4 * ttn],
                    stat[:, 49 + 4 * ttn:50 + 4 * ttn], meanw_sb[:, 24:25]),
                    waits=[cd1] + cm_ops, tag=f"Fcd2{s}_{ti}")
                ffn_done.append(cdn)
                for ob in range(8):
                    hsrc = hsrc_of(ob)
                    qi2 = sch.v(lambda e, hsrc=hsrc, ttn=ttn: e.tensor_scalar(
                        xqi[:, 0:512], hsrc[:, 0:512],
                        stat[:, 48 + 4 * ttn:49 + 4 * ttn], None, op0=ALU.mult),
                        waits=[s2f], tag=f"Fq2{s}_{ti}_{ob}")
                    if ob < 6:
                        hbanks[ob].add_reader(qi2)
                    qb2 = sch.v(lambda e: e.tensor_copy(xq2b[:, 0:512], xqi[:, 0:512]),
                                waits=[qi2], tag=f"Fqb2{s}_{ti}_{ob}")
                    tb2 = hbanks[ob] if ob < 6 else trb[ob % 2]
                    tdst2 = tb2.ap.bitcast(BF16)
                    tms2 = []
                    for c4 in range(4):
                        tm2 = sch.t(lambda e, c4=c4, tdst2=tdst2: e.transpose(
                            tdst2[:, c4 * 128:(c4 + 1) * 128],
                            xq2b[:, c4 * 128:(c4 + 1) * 128], ident_b[:]),
                            waits=[qb2] + (tb2.wwaits() if c4 == 0 else []),
                            tag=f"FqT{s}_{ti}_{ob}_{c4}")
                        tms2.append(tm2)
                    tb2.set_writer(tms2[-1])
                    t2e = sch.v(lambda e, ttn=ttn, ob=ob, tdst2=tdst2: e.tensor_copy(
                        xq2T[:, ttn * 4096 + ob * 512: ttn * 4096 + (ob + 1) * 512],
                        tdst2[:, 0:512]),
                        waits=tb2.rwaits() + (xq2Ts.pwaits() if ttn == 0 and ob == 0 else []),
                        tag=f"FqTe{s}_{ti}_{ob}")
                    tb2.add_reader(t2e)
                    tile_quant_done.append(t2e)
                if s == 0 and grp == 0 and ttn == 0 and "ffn0" in taps:
                    td = mktap("f_xqb", [128, D], BF16)
                    sch.q(lambda e, td=td: e.dma_start(td.ap(), xqb[:]),
                          waits=list(tile_quant_done), tag="tap_fxqb")
                    td2 = mktap("f_xq2T", [128, 8192], BF16)
                    sch.q(lambda e, td2=td2: e.dma_start(td2.ap(), xq2T[:]),
                          waits=list(tile_quant_done), tag="tap_fxq2T")
                    td3 = mktap("f_y1t", [128, 2 * D])
                    sch.q(lambda e, td3=td3: e.dma_start(td3.ap(), y1t[:]),
                          waits=list(tile_quant_done), tag="tap_fy1t")
                    td4 = mktap("f_stat", [128, 64])
                    sch.q(lambda e, td4=td4: e.dma_start(td4.ap(), stat[:]),
                          waits=list(tile_quant_done), tag="tap_fstat")
                    td5 = mktap("f_meanw", [128, 32])
                    sch.q(lambda e, td5=td5: e.dma_start(td5.ap(), meanw_sb[:]),
                          waits=list(tile_quant_done), tag="tap_fmeanw")
            xq2Ts.produced(tile_quant_done[-1])
            if False and "ffn0" in taps:
                td = mktap("f_xqb", [128, D], BF16)
                sch.q(lambda e, td=td: e.dma_start(td.ap(), xqb[:]),
                      waits=tile_quant_done, tag="tap_fxqb")
                td2 = mktap("f_xq2T", [128, 8192], BF16)
                sch.q(lambda e, td2=td2: e.dma_start(td2.ap(), xq2T[:]),
                      waits=tile_quant_done, tag="tap_fxq2T")
                td3 = mktap("f_y1t", [128, 2 * D])
                sch.q(lambda e, td3=td3: e.dma_start(td3.ap(), y1t[:]),
                      waits=tile_quant_done, tag="tap_fy1t")
                td4 = mktap("f_stat", [128, 64])
                sch.q(lambda e, td4=td4: e.dma_start(td4.ap(), stat[:]),
                      waits=tile_quant_done, tag="tap_fstat")
            dbanks = [(banks[0], banks[1]), (banks[2], banks[3])]
            for ttn in range(2):
                for half in range(2):
                    b = dbanks[ttn][half]
            dmms = {0: [], 1: []}
            for oc in range(32):
                wsl = wdslot[oc % 2]
                ldw = sch.g(lambda e, oc=oc: e.dma_start(
                    wdb[:, (oc % 2) * D:(oc % 2 + 1) * D],
                    wdT_d[oc * 128:(oc + 1) * 128, :]),
                    waits=wsl.pwaits() + (tile_quant_done if oc < 2 else []),
                    tag=f"Fwd{s}_{grp}_{oc}")
                wsl.produced(ldw)
                for ttn in range(2):
                    for half in range(2):
                        b = dbanks[ttn][half]
                        mm = sch.t(lambda e, oc=oc, ttn=ttn, half=half, b=b: e.matmul(
                            b.ap[:, 0:512],
                            xq2T[:, ttn * 4096 + oc * 128: ttn * 4096 + (oc + 1) * 128],
                            wdb[:, (oc % 2) * D + half * 512:(oc % 2) * D + (half + 1) * 512],
                            start=(oc == 0), stop=(oc == 31)),
                            waits=[ldw] + tile_quant_done +
                                  (b.wwaits() if oc == 0 else []),
                            tag=f"Fd{s}_{grp}_{oc}_{ttn}_{half}")
                        if oc == 0:
                            b.set_writer(mm)
                        else:
                            b.writer = mm
                        dmms[ttn].append(mm)
                        wsl.consumed(mm)
                        xq2Ts.consumed(mm)
            for ttn in range(2):
                ti = grp * 2 + ttn
                ya = y1t[:, ttn * D:(ttn + 1) * D]
                ev_ops = []
                for half in range(2):
                    b = dbanks[ttn][half]
                    e1 = sch.v(lambda e, b=b, half=half, ttn=ttn: e.tensor_scalar(
                        sq_scr[:, half * 512:(half + 1) * 512], b.ap[:, 0:512],
                        stat[:, 49 + 4 * ttn:50 + 4 * ttn], None, op0=ALU.mult),
                        waits=[dmms[ttn][-1]] + rsq.pwaits() if False else
                              [dmms[ttn][-1]], tag=f"Fe{s}_{ti}_{half}")
                    b.add_reader(e1)
                    e2 = sch.v(lambda e, half=half, ya=ya: e.tensor_tensor(
                        sq_scr[:, half * 512:(half + 1) * 512],
                        sq_scr[:, half * 512:(half + 1) * 512],
                        ya[:, half * 512:(half + 1) * 512], op=ALU.add),
                        waits=[e1], tag=f"Fr{s}_{ti}_{half}")
                    y1s[ttn].consumed(e2)
                    ev_ops.append(e2)
                osl = oslot[0]
                oa = o_stage[:, 0:D]
                nrm2 = emit_ln2(ev_ops + osl.pwaits(), oa, f"FL{s}_{ti}")
                osl.produced(nrm2)
                sto = sch.q(lambda e, s=s, ti=ti, oa=oa: e.dma_start(
                    out_d[s, ti * 128:(ti + 1) * 128, :], oa),
                    waits=[nrm2], tag=f"out{s}_{ti}")
                osl.consumed(sto)
                ffn_done.append(sto)
        prev_seq_done = ffn_done
        return ffn_done

    for s in range(SEQ_PER_CORE if stop_after != "A2_seq0" else 1):
        if stop_after in ("consts", "prep1", "prep2", "prepall"):
            break
        phase_attn(s)
        if stop_after in ("A2", "A2_seq0"):
            break
        phase_ffn(s)

    if stop_after is not None:
        zfill = sch.g(lambda e: e.memset(sq_scr[:], 0.0), dma=False,
                      waits=prev_seq_done, tag="zf")
        for s in range(SEQ_PER_CORE):
            for ti in range(NT):
                sch.q(lambda e, s=s, ti=ti: e.dma_start(
                    out_d[s, ti * 128:(ti + 1) * 128, :], sq_scr[:]),
                    waits=[zfill], tag=f"zero{s}_{ti}")

    sch.finalize(nc)
    return nc, tap_d


def kernel(L0_attn_tokens, L0_attn_wq, L0_attn_wk, L0_attn_wv, L0_attn_wo,
           L0_attn_w_lr, L0_attn_w_target, L0_attn_w_gates, L0_attn_rms_w,
           L0_ffn_w_gate, L0_ffn_w_up, L0_ffn_w_down):
    x = np.ascontiguousarray(np.asarray(L0_attn_tokens, dtype=np.float32))
    consts = _host_consts()
    nc, tap_d = build_program(taps=TAPS, serial=SERIAL, stop_after=STOP_AFTER)
    in_maps = []
    for c in range(NCORES):
        m = {
            "x": x[c * SEQ_PER_CORE:(c + 1) * SEQ_PER_CORE],
            "wq": np.ascontiguousarray(np.asarray(L0_attn_wq, np.float32)),
            "wk": np.ascontiguousarray(np.asarray(L0_attn_wk, np.float32)),
            "wv": np.ascontiguousarray(np.asarray(L0_attn_wv, np.float32)),
            "wo": np.ascontiguousarray(np.asarray(L0_attn_wo, np.float32)),
            "wgates": np.ascontiguousarray(np.asarray(L0_attn_w_gates, np.float32)),
            "rmsw": np.ascontiguousarray(np.asarray(L0_attn_rms_w, np.float32)).reshape(1, D),
            "wgate": np.ascontiguousarray(np.asarray(L0_ffn_w_gate, np.float32)),
            "wup": np.ascontiguousarray(np.asarray(L0_ffn_w_up, np.float32)),
            "wdown": np.ascontiguousarray(np.asarray(L0_ffn_w_down, np.float32)),
        }
        m.update(consts)
        in_maps.append(m)
    res = run_bass_kernel_spmd(nc, in_maps, core_ids=list(range(NCORES)))
    kernel._last_results = res
    out = np.concatenate([np.asarray(r["out"]) for r in res.results], axis=0)
    return out


# version 06 (final)
# speedup vs baseline: 1.3517x; 1.0510x over previous
import numpy as np
import ml_dtypes
import concourse.bass as bass
import concourse.mybir as mybir
from concourse.bass_utils import run_bass_kernel_spmd

F32 = mybir.dt.float32
BF16 = mybir.dt.bfloat16
I32 = mybir.dt.int32
AF = mybir.ActivationFunctionType
ALU = mybir.AluOpType
AX = mybir.AxisListType

B, N, D, H, DH = 32, 2048, 1024, 8, 64
NCORES = 8
SEQ_PER_CORE = B // NCORES
NT = N // 128
NW = N // 512
DC = D // 128
INNER = 4096
ATTN_SCALE = float(DH) ** -0.5
RMS_EPS = float(np.finfo(np.float32).eps)
LN_EPS = 1e-5
QEPS = 1e-5

SERIAL = False
TAPS = ()
STOP_AFTER = None


class _Op:
    __slots__ = ("engine", "fn", "waits", "need_inc", "inc_amt", "eidx", "sem_val",
                 "tag", "is_dma", "semkey", "waited", "need_drain")

    def __init__(self, engine, fn, waits, inc_amt, eidx, tag, is_dma):
        self.engine = engine
        self.fn = fn
        self.waits = list(waits)
        self.need_inc = False
        self.inc_amt = inc_amt
        self.eidx = eidx
        self.sem_val = None
        self.tag = tag
        self.is_dma = is_dma
        self.semkey = engine + "_dma" if is_dma else engine
        self.waited = False
        self.need_drain = False


class Sched:
    ENGINES = ("tensor", "vector", "scalar", "sync", "gpsimd")

    def __init__(self, serial=False):
        self.ops = {e: [] for e in self.ENGINES}
        self.serial = serial
        self.order = []

    def emit(self, engine, fn, waits=(), dma=False, tag="", semkey=None):
        ws = []
        for w in waits:
            if w is None or w is False:
                continue
            if not isinstance(w, _Op):
                raise TypeError(f"bad wait {w!r} at {tag}")
            if w.engine != engine:
                ws.append(w)
            elif engine == "vector":
                w.need_drain = True
        op = _Op(engine, fn, ws, 16 if dma else 1, len(self.ops[engine]), tag, dma)
        if semkey is not None:
            op.semkey = semkey
            op.need_drain = True
        if dma:
            op.need_inc = True
        self.ops[engine].append(op)
        self.order.append(op)
        for w in op.waits:
            w.need_inc = True
            w.waited = True
        return op

    def t(self, fn, waits=(), tag=""):
        return self.emit("tensor", fn, waits, tag=tag)

    def v(self, fn, waits=(), tag=""):
        return self.emit("vector", fn, waits, tag=tag)

    def s(self, fn, waits=(), tag=""):
        return self.emit("scalar", fn, waits, tag=tag)

    def q(self, fn, waits=(), tag=""):
        return self.emit("sync", fn, waits, dma=True, tag=tag)

    def g(self, fn, waits=(), dma=True, tag=""):
        return self.emit("gpsimd", fn, waits, dma=dma, tag=tag)

    def finalize(self, nc):
        if self.serial:
            prev = None
            for op in self.order:
                if prev is not None and prev.engine != op.engine:
                    op.waits.append(prev)
                    prev.need_inc = True
                    prev.waited = True
                prev = op
        semkeys = set()
        for e in self.ENGINES:
            for op in self.ops[e]:
                semkeys.add(op.semkey)
        sems = {k: nc.alloc_semaphore(f"sem_{k}") for k in sorted(semkeys)}
        with nc.Block() as b0:
            @b0.gpsimd
            def _(g):
                for k in sorted(sems):
                    g.sem_clear(sems[k])
        nc.all_engine_barrier()
        totals = {k: 0 for k in semkeys}
        for e in self.ENGINES:
            cum = {}
            for op in self.ops[e]:
                if op.need_inc:
                    cum[op.semkey] = cum.get(op.semkey, 0) + op.inc_amt
                    op.sem_val = cum[op.semkey]
            for k, vv in cum.items():
                totals[k] = vv

        def replay(engine_obj, ename):
            for op in self.ops[ename]:
                best = {}
                for w in op.waits:
                    if w.sem_val is None:
                        raise RuntimeError(f"wait target lacks inc: {w.tag}")
                    if w.sem_val > best.get(w.semkey, -1):
                        best[w.semkey] = w.sem_val
                for we in sorted(best):
                    engine_obj.wait_ge(sems[we], best[we])
                inst = op.fn(engine_obj)
                if op.engine == "vector" and (op.need_inc or op.need_drain):
                    inst = engine_obj.drain()
                if op.need_inc:
                    if inst is None:
                        raise RuntimeError(f"op needs inc but returned None: {op.tag}")
                    inst.then_inc(sems[op.semkey], op.inc_amt)
                if op.is_dma and op.waited and not op.need_drain:
                    engine_obj.wait_ge(sems[op.semkey], op.sem_val)

        with nc.Block() as block:
            @block.tensor
            def _(t):
                replay(t, "tensor")

            @block.vector
            def _(v):
                replay(v, "vector")

            @block.scalar
            def _(s):
                replay(s, "scalar")

            @block.gpsimd
            def _(g):
                replay(g, "gpsimd")

            @block.sync
            def _(s):
                replay(s, "sync")
                for k in sorted(totals):
                    if totals[k] and k != "sync_dma":
                        s.wait_ge(sems[k], totals[k])
                if totals.get("sync_dma"):
                    s.wait_ge(sems["sync_dma"], totals["sync_dma"])


class Bank:

    def __init__(self, ap):
        self.ap = ap
        self.readers = []
        self.writer = None

    def wwaits(self):
        return list(self.readers)

    def set_writer(self, op):
        self.writer = op
        self.readers = []

    def rwaits(self):
        return [self.writer] if self.writer is not None else []

    def add_reader(self, op):
        self.readers.append(op)


class Slot:

    def __init__(self):
        self.consumers = []
        self.producer = None

    def pwaits(self):
        return list(self.consumers)

    def produced(self, op):
        self.producer = op
        self.consumers = []

    def cwaits(self):
        return [self.producer] if self.producer is not None else []

    def consumed(self, op):
        self.consumers.append(op)


def _host_consts():
    c = {}
    c["c_ident"] = np.eye(128, dtype=np.float32)
    c["c_ones"] = np.ones((128, 128), dtype=np.float32)
    masks = np.zeros((4, 128, 512), dtype=np.float32)
    jj = np.arange(128)[:, None]
    ii = np.arange(512)[None, :]
    for r, rel in enumerate((0, 128, 256, 384)):
        masks[r] = (ii >= rel + jj).astype(np.float32)
    c["c_masks"] = masks.reshape(4 * 128, 512)
    selr = np.zeros((4, 128, 128), dtype=np.float32)
    for p in range(4):
        selr[p, 32 * ((2 * p) % 4), 0:64] = 1.0
        selr[p, 32 * ((2 * p + 1) % 4), 64:128] = 1.0
    c["c_selr"] = selr.transpose(1, 0, 2).reshape(128, 4 * 128)
    selg = np.zeros((128, 4 * 128), dtype=np.float32)
    for p in range(4):
        selg[2 * p, p * 128: p * 128 + 64] = 1.0
        selg[2 * p + 1, p * 128 + 64: p * 128 + 128] = 1.0
    c["c_selg"] = selg
    return c


def build_program(taps=(), serial=False, stop_after=None):
    nc = bass.Bass("TRN2", target_bir_lowering=False)
    sch = Sched(serial=serial)
    taps = set(taps)

    x_d = nc.dram_tensor("x", [SEQ_PER_CORE, N, D], F32, kind="ExternalInput")
    wq_d = nc.dram_tensor("wq", [H, D, DH], F32, kind="ExternalInput")
    wk_d = nc.dram_tensor("wk", [H, D, DH], F32, kind="ExternalInput")
    wv_d = nc.dram_tensor("wv", [H, D, DH], F32, kind="ExternalInput")
    wo_d = nc.dram_tensor("wo", [H, DH, D], F32, kind="ExternalInput")
    wgates_d = nc.dram_tensor("wgates", [D, H], F32, kind="ExternalInput")
    rmsw_d = nc.dram_tensor("rmsw", [1, D], F32, kind="ExternalInput")
    wgate_d = nc.dram_tensor("wgate", [INNER, D], F32, kind="ExternalInput")
    wup_d = nc.dram_tensor("wup", [INNER, D], F32, kind="ExternalInput")
    wdown_d = nc.dram_tensor("wdown", [D, INNER], F32, kind="ExternalInput")
    cident_d = nc.dram_tensor("c_ident", [128, 128], F32, kind="ExternalInput")
    cones_d = nc.dram_tensor("c_ones", [128, 128], F32, kind="ExternalInput")
    cmask_d = nc.dram_tensor("c_masks", [4 * 128, 512], F32, kind="ExternalInput")
    cselr_d = nc.dram_tensor("c_selr", [128, 4 * 128], F32, kind="ExternalInput")
    cselg_d = nc.dram_tensor("c_selg", [128, 4 * 128], F32, kind="ExternalInput")

    out_d = nc.dram_tensor("out", [SEQ_PER_CORE, N, D], F32, kind="ExternalOutput")
    y1_d = nc.dram_tensor("y1_scr", [N, D], F32, kind="Internal")
    wgT_d = nc.dram_tensor("wgT_scr", [D, INNER], BF16, kind="Internal")
    scal_d = nc.dram_tensor("scal_scr", [1, 8], F32, kind="Internal")
    col_d = nc.dram_tensor("col_scr", [128, 8], F32, kind="Internal")
    wuT_d = nc.dram_tensor("wuT_scr", [D, INNER], BF16, kind="Internal")
    wdT_d = nc.dram_tensor("wdT_scr", [INNER, D], BF16, kind="Internal")

    tap_d = {}

    def mktap(name, shape, dtype=F32):
        tap_d[name] = nc.dram_tensor("tap_" + name, list(shape), dtype, kind="ExternalOutput")
        return tap_d[name]

    U8 = mybir.dt.uint8
    ARENA_B = 211200
    arena = nc.alloc_sbuf_tensor("arena", [128, ARENA_B], U8)
    _dtsz = {F32: 4, BF16: 2, I32: 4, U8: 1}
    _cur = {"static": 0, "phase": None}

    def AR(name, shape, dtype, phase="static"):
        nb = shape[1] * _dtsz[dtype]
        nb = (nb + 63) & ~63
        if phase == "static":
            off = _cur["static"]
            _cur["static"] += nb
            assert _cur["static"] <= ARENA_B, f"static overflow at {name}"
        else:
            off = _cur[phase]
            _cur[phase] += nb
            assert _cur[phase] <= ARENA_B, f"{phase} overflow at {name} ({_cur[phase]})"
        return arena[0:shape[0], off:off + shape[1] * _dtsz[dtype]].bitcast(dtype)

    ident_f = AR("ident_f", [128, 128], F32)
    ident_b = AR("ident_b", [128, 128], BF16)
    ones_f = AR("ones_f", [128, 128], F32)
    ones_b = AR("ones_b", [128, 128], BF16)
    masks_b = AR("masks_b", [128, 4 * 512], BF16)
    selr_f = AR("selr_f", [128, 4 * 128], F32)
    selg_f = AR("selg_f", [128, 4 * 128], BF16)
    rmsw_rep = AR("rmsw_rep", [128, D], F32)
    attn_scr = AR("attn_scr", [128, 192], F32)
    zrow_b = AR("zrow_b", [1, 512], BF16)
    x_tile = AR("x_tile", [128, 2 * D], F32)
    sq_scr = AR("sq_scr", [128, D], F32)
    dump_scr = AR("dump_scr", [128, D], F32)
    stat = AR("stat", [128, 64], F32)
    meanw_sb = AR("meanw_sb", [128, 32], F32)
    base = _cur["static"]
    _cur["attn"] = base
    _cur["ffn"] = base

    wq_l = AR("wq_l", [128, DC * 512], BF16, "attn")
    wk_l = AR("wk_l", [128, DC * 512], BF16, "attn")
    wv_l = AR("wv_l", [128, DC * 512], BF16, "attn")
    wgates_l = AR("wgates_l", [128, DC * 8], BF16, "attn")
    wo_l = AR("wo_l", [128, 4 * D], BF16, "attn")
    tT = AR("tT", [128, DC * N], BF16, "attn")
    qT = AR("qT", [128, 4 * N], BF16, "attn")
    kT = AR("kT", [128, 4 * N], BF16, "attn")
    v_sb = AR("v_sb", [128, NT * 512], BF16, "attn")
    gT_sb = AR("gT_sb", [128, N], BF16, "attn")
    t_tile = AR("t_tile", [128, 2 * D], BF16, "attn")
    nm_wide = AR("nm_wide", [128, 4 * 256], F32, "attn")
    corrT_sb = AR("corrT_sb", [128, 2 * 512], F32, "attn")
    pT_sb = AR("pT_sb", [128, 4 * 512], BF16, "attn")
    rl_sb = AR("rl_sb", [128, 2 * 512], F32, "attn")
    tg_sb = AR("tg_sb", [128, 2 * 512], F32, "attn")
    outT_sb = AR("outT_sb", [128, 4 * 512], BF16, "attn")
    y1_stage = AR("y1_stage", [128, 2 * D], F32, "attn")

    wgT_sb = AR("wgT_sb", [128, DC * INNER], BF16, "ffn")
    wuT_sb = AR("wuT_sb", [128, DC * INNER], BF16, "ffn")
    y1t = AR("y1t", [128, 2 * D], F32, "ffn")
    xqi = AR("xqi", [128, 512], I32, "ffn")
    xqb = AR("xqb", [128, D], BF16, "ffn")
    xqT = AR("xqT", [128, DC * 128], BF16, "ffn")
    sg_sb = AR("sg_sb", [128, 2 * 512], F32, "ffn")
    h6_sb = AR("h6_sb", [128, 512], F32, "ffn")
    h7_sb = AR("h7_sb", [128, 512], F32, "ffn")
    xq2b = AR("xq2b", [128, 512], BF16, "ffn")
    xq2T = AR("xq2T", [128, 2 * 32 * 128], BF16, "ffn")
    wdb = AR("wdb", [128, 2 * D], BF16, "ffn")
    o_stage = AR("o_stage", [128, D], F32, "ffn")
    assert max(_cur["attn"], _cur["ffn"]) <= ARENA_B

    P = nc.alloc_psum_tensor
    psA = P("psA", [128, 2048], F32)
    psB = P("psB", [128, 2048], F32)
    banks = [Bank(psA[:, i * 512:(i + 1) * 512]) for i in range(4)] + \
            [Bank(psB[:, i * 512:(i + 1) * 512]) for i in range(4)]

    ld_ident = sch.q(lambda e: e.dma_start(ident_f[:], cident_d.ap()), tag="ident")
    ld_identb = sch.g(lambda e: e.dma_start(ident_b[:], cident_d.ap()), tag="identb")
    ld_ones = sch.q(lambda e: e.dma_start(ones_f[:], cones_d.ap()), tag="ones")
    ld_masks = sch.g(lambda e: e.dma_start(
        masks_b[:].rearrange("p (r f) -> p r f", r=4),
        cmask_d.ap().rearrange("(r p) f -> p r f", r=4)), tag="masks")
    ld_selr = sch.q(lambda e: e.dma_start(selr_f[:], cselr_d.ap()), tag="selr")
    ld_selg = sch.g(lambda e: e.dma_start(selg_f[:], cselg_d.ap()), tag="selg")
    ld_rmsw = sch.g(lambda e: e.dma_start(
        rmsw_rep[:], rmsw_d.ap().broadcast_to([128, D])), tag="rmsw")
    ld_onesb = sch.g(lambda e: e.dma_start(ones_b[:], cones_d.ap()), tag="onesb")
    ld_zrow = sch.g(lambda e: e.memset(zrow_b[:], 0.0), dma=False, tag="zrow")

    def load_attn_weights(deps, s):
        ops = []
        for dst, src_d, pariter, nm2 in ((wq_l, wq_d, True, "wq"), (wk_l, wk_d, True, "wk"),
                                         (wv_l, wv_d, False, "wv")):
            for h in range(H):
                coloff = ((h // 2) * 128 + (h % 2) * 64) if pariter else h * 64
                dst_ap = dst.rearrange("p (dc c) -> p dc c", dc=DC)[:, :, coloff:coloff + 64]
                src_ap = src_d[h].rearrange("(dc p) e -> p dc e", p=128)
                ops.append(sch.g(lambda e, d=dst_ap, s2=src_ap: e.dma_start(d, s2),
                                 waits=deps if (h == 0 and nm2 == "wq") else [],
                                 tag=f"attw{s}_{nm2}{h}"))
        ops.append(sch.g(lambda e: e.dma_start(
            wgates_l.rearrange("p (dc h) -> p dc h", dc=DC),
            wgates_d.ap().rearrange("(dc p) h -> p dc h", p=128)),
            waits=deps, tag=f"wgates{s}"))
        ops.append(sch.g(lambda e: e.dma_start(
            wo_l.rearrange("p (pc d) -> p pc d", pc=4),
            wo_d.ap().rearrange("(pc hh) e d -> (hh e) pc d", hh=2)),
            waits=deps, tag=f"wo{s}"))
        return ops

    def rsqrt_col(dep, src_col, dst_col, eps, tag, scratch_col=9):
        eps_ap = _cap(eps)
        o1 = sch.s(lambda e, eps_ap=eps_ap: e.activation(
            stat[:, scratch_col:scratch_col + 1], stat[:, src_col:src_col + 1], AF.Ln,
            bias=eps_ap), waits=[dep], tag=tag + "_ln")
        o2 = sch.s(lambda e: e.activation(
            stat[:, dst_col:dst_col + 1], stat[:, scratch_col:scratch_col + 1],
            AF.Exp, scale=-0.5), waits=[o1], tag=tag + "_ex")
        return o2

    _const_cols = {}

    def _cap(val):
        if val not in _const_cols:
            idx = 56 + len(_const_cols)
            assert idx < 64
            _const_cols[val] = idx
            sch.g(lambda e, idx=idx, val=val: e.memset(stat[:, idx:idx + 1], float(val)),
                  dma=False, tag=f"const{val}")
        idx = _const_cols[val]
        return stat[:, idx:idx + 1]

    wprep_scr = tT[:].bitcast(F32)[:, 0:INNER]
    wprep_i = qT[:].bitcast(I32)[:, 0:INNER]
    abs_scr = kT[:].bitcast(F32)[:, 0:INNER]
    wprep_q = v_sb[:, 0:INNER]
    wprep_Ts = outT_sb[:, 0:1024]

    def prep_ffn_weight(src_d, dst_T_d, rows, cols, name):
        RT = rows // 128
        sum_ops = []
        scr = Slot()
        for r in range(RT):
            ld = sch.q(lambda e, r=r: e.dma_start(wprep_scr[:, 0:cols],
                                                  src_d[r * 128:(r + 1) * 128, :]),
                       waits=scr.pwaits(), tag=f"{name}ld{r}")
            scr.produced(ld)
            so = sch.s(lambda e, r=r: e.activation(
                abs_scr[:, 0:cols], wprep_scr[:, 0:cols], AF.Abs,
                accum_out=stat[:, r:r + 1]), waits=[ld], tag=f"{name}abs{r}")
            scr.consumed(so)
            sum_ops.append(so)
        st_free = sch.s(lambda e, RT=RT: e.activation(
            abs_scr[:, 0:RT], stat[:, 0:RT], AF.Copy, accum_out=stat[:, 32:33]),
            waits=[sum_ops[-1]], tag=f"{name}stf")
        if name == "Pg" and "pstat0" in taps:
            td = mktap("pstat0", [128, 64])
            sch.q(lambda e, td=td: e.dma_start(td.ap(), stat[:]),
                  waits=[st_free], tag="tap_pstat0")
        g0 = sch.q(lambda e: e.dma_start(col_d.ap(), stat[:, 32:40]),
                   waits=[st_free], tag=f"{name}pg0")
        g1 = sch.q(lambda e: e.dma_start(sq_scr[0:1, 0:1024],
                                         col_d.ap().rearrange("p o -> (p o)").unsqueeze(0)),
                   waits=[g0], tag=f"{name}pg")
        rr = sch.v(lambda e: e.reduce_sum(
            stat[0:1, 33:34],
            sq_scr[0:1, 0:1024].rearrange("o (p c) -> o p c", c=8)[:, :, 0:1],
            axis=AX.XY), waits=[g1], tag=f"{name}rr")
        midx = {"Pg": 0, "Pu": 1, "Pd": 2}[name]
        s1 = sch.q(lambda e, midx=midx: e.dma_start(
            scal_d[0:1, midx:midx + 1], stat[0:1, 33:34]),
            waits=[rr], tag=f"{name}s1")
        bcast = sch.q(lambda e, midx=midx: e.dma_start(
            stat[:, 33:34], scal_d[0:1, midx:midx + 1].broadcast_to([128, 1])),
            waits=[s1], tag=f"{name}bc")
        c1 = sch.v(lambda e: e.tensor_scalar(
            stat[:, 41:49], stat[:, 33:34].broadcast_to([128, 8]),
            1.0 / (rows * cols), QEPS,
            op0=ALU.mult, op1=ALU.max), waits=[bcast], tag=f"{name}clip")
        c2a = sch.s(lambda e: e.activation(stat[:, 49:50], stat[:, 41:42], AF.Ln),
                    waits=[c1], tag=f"{name}rln")
        c2 = sch.s(lambda e: e.activation(stat[:, 50:51], stat[:, 49:50], AF.Exp,
                                          scale=-1.0), waits=[c2a], tag=f"{name}rex")
        if name == "Pg" and "pstat1" in taps:
            td = mktap("pstat1", [128, 64])
            sch.q(lambda e, td=td: e.dma_start(td.ap(), stat[:]), waits=[c2],
                  tag="tap_pstat1")

        last_stores = []
        tb = banks[3]
        qslot = Slot()
        for r in range(RT):
            ld = sch.q(lambda e, r=r: e.dma_start(wprep_scr[:, 0:cols],
                                                  src_d[r * 128:(r + 1) * 128, :]),
                       waits=[c2] + scr.pwaits(), tag=f"{name}ld2_{r}")
            scr.produced(ld)
            qi = sch.v(lambda e: e.tensor_scalar_mul(
                wprep_i[:, 0:cols], wprep_scr[:, 0:cols], stat[:, 50:51]),
                waits=[ld], tag=f"{name}rnd{r}")
            scr.consumed(qi)
            qc = sch.v(lambda e: e.tensor_scalar(
                wprep_q[:, 0:cols], wprep_i[:, 0:cols], -1.0, 1.0,
                op0=ALU.max, op1=ALU.min), waits=[qi] + qslot.pwaits(), tag=f"{name}clp{r}")
            qslot.produced(qc)
            CT = cols // 128
            for ct0 in range(0, CT, 8):
                nct = min(8, CT - ct0)
                tms = []
                for k in range(nct):
                    ct = ct0 + k
                    tm = sch.t(lambda e, ct=ct, k=k: e.transpose(
                        tb.ap[:].bitcast(BF16)[:, k * 128:(k + 1) * 128],
                        wprep_q[:, ct * 128:(ct + 1) * 128], ident_b[:]),
                        waits=[qc, ld_identb] + (tb.wwaits() if k == 0 else []),
                        tag=f"{name}T{r}_{ct}")
                    tms.append(tm)
                tb.set_writer(tms[-1])
                qslot.consumed(tms[-1])
                ev = sch.v(lambda e, nct=nct: e.tensor_copy(
                    wprep_Ts[:, 0:nct * 128], tb.ap[:].bitcast(BF16)[:, 0:nct * 128]),
                    waits=tb.rwaits(), tag=f"{name}Tev{r}_{ct0}")
                tb.add_reader(ev)
                stv = sch.q(lambda e, r=r, ct0=ct0, nct=nct: e.dma_start(
                    dst_T_d[ct0 * 128:ct0 * 128 + nct * 128, r * 128:(r + 1) * 128]
                        .rearrange("(n p) f -> p n f", p=128),
                    wprep_Ts[:, 0:nct * 128].rearrange("p (n f) -> p n f", f=128)),
                    waits=[ev], tag=f"{name}st{r}_{ct0}")
                last_stores.append(stv)
        return last_stores

    if stop_after not in ("consts",):
        wprep_done = []
        wprep_done += prep_ffn_weight(wgate_d, wgT_d, INNER, D, "Pg")
        if "pstat" in taps:
            td = mktap("pstat", [128, 64])
            sch.q(lambda e, td=td: e.dma_start(td.ap(), stat[:]), waits=wprep_done,
                  tag="tap_pstat")
        if stop_after != "prep1":
            wprep_done += prep_ffn_weight(wup_d, wuT_d, INNER, D, "Pu")
            if stop_after != "prep2":
                wprep_done += prep_ffn_weight(wdown_d, wdT_d, D, INNER, "Pd")
        if "wgT" in taps:
            td = mktap("wgT", [D, INNER], BF16)
            sch.q(lambda e, td=td: e.dma_start(td.ap(), wgT_d.ap()), waits=wprep_done, tag="tapwgT")
        if "wdT" in taps:
            td = mktap("wdT", [INNER, D], BF16)
            sch.q(lambda e, td=td: e.dma_start(td.ap(), wdT_d.ap()), waits=wprep_done, tag="tapwdT")

    prev_seq_done = []

    def phase_attn(s):
        nonlocal prev_seq_done
        ld_attw = load_attn_weights(list(prev_seq_done), s)
        zrl = sch.g(lambda e: e.memset(rl_sb[:], 0.0), dma=False,
                    waits=list(prev_seq_done), tag=f"zrl{s}")
        deps = list(prev_seq_done) + ld_attw + [ld_identb, ld_rmsw, zrl]

        xslot = [Slot(), Slot()]
        tslot = [Slot(), Slot()]
        a1_done = []
        tb = banks[0]
        for ti in range(NT):
            w = ti % 2
            xa = x_tile[:, w * D:(w + 1) * D]
            ta = t_tile[:, w * D:(w + 1) * D]
            ld = sch.q(lambda e, s=s, ti=ti, xa=xa: e.dma_start(
                xa, x_d[s, ti * 128:(ti + 1) * 128, :]),
                waits=deps + xslot[w].pwaits(), tag=f"x{s}_{ti}")
            xslot[w].produced(ld)
            sq = sch.s(lambda e, xa=xa: e.activation(
                sq_scr[:], xa, AF.Square, accum_out=stat[:, 0:1]),
                waits=[ld], tag=f"sq{s}_{ti}")
            st1 = sch.v(lambda e: e.tensor_scalar(
                stat[:, 1:9], stat[:, 0:1].broadcast_to([128, 8]), 1.0 / D, None,
                op0=ALU.mult), waits=[sq], tag=f"rmsm{s}_{ti}")
            rs = rsqrt_col(st1, 1, 10, RMS_EPS, f"rms{s}_{ti}")
            tt = sch.v(lambda e, xa=xa, ta=ta: e.scalar_tensor_tensor(
                ta, xa, stat[:, 10:11], rmsw_rep[:], op0=ALU.mult, op1=ALU.mult),
                waits=[rs] + tslot[w].pwaits(), tag=f"t{s}_{ti}")
            xslot[w].consumed(tt)
            tslot[w].produced(tt)
            tms = []
            for dc in range(DC):
                tm = sch.t(lambda e, dc=dc, ta=ta: e.transpose(
                    tb.ap[:].bitcast(BF16)[:, dc * 128:(dc + 1) * 128],
                    ta[:, dc * 128:(dc + 1) * 128], ident_b[:]),
                    waits=[tt] + (tb.wwaits() if dc == 0 else []), tag=f"tT{s}_{ti}_{dc}")
                tms.append(tm)
            tb.set_writer(tms[-1])
            ev = sch.v(lambda e, ti=ti: e.tensor_copy(
                tT[:].rearrange("p (dc n) -> p dc n", dc=DC)
                    [:, :, ti * 128:(ti + 1) * 128],
                tb.ap[:].bitcast(BF16)[:, 0:1024].rearrange("p (dc n) -> p dc n", dc=DC)),
                waits=tb.rwaits(), tag=f"tTev{s}_{ti}")
            tb.add_reader(ev)
            tslot[w].consumed(ev)
            a1_done.append(ev)

        if "tT" in taps and s == 0:
            td = mktap("tT", [128, DC * N], BF16)
            a1_done.append(sch.q(lambda e, td=td: e.dma_start(td.ap(), tT[:]),
                                 waits=list(a1_done), tag="tap_tT"))

        a2_done = []
        for dst, wl, nm in ((qT, wq_l, "q"), (kT, wk_l, "k")):
            for c in range(4):
                for nb in range(4):
                    b = banks[2 + (nb + c * 4) % 2]
                    mms = []
                    for dc in range(DC):
                        mm = sch.t(lambda e, c=c, nb=nb, dc=dc, wl=wl, b=b: e.matmul(
                            b.ap[:, 0:512],
                            wl[:, dc * 512 + c * 128: dc * 512 + (c + 1) * 128],
                            tT[:, dc * N + nb * 512: dc * N + (nb + 1) * 512],
                            start=(dc == 0), stop=(dc == DC - 1)),
                            waits=(a1_done + b.wwaits()) if dc == 0 else [],
                            tag=f"{nm}mm{s}_{c}_{nb}_{dc}")
                        mms.append(mm)
                    b.set_writer(mms[-1])
                    ev = sch.v(lambda e, c=c, nb=nb, dst=dst, b=b: e.tensor_copy(
                        dst[:, c * N + nb * 512: c * N + (nb + 1) * 512], b.ap[:, 0:512]),
                        waits=b.rwaits(), tag=f"{nm}ev{s}_{c}_{nb}")
                    b.add_reader(ev)
                    a2_done.append(ev)
        v_done = []
        for ti in range(NT):
            b = banks[2 + ti % 2]
            mms = []
            for dc in range(DC):
                mm = sch.t(lambda e, ti=ti, dc=dc, b=b: e.matmul(
                    b.ap[:, 0:512],
                    tT[:, dc * N + ti * 128: dc * N + (ti + 1) * 128],
                    wv_l[:, dc * 512:(dc + 1) * 512],
                    start=(dc == 0), stop=(dc == DC - 1)),
                    waits=(a1_done + b.wwaits()) if dc == 0 else [],
                    tag=f"vmm{s}_{ti}_{dc}")
                mms.append(mm)
            b.set_writer(mms[-1])
            ev = sch.v(lambda e, ti=ti, b=b: e.tensor_copy(
                v_sb[:, ti * 512:(ti + 1) * 512], b.ap[:, 0:512]),
                waits=b.rwaits(), tag=f"vev{s}_{ti}")
            b.add_reader(ev)
            v_done.append(ev)
            a2_done.append(ev)
        g_done = []
        for nb in range(4):
            b = banks[4 + nb % 2]
            mms = []
            for dc in range(DC):
                mm = sch.t(lambda e, nb=nb, dc=dc, b=b: e.matmul(
                    b.ap[0:8, 0:512],
                    wgates_l[:, dc * 8:(dc + 1) * 8],
                    tT[:, dc * N + nb * 512: dc * N + (nb + 1) * 512],
                    start=(dc == 0), stop=(dc == DC - 1)),
                    waits=(a1_done + b.wwaits()) if dc == 0 else [],
                    tag=f"gmm{s}_{nb}_{dc}")
                mms.append(mm)
            b.set_writer(mms[-1])
            sg = sch.s(lambda e, nb=nb, b=b: e.activation(
                gT_sb[0:8, nb * 512:(nb + 1) * 512], b.ap[0:8, 0:512], AF.Sigmoid),
                waits=b.rwaits(), tag=f"gsig{s}_{nb}")
            b.add_reader(sg)
            g_done.append(sg)
            a2_done.append(sg)

        if s == 0:
            for nm, buf, shape in (("qT", qT, [128, 4 * N]), ("kT", kT, [128, 4 * N]),
                                   ("v", v_sb, [128, NT * 512])):
                if nm in taps:
                    td = mktap(nm, shape, BF16)
                    a2_done.append(sch.q(lambda e, td=td, buf=buf: e.dma_start(td.ap(), buf[:]),
                                         waits=list(a2_done), tag=f"tap_{nm}"))
            if "gT" in taps:
                td = mktap("gT", [128, N], BF16)
                a2_done.append(sch.q(lambda e, td=td: e.dma_start(td.ap(), gT_sb[:]),
                                     waits=list(g_done), tag="tap_gT"))

        pT = [Slot() for _ in range(4)]
        y1slot = [Slot(), Slot()]
        outT_slot = Slot()
        rsq = Slot()
        xr = [Slot(), Slot()]
        a4_done = []
        sbk = [banks[0], banks[1]]
        lbk = [banks[2], banks[3]]
        pvb = [banks[4], banks[5], banks[6], banks[7]]
        HROW = [32 * (h % 4) for h in range(8)]

        for w in range(NW):
            nm_ready = {}
            jw = (w + 1) * 512
            for k in range(4):
                it = 4 * w + k
                for p in range(4):
                    hA, hB = 2 * p, 2 * p + 1
                    mmsA, mmsB = [], []
                    for jc5 in range(w + 1):
                        bA, bB = banks[jc5], banks[4 + jc5]
                        mmA = sch.t(lambda e, p=p, it=it, jc5=jc5, bA=bA: e.matmul(
                            bA.ap[:, 0:512],
                            qT[0:64, p * N + it * 128: p * N + (it + 1) * 128],
                            kT[0:64, p * N + jc5 * 512: p * N + (jc5 + 1) * 512],
                            start=True, stop=True, tile_position=(0, 0)),
                            waits=(a2_done if (w == 0 and k == 0 and p == 0 and jc5 == 0)
                                   else []) + bA.wwaits(),
                            tag=f"S{s}w{w}k{k}p{p}c{jc5}A")
                        bA.set_writer(mmA)
                        mmsA.append((bA, mmA))
                        mmB = sch.t(lambda e, p=p, it=it, jc5=jc5, bB=bB: e.matmul(
                            bB.ap[:, 0:512],
                            qT[64:128, p * N + it * 128: p * N + (it + 1) * 128],
                            kT[64:128, p * N + jc5 * 512: p * N + (jc5 + 1) * 512],
                            start=True, stop=True, tile_position=(64, 0)),
                            waits=bB.wwaits(), tag=f"S{s}w{w}k{k}p{p}c{jc5}B")
                        bB.set_writer(mmB)
                        mmsB.append((bB, mmB))
                    for h, reg, mml_ in ((hA, psA, mmsA), (hB, psB, mmsB)):
                        slot = k * 256 + (h // 4) * 128 + HROW[h]
                        nm = sch.v(lambda e, reg=reg, slot=slot, jw=jw: e.tensor_scalar(
                            pT_sb[:, 0:jw], reg[:, 0:jw],
                            -1.0, None, op0=ALU.mult, op1=ALU.min,
                            accum_out=nm_wide[:, slot:slot + 1]),
                            waits=[m for _, m in mml_] + sum(
                                (pT[i].pwaits() for i in range(4)), []),
                            tag=f"nm{s}w{w}k{k}h{h}")
                        for bk, _ in mml_:
                            bk.add_reader(nm)
                        nm_ready[(k, h)] = nm
            for i in range(4):
                pT[i].produced(list(nm_ready.values())[-1])
            if s == 0 and w == 0 and "pmax0" in taps:
                td = mktap("pmax0", [128, 192])
                sch.q(lambda e, td=td: e.dma_start(td.ap(), attn_scr[:]),
                      waits=list(nm_ready.values()), tag="tap_pmax0")
            corr_evs = []
            for g in range(2):
                cb = pvb[g]
                tms = []
                for k in range(4):
                    base = k * 256 + g * 128
                    deps = [nm_ready[(k, h)] for h in range(4 * g, 4 * g + 4)]
                    tm = sch.t(lambda e, base=base, k=k, cb=cb: e.transpose(
                        cb.ap[0:97, k * 128:(k + 1) * 128],
                        nm_wide[:, base:base + 97], ident_f[:]),
                        waits=deps + [ld_ident] + (cb.wwaits() if k == 0 else []),
                        tag=f"cT{s}w{w}g{g}k{k}")
                    tms.append(tm)
                cb.set_writer(tms[-1])
                ev = sch.v(lambda e, g=g, cb=cb: e.tensor_copy(
                    corrT_sb[:, g * 512:(g + 1) * 512], cb.ap[:, 0:512]),
                    waits=cb.rwaits(), tag=f"cTe{s}w{w}g{g}")
                cb.add_reader(ev)
                corr_evs.append(ev)
            if s == 0 and w == 0 and "nm0" in taps:
                td = mktap("nm0", [128, 4 * 256])
                sch.q(lambda e, td=td: e.dma_start(td.ap(), nm_wide[:]),
                      waits=list(nm_ready.values()), tag="tap_nm0")
                td2 = mktap("corrT0", [128, 1024])
                sch.q(lambda e, td2=td2: e.dma_start(td2.ap(), corrT_sb[:]),
                      waits=corr_evs, tag="tap_corrT0")
            zb_ops = []
            for bk in (lbk[0], lbk[1], pvb[0], pvb[1], pvb[2], pvb[3]):
                z = sch.t(lambda e, bk=bk: e.matmul(
                    bk.ap[:, 0:512], ones_b[0:1, :], zrow_b[:],
                    start=True, stop=True),
                    waits=[ld_zrow, ld_onesb] + bk.wwaits(), tag=f"zb{s}w{w}")
                bk.set_writer(z)
                zb_ops.append(z)
            njc = 4 * w + 4
            for jc in range(njc):
                for p in range(4):
                    hA, hB = 2 * p, 2 * p + 1
                    stA, stB = sbk[0], sbk[1]
                    mmA = sch.t(lambda e, p=p, jc=jc, w=w, stA=stA: e.matmul(
                        stA.ap[:, 0:512],
                        kT[0:64, p * N + jc * 128: p * N + (jc + 1) * 128],
                        qT[0:64, p * N + w * 512: p * N + (w + 1) * 512],
                        start=True, stop=False, tile_position=(0, 0)),
                        waits=stA.wwaits() + corr_evs, tag=f"sT{s}w{w}j{jc}p{p}A")
                    mmB = sch.t(lambda e, p=p, jc=jc, w=w, stB=stB: e.matmul(
                        stB.ap[:, 0:512],
                        kT[64:128, p * N + jc * 128: p * N + (jc + 1) * 128],
                        qT[64:128, p * N + w * 512: p * N + (w + 1) * 512],
                        start=True, stop=False, tile_position=(64, 0)),
                        waits=stB.wwaits() + corr_evs, tag=f"sT{s}w{w}j{jc}p{p}B")
                    crA = sch.t(lambda e, hA=hA, stA=stA: e.matmul(
                        stA.ap[:, 0:512],
                        ones_f[HROW[hA]:HROW[hA] + 1, 0:128],
                        corrT_sb[HROW[hA]:HROW[hA] + 1, (hA // 4) * 512:(hA // 4) * 512 + 512],
                        start=False, stop=True, tile_position=(HROW[hA], 0)),
                        waits=[mmA, ld_ones], tag=f"cr{s}w{w}j{jc}p{p}A")
                    stA.set_writer(crA)
                    crB = sch.t(lambda e, hB=hB, stB=stB: e.matmul(
                        stB.ap[:, 0:512],
                        ones_f[HROW[hB]:HROW[hB] + 1, 0:128],
                        corrT_sb[HROW[hB]:HROW[hB] + 1, (hB // 4) * 512:(hB // 4) * 512 + 512],
                        start=False, stop=True, tile_position=(HROW[hB], 0)),
                        waits=[mmB, ld_ones], tag=f"cr{s}w{w}j{jc}p{p}B")
                    stB.set_writer(crB)
                    slA = pT[(2 * (jc * 4 + p)) % 4]
                    slB = pT[(2 * (jc * 4 + p) + 1) % 4]
                    siA, siB = pT.index(slA), pT.index(slB)
                    assert siB == siA + 1 and siA % 2 == 0
                    ex2 = sch.s(lambda e, siA=siA: e.activation(
                        pT_sb[:, siA * 512: siA * 512 + 1024],
                        psA[:, 0:1024], AF.Exp, scale=ATTN_SCALE),
                        waits=stA.rwaits() + stB.rwaits() + slA.pwaits() + slB.pwaits(),
                        tag=f"ex{s}w{w}j{jc}p{p}")
                    stA.add_reader(ex2)
                    stB.add_reader(ex2)
                    slA.produced(ex2)
                    slB.produced(ex2)
                    srcs = {hA: ex2, hB: ex2}
                    if jc >= 4 * w:
                        r_idx = jc - 4 * w
                        mkA = sch.v(lambda e, siA=siA, r_idx=r_idx: e.tensor_tensor(
                            pT_sb[:, siA * 512: siA * 512 + 512],
                            pT_sb[:, siA * 512: siA * 512 + 512],
                            masks_b[:, r_idx * 512:(r_idx + 1) * 512], op=ALU.mult),
                            waits=[ex2], tag=f"mkA{s}w{w}j{jc}p{p}")
                        mkB = sch.v(lambda e, siB=siB, r_idx=r_idx: e.tensor_tensor(
                            pT_sb[:, siB * 512: siB * 512 + 512],
                            pT_sb[:, siB * 512: siB * 512 + 512],
                            masks_b[:, r_idx * 512:(r_idx + 1) * 512], op=ALU.mult),
                            waits=[ex2, ld_masks], tag=f"mkB{s}w{w}j{jc}p{p}")
                        slA.produced(mkA)
                        slB.produced(mkB)
                        srcs = {hA: mkA, hB: mkB}
                    for h, bk, sl, si in ((hA, stA, slA, siA), (hB, stB, slB, siB)):
                        src_op = srcs[h]
                        pb_ = pvb[p]
                        mmpv = sch.t(lambda e, h=h, jc=jc, si=si, pb_=pb_: e.matmul(
                            pb_.ap[(h % 2) * 64:(h % 2) * 64 + 64, 0:512],
                            v_sb[:, jc * 512 + h * 64: jc * 512 + (h + 1) * 64],
                            pT_sb[:, si * 512: si * 512 + 512],
                            start=False, stop=False,
                            tile_position=(0, (h % 2) * 64)),
                            waits=[src_op],
                            tag=f"pv{s}w{w}j{jc}h{h}")
                        pb_.set_writer(mmpv)
                        sl.consumed(mmpv)
                        lb = lbk[h // 4]
                        mml = sch.t(lambda e, h=h, jc=jc, si=si, lb=lb: e.matmul(
                            lb.ap[HROW[h]:HROW[h] + 1, 0:512],
                            ones_b[:, 0:1],
                            pT_sb[:, si * 512: si * 512 + 512],
                            start=False, stop=False,
                            tile_position=(0, HROW[h])),
                            waits=[src_op],
                            tag=f"l{s}w{w}j{jc}h{h}")
                        lb.set_writer(mml)
                        sl.consumed(mml)
            rl_ops = []
            for g in range(2):
                lb = lbk[g]
                for hh in range(4):
                    row = 32 * hh
                    rc = sch.v(lambda e, g=g, row=row, lb=lb: e.reciprocal(
                        rl_sb[row:row + 1, g * 512:(g + 1) * 512],
                        lb.ap[row:row + 1, 0:512]),
                        waits=lb.rwaits(), tag=f"rl{s}w{w}g{g}r{row}")
                    lb.add_reader(rc)
                    rl_ops.append(rc)
            if s == 0 and w == 0 and "rl0" in taps:
                td = mktap("rl0", [128, 1024])
                sch.q(lambda e, td=td: e.dma_start(td.ap(), rl_sb[:]),
                      waits=rl_ops, tag="tap_rl0")
            outT_evs = []
            for p in range(4):
                g = p // 2
                trl_b, tg_b = sbk[0], sbk[1]
                m1 = sch.t(lambda e, p=p, g=g, trl_b=trl_b: e.matmul(
                    trl_b.ap[:, 0:512], selr_f[:, p * 128:(p + 1) * 128],
                    rl_sb[:, g * 512:(g + 1) * 512], start=True, stop=True),
                    waits=rl_ops + [ld_selr] + trl_b.wwaits(), tag=f"trl{s}w{w}p{p}")
                trl_b.set_writer(m1)
                m2 = sch.t(lambda e, p=p, w=w, tg_b=tg_b: e.matmul(
                    tg_b.ap[:, 0:512], selg_f[0:8, p * 128:(p + 1) * 128],
                    gT_sb[0:8, w * 512:(w + 1) * 512], start=True, stop=True),
                    waits=g_done + [ld_selg] + tg_b.wwaits(), tag=f"tg{s}w{w}p{p}")
                tg_b.set_writer(m2)
                c1_ = sch.v(lambda e, tg_b=tg_b: e.tensor_copy(
                    tg_sb[:, 0:512], tg_b.ap[:, 0:512]),
                    waits=tg_b.rwaits(), tag=f"tgc{s}w{w}p{p}")
                tg_b.add_reader(c1_)
                c2_ = sch.v(lambda e, trl_b=trl_b: e.tensor_tensor(
                    tg_sb[:, 512:1024], trl_b.ap[:, 0:512], tg_sb[:, 0:512],
                    op=ALU.mult), waits=trl_b.rwaits() + [c1_], tag=f"tgm{s}w{w}p{p}")
                trl_b.add_reader(c2_)
                pb_ = pvb[p]
                ev = sch.v(lambda e, p=p, pb_=pb_: e.tensor_tensor(
                    outT_sb[:, p * 512:(p + 1) * 512], pb_.ap[:, 0:512],
                    tg_sb[:, 512:1024], op=ALU.mult),
                    waits=pb_.rwaits() + [c2_] + (outT_slot.pwaits() if p == 0 else []),
                    tag=f"oev{s}w{w}p{p}")
                pb_.add_reader(ev)
                outT_evs.append(ev)
            outT_slot.produced(outT_evs[-1])
            if s == 0 and w == 0 and "outT0" in taps:
                td = mktap("outT0", [128, 2048], BF16)
                sch.q(lambda e, td=td: e.dma_start(td.ap(), outT_sb[:]),
                      waits=outT_evs, tag="tap_outT0")
            for k in range(4):
                it = 4 * w + k
                wslot = xr[k % 2]
                xa = x_tile[:, (k % 2) * D:(k % 2 + 1) * D]
                ldx = sch.q(lambda e, s=s, it=it, xa=xa: e.dma_start(
                    xa, x_d[s, it * 128:(it + 1) * 128, :]),
                    waits=wslot.pwaits(), tag=f"xr{s}_{it}")
                wslot.produced(ldx)
                pbs = [lbk[0], lbk[1]]
                for half in range(2):
                    pbh = pbs[half]
                    mms = []
                    for p in range(4):
                        mm = sch.t(lambda e, p=p, k=k, half=half, pbh=pbh: e.matmul(
                            pbh.ap[:, 0:512],
                            outT_sb[:, p * 512 + k * 128: p * 512 + (k + 1) * 128],
                            wo_l[:, p * D + half * 512: p * D + (half + 1) * 512],
                            start=(p == 0), stop=(p == 3)),
                            waits=[outT_evs[p]] + (pbh.wwaits() if p == 0 else []),
                            tag=f"pr{s}w{w}k{k}h{half}p{p}")
                        mms.append(mm)
                    pbh.set_writer(mms[-1])
                    outT_slot.consumed(mms[-1])
                radd = []
                for half in range(2):
                    pbh = pbs[half]
                    ra = sch.v(lambda e, half=half, xa=xa, pbh=pbh: e.scalar_tensor_tensor(
                        sq_scr[:, half * 512:(half + 1) * 512],
                        xa[:, half * 512:(half + 1) * 512], 1.0, pbh.ap[:, 0:512],
                        op0=ALU.mult, op1=ALU.add),
                        waits=[ldx] + pbh.rwaits() + (rsq.pwaits() if half == 0 else []),
                        tag=f"ra{s}w{w}k{k}h{half}")
                    pbh.add_reader(ra)
                    radd.append(ra)
                wslot.consumed(radd[-1])
                rsq.produced(radd[-1])
                if s == 0 and w == 0 and k == 0 and "r0" in taps:
                    td = mktap("r0", [128, D])
                    sch.q(lambda e, td=td: e.dma_start(td.ap(), sq_scr[:, 0:D]),
                          waits=radd, tag="tap_r0")
                s1_ = sch.s(lambda e: e.activation(
                    dump_scr[:, 0:D], sq_scr[:, 0:D], AF.Copy,
                    accum_out=stat[:, 11:12]), waits=radd, tag=f"lns{s}w{w}k{k}")
                s2_ = sch.s(lambda e: e.activation(
                    dump_scr[:, 0:D], sq_scr[:, 0:D], AF.Square,
                    accum_out=stat[:, 12:13]), waits=[s1_], tag=f"lnq{s}w{w}k{k}")
                m8 = sch.v(lambda e: e.tensor_scalar(
                    stat[:, 16:24], stat[:, 11:12].broadcast_to([128, 8]), 1.0 / D,
                    None, op0=ALU.mult), waits=[s2_], tag=f"lnm{s}w{w}k{k}")
                v8 = sch.v(lambda e: e.tensor_scalar(
                    stat[:, 24:32], stat[:, 12:13].broadcast_to([128, 8]), 1.0 / D,
                    None, op0=ALU.mult), waits=[s2_], tag=f"lnv{s}w{w}k{k}")
                msq = sch.v(lambda e: e.tensor_tensor(
                    stat[:, 32:40], stat[:, 16:24], stat[:, 16:24], op=ALU.mult),
                    waits=[m8], tag=f"lnw{s}w{w}k{k}")
                var8 = sch.v(lambda e: e.tensor_tensor(
                    stat[:, 40:48], stat[:, 24:32], stat[:, 32:40], op=ALU.subtract),
                    waits=[v8, msq], tag=f"lnx{s}w{w}k{k}")
                rstd = rsqrt_col(var8, 40, 14, LN_EPS, f"lnr{s}w{w}k{k}", scratch_col=15)
                nb8 = sch.v(lambda e: e.tensor_scalar(
                    stat[:, 32:40], stat[:, 16:24], stat[:, 14:15], -1.0,
                    op0=ALU.mult, op1=ALU.mult), waits=[rstd], tag=f"lnb{s}w{w}k{k}")
                if s == 0 and w == 0 and k == 0 and "st0" in taps:
                    td = mktap("st0", [128, 64])
                    sch.q(lambda e, td=td: e.dma_start(td.ap(), stat[:]),
                          waits=[nb8], tag="tap_st0")
                ysl = y1slot[k % 2]
                ya = y1_stage[:, (k % 2) * D:(k % 2 + 1) * D]
                nrm = sch.v(lambda e, ya=ya: e.tensor_scalar(
                    ya, sq_scr[:, 0:D], stat[:, 14:15], stat[:, 32:33],
                    op0=ALU.mult, op1=ALU.add),
                    waits=[nb8] + ysl.pwaits(), tag=f"lnn{s}w{w}k{k}")
                rsq.consumed(nrm)
                ysl.produced(nrm)
                sty = sch.q(lambda e, it=it, ya=ya: e.dma_start(
                    y1_d[it * 128:(it + 1) * 128, :], ya),
                    waits=[nrm], tag=f"y1{s}_{it}")
                ysl.consumed(sty)
                a4_done.append(sty)
        if "y1" in taps and s == 0:
            td = mktap("y1", [N, D])
            a4_done.append(sch.q(lambda e, td=td: e.dma_start(td.ap(), y1_d.ap()),
                                 waits=list(a4_done), tag="tap_y1"))
        prev_seq_done = a2_done + a4_done
        return prev_seq_done

    def phase_ffn(s):
        nonlocal prev_seq_done
        deps = list(prev_seq_done)
        ffn_done = []
        ldmw = sch.q(lambda e: e.dma_start(
            meanw_sb[:, 0:8], scal_d[0:1, 0:8].broadcast_to([128, 8])),
            waits=deps, tag=f"Fmw{s}")
        cm_ops = []
        for j, cnt in ((0, INNER * D), (1, INNER * D), (2, D * INNER)):
            cm_ops.append(sch.v(lambda e, j=j, cnt=cnt: e.tensor_scalar(
                meanw_sb[:, 8 + 8 * j:16 + 8 * j],
                meanw_sb[:, j:j + 1].broadcast_to([128, 8]),
                1.0 / cnt, QEPS, op0=ALU.mult, op1=ALU.max),
                waits=[ldmw], tag=f"Fcmw{s}_{j}"))
        ldw_ops = []
        for mat_d, dst in ((wgT_d, wgT_sb), (wuT_d, wuT_sb)):
            for dc in range(DC):
                ldw_ops.append(sch.q(lambda e, mat_d=mat_d, dst=dst, dc=dc: e.dma_start(
                    dst[:, dc * INNER:(dc + 1) * INNER],
                    mat_d[dc * 128:(dc + 1) * 128, :]),
                    waits=deps if dc == 0 else [], tag=f"F{s}w{dc}"))
        y1s = [Slot(), Slot()]
        xq2Ts = Slot()
        oslot = [Slot(), Slot()]
        wdslot = [Slot(), Slot()]
        hbanks = banks[0:7]
        sbank = banks[7]
        tpb = banks[4]

        def emit_ln2(r_waits, out_ap, tag):
            s1_ = sch.s(lambda e: e.activation(
                dump_scr[:, 0:D], sq_scr[:, 0:D], AF.Copy,
                accum_out=stat[:, 11:12]), waits=r_waits, tag=tag + "s1")
            s2_ = sch.s(lambda e: e.activation(
                dump_scr[:, 0:D], sq_scr[:, 0:D], AF.Square,
                accum_out=stat[:, 12:13]), waits=[s1_], tag=tag + "s2")
            m8 = sch.v(lambda e: e.tensor_scalar(
                stat[:, 16:24], stat[:, 11:12].broadcast_to([128, 8]), 1.0 / D,
                None, op0=ALU.mult), waits=[s2_], tag=tag + "m8")
            v8 = sch.v(lambda e: e.tensor_scalar(
                stat[:, 24:32], stat[:, 12:13].broadcast_to([128, 8]), 1.0 / D,
                None, op0=ALU.mult), waits=[s2_], tag=tag + "v8")
            msq = sch.v(lambda e: e.tensor_tensor(
                stat[:, 32:40], stat[:, 16:24], stat[:, 16:24], op=ALU.mult),
                waits=[m8], tag=tag + "msq")
            var8 = sch.v(lambda e: e.tensor_tensor(
                stat[:, 40:48], stat[:, 24:32], stat[:, 32:40], op=ALU.subtract),
                waits=[v8, msq], tag=tag + "var")
            rstd = rsqrt_col(var8, 40, 14, LN_EPS, tag + "rs", scratch_col=15)
            nb8 = sch.v(lambda e: e.tensor_scalar(
                stat[:, 32:40], stat[:, 16:24], stat[:, 14:15], -1.0,
                op0=ALU.mult, op1=ALU.mult), waits=[rstd], tag=tag + "nb")
            return sch.v(lambda e, out_ap=out_ap: e.tensor_scalar(
                out_ap, sq_scr[:, 0:D], stat[:, 14:15], stat[:, 32:33],
                op0=ALU.mult, op1=ALU.add), waits=[nb8], tag=tag + "nrm")

        for grp in range(NT // 2):
            tile_quant_done = []
            for ttn in range(2):
                ti = grp * 2 + ttn
                ya = y1t[:, ttn * D:(ttn + 1) * D]
                ldy = sch.q(lambda e, ti=ti, ya=ya: e.dma_start(
                    ya, y1_d[ti * 128:(ti + 1) * 128, :]),
                    waits=deps + y1s[ttn].pwaits(), tag=f"Fy{s}_{ti}")
                y1s[ttn].produced(ldy)
                q1a = sch.v(lambda e, ya=ya: e.tensor_scalar(
                    dump_scr[:, 0:D], ya, 1.0, None, op0=ALU.mult, op1=ALU.max,
                    accum_out=stat[:, 20:21]), waits=[ldy], tag=f"Fq1a{s}_{ti}")
                q1b = sch.v(lambda e, ya=ya: e.tensor_scalar(
                    dump_scr[:, 0:D], ya, -1.0, None, op0=ALU.mult, op1=ALU.max,
                    accum_out=stat[:, 21:22]), waits=[ldy], tag=f"Fq1b{s}_{ti}")
                y1s[ttn].consumed(q1a)
                y1s[ttn].consumed(q1b)
                rm2 = sch.v(lambda e: e.tensor_scalar(
                    stat[:, 0:8], stat[:, 20:21].broadcast_to([128, 8]),
                    stat[:, 21:22], QEPS, op0=ALU.max, op1=ALU.max),
                    waits=[q1a, q1b], tag=f"Frm{s}_{ti}")
                l1 = sch.s(lambda e: e.activation(stat[:, 9:10], stat[:, 0:1], AF.Ln),
                           waits=[rm2], tag=f"Fl1{s}_{ti}")
                l2 = sch.s(lambda e: e.activation(stat[:, 10:11], stat[:, 9:10],
                                                  AF.Exp, scale=-1.0),
                           waits=[l1], tag=f"Fl2{s}_{ti}")
                sx = sch.s(lambda e: e.mul(stat[:, 9:10], stat[:, 10:11], 127.0),
                           waits=[l2], tag=f"Fsx{s}_{ti}")
                cg = sch.v(lambda e: e.tensor_scalar(
                    stat[:, 24:32], stat[:, 0:1].broadcast_to([128, 8]),
                    meanw_sb[:, 8:9], 1.0 / 127.0, op0=ALU.mult, op1=ALU.mult),
                    waits=[rm2] + cm_ops, tag=f"Fcg{s}_{ti}")
                cu = sch.v(lambda e: e.tensor_scalar(
                    stat[:, 32:40], stat[:, 0:1].broadcast_to([128, 8]),
                    meanw_sb[:, 16:17], 1.0 / 127.0, op0=ALU.mult, op1=ALU.mult),
                    waits=[rm2] + cm_ops, tag=f"Fcu{s}_{ti}")
                xq_ops = []
                for half in range(2):
                    qi = sch.v(lambda e, ya=ya, half=half: e.tensor_scalar(
                        xqi[:, 0:512], ya[:, half * 512:(half + 1) * 512],
                        stat[:, 9:10], None, op0=ALU.mult),
                        waits=[sx] + xq_ops[-1:], tag=f"Fxi{s}_{ti}_{half}")
                    y1s[ttn].consumed(qi)
                    qb = sch.v(lambda e, half=half: e.tensor_copy(
                        xqb[:, half * 512:(half + 1) * 512], xqi[:, 0:512]),
                        waits=[qi], tag=f"Fxb{s}_{ti}_{half}")
                    xq_ops.append(qb)
                tdst = tpb.ap.bitcast(BF16)
                tms = []
                for dc in range(DC):
                    tm = sch.t(lambda e, dc=dc, tdst=tdst: e.transpose(
                        tdst[:, dc * 128:(dc + 1) * 128],
                        xqb[:, dc * 128:(dc + 1) * 128], ident_b[:]),
                        waits=xq_ops + (tpb.wwaits() if dc == 0 else []),
                        tag=f"FxT{s}_{ti}_{dc}")
                    tms.append(tm)
                tpb.set_writer(tms[-1])
                xTe = sch.v(lambda e, tdst=tdst: e.tensor_copy(xqT[:, 0:1024],
                                                               tdst[:, 0:1024]),
                            waits=tpb.rwaits(), tag=f"FxTe{s}_{ti}")
                tpb.add_reader(xTe)
                hready = []
                trb = [banks[6], banks[7]]

                def emit_g(ob):
                    tb_ = trb[ob % 2]
                    gms = []
                    for dc in range(DC):
                        gm = sch.t(lambda e, dc=dc, ob=ob, tb_=tb_: e.matmul(
                            tb_.ap[:, 0:512],
                            xqT[:, dc * 128:(dc + 1) * 128],
                            wgT_sb[:, dc * INNER + ob * 512: dc * INNER + (ob + 1) * 512],
                            start=(dc == 0), stop=(dc == DC - 1)),
                            waits=([xTe] + ldw_ops + tb_.wwaits()) if dc == 0 else [],
                            tag=f"Fg{s}_{ti}_{ob}_{dc}")
                        gms.append(gm)
                    tb_.set_writer(gms[-1])
                    sil = sch.s(lambda e, ob=ob, tb_=tb_: e.activation(
                        sg_sb[:, (ob % 2) * 512:(ob % 2) * 512 + 512],
                        tb_.ap[:, 0:512], AF.Silu,
                        scale=stat[:, 24:25]), waits=tb_.rwaits() + [cg],
                        tag=f"Fsl{s}_{ti}_{ob}")
                    tb_.add_reader(sil)
                    return sil

                def emit_u(ob, sil):
                    tb_ = trb[ob % 2]
                    ums = []
                    for dc in range(DC):
                        um = sch.t(lambda e, dc=dc, ob=ob, tb_=tb_: e.matmul(
                            tb_.ap[:, 0:512],
                            xqT[:, dc * 128:(dc + 1) * 128],
                            wuT_sb[:, dc * INNER + ob * 512: dc * INNER + (ob + 1) * 512],
                            start=(dc == 0), stop=(dc == DC - 1)),
                            waits=tb_.wwaits() if dc == 0 else [],
                            tag=f"Fu{s}_{ti}_{ob}_{dc}")
                        ums.append(um)
                    tb_.set_writer(ums[-1])
                    if ob < 6:
                        hdst = hbanks[ob]
                        hm = sch.v(lambda e, hdst=hdst, ob=ob, tb_=tb_: e.scalar_tensor_tensor(
                            hdst.ap[:, 0:512],
                            sg_sb[:, (ob % 2) * 512:(ob % 2) * 512 + 512],
                            stat[:, 32:33], tb_.ap[:, 0:512],
                            op0=ALU.mult, op1=ALU.mult),
                            waits=[sil, cu] + tb_.rwaits() + hdst.wwaits() + hdst.rwaits(),
                            tag=f"Fh{s}_{ti}_{ob}")
                        hdst.set_writer(hm)
                        tb_.add_reader(hm)
                    else:
                        hsb = h6_sb if ob == 6 else h7_sb
                        hm = sch.v(lambda e, hsb=hsb, ob=ob, tb_=tb_: e.scalar_tensor_tensor(
                            hsb[:, 0:512],
                            sg_sb[:, (ob % 2) * 512:(ob % 2) * 512 + 512],
                            stat[:, 32:33], tb_.ap[:, 0:512],
                            op0=ALU.mult, op1=ALU.mult),
                            waits=[sil, cu] + tb_.rwaits(), tag=f"Fh{s}_{ti}_{ob}")
                        tb_.add_reader(hm)
                    return hm

                sils = {}
                for kp in range(0, 8, 2):
                    sils[kp] = emit_g(kp)
                    sils[kp + 1] = emit_g(kp + 1)
                    hready.append(emit_u(kp, sils[kp]))
                    hready.append(emit_u(kp + 1, sils[kp + 1]))
                pm_ops = []
                def hsrc_of(ob):
                    if ob < 6:
                        return hbanks[ob].ap
                    return h6_sb if ob == 6 else h7_sb
                for ob in range(8):
                    hsrc = hsrc_of(ob)
                    for sgn in range(2):
                        pm = sch.v(lambda e, hsrc=hsrc, ob=ob, sgn=sgn: e.tensor_scalar(
                            dump_scr[:, 0:512], hsrc[:, 0:512],
                            1.0 if sgn == 0 else -1.0, None,
                            op0=ALU.mult, op1=ALU.max,
                            accum_out=attn_scr[:, 8 * sgn + ob:8 * sgn + ob + 1]),
                            waits=[hready[ob]], tag=f"Fp{s}_{ti}_{ob}_{sgn}")
                        if ob < 6:
                            hbanks[ob].add_reader(pm)
                        pm_ops.append(pm)
                cmb = sch.v(lambda e: e.tensor_scalar(
                    dump_scr[:, 0:16], attn_scr[:, 0:16], 1.0, None,
                    op0=ALU.mult, op1=ALU.max, accum_out=stat[:, 20:21]),
                    waits=pm_ops, tag=f"Fcm{s}_{ti}")
                rm2b = sch.v(lambda e: e.tensor_scalar(
                    stat[:, 0:8], stat[:, 20:21].broadcast_to([128, 8]), QEPS, None,
                    op0=ALU.max), waits=[cmb], tag=f"Frm2{s}_{ti}")
                b1 = sch.s(lambda e: e.activation(stat[:, 10:11], stat[:, 0:1], AF.Ln),
                           waits=[rm2b], tag=f"Fb1{s}_{ti}")
                b2 = sch.s(lambda e, ttn=ttn: e.activation(
                    stat[:, 48 + 4 * ttn:49 + 4 * ttn], stat[:, 10:11], AF.Exp,
                    scale=-1.0), waits=[b1], tag=f"Fb2{s}_{ti}")
                s2f = sch.s(lambda e, ttn=ttn: e.mul(
                    stat[:, 48 + 4 * ttn:49 + 4 * ttn],
                    stat[:, 48 + 4 * ttn:49 + 4 * ttn], 127.0),
                    waits=[b2], tag=f"Fs2{s}_{ti}")
                cd1 = sch.s(lambda e, ttn=ttn: e.mul(
                    stat[:, 49 + 4 * ttn:50 + 4 * ttn], stat[:, 0:1],
                    1.0 / 127.0), waits=[rm2b], tag=f"Fcd1{s}_{ti}")
                cdn = sch.s(lambda e, ttn=ttn: e.mul(
                    stat[:, 49 + 4 * ttn:50 + 4 * ttn],
                    stat[:, 49 + 4 * ttn:50 + 4 * ttn], meanw_sb[:, 24:25]),
                    waits=[cd1] + cm_ops, tag=f"Fcd2{s}_{ti}")
                ffn_done.append(cdn)
                for ob in range(8):
                    hsrc = hsrc_of(ob)
                    qi2 = sch.v(lambda e, hsrc=hsrc, ttn=ttn: e.tensor_scalar(
                        xqi[:, 0:512], hsrc[:, 0:512],
                        stat[:, 48 + 4 * ttn:49 + 4 * ttn], None, op0=ALU.mult),
                        waits=[s2f], tag=f"Fq2{s}_{ti}_{ob}")
                    if ob < 6:
                        hbanks[ob].add_reader(qi2)
                    qb2 = sch.v(lambda e: e.tensor_copy(xq2b[:, 0:512], xqi[:, 0:512]),
                                waits=[qi2], tag=f"Fqb2{s}_{ti}_{ob}")
                    tb2 = hbanks[ob] if ob < 6 else trb[ob % 2]
                    tdst2 = tb2.ap.bitcast(BF16)
                    tms2 = []
                    for c4 in range(4):
                        tm2 = sch.t(lambda e, c4=c4, tdst2=tdst2: e.transpose(
                            tdst2[:, c4 * 128:(c4 + 1) * 128],
                            xq2b[:, c4 * 128:(c4 + 1) * 128], ident_b[:]),
                            waits=[qb2] + (tb2.wwaits() if c4 == 0 else []),
                            tag=f"FqT{s}_{ti}_{ob}_{c4}")
                        tms2.append(tm2)
                    tb2.set_writer(tms2[-1])
                    t2e = sch.v(lambda e, ttn=ttn, ob=ob, tdst2=tdst2: e.tensor_copy(
                        xq2T[:, ttn * 4096 + ob * 512: ttn * 4096 + (ob + 1) * 512],
                        tdst2[:, 0:512]),
                        waits=tb2.rwaits() + (xq2Ts.pwaits() if ttn == 0 and ob == 0 else []),
                        tag=f"FqTe{s}_{ti}_{ob}")
                    tb2.add_reader(t2e)
                    tile_quant_done.append(t2e)
                if s == 0 and grp == 0 and ttn == 0 and "ffn0" in taps:
                    td = mktap("f_xqb", [128, D], BF16)
                    sch.q(lambda e, td=td: e.dma_start(td.ap(), xqb[:]),
                          waits=list(tile_quant_done), tag="tap_fxqb")
                    td2 = mktap("f_xq2T", [128, 8192], BF16)
                    sch.q(lambda e, td2=td2: e.dma_start(td2.ap(), xq2T[:]),
                          waits=list(tile_quant_done), tag="tap_fxq2T")
                    td3 = mktap("f_y1t", [128, 2 * D])
                    sch.q(lambda e, td3=td3: e.dma_start(td3.ap(), y1t[:]),
                          waits=list(tile_quant_done), tag="tap_fy1t")
                    td4 = mktap("f_stat", [128, 64])
                    sch.q(lambda e, td4=td4: e.dma_start(td4.ap(), stat[:]),
                          waits=list(tile_quant_done), tag="tap_fstat")
                    td5 = mktap("f_meanw", [128, 32])
                    sch.q(lambda e, td5=td5: e.dma_start(td5.ap(), meanw_sb[:]),
                          waits=list(tile_quant_done), tag="tap_fmeanw")
            xq2Ts.produced(tile_quant_done[-1])
            if False and "ffn0" in taps:
                td = mktap("f_xqb", [128, D], BF16)
                sch.q(lambda e, td=td: e.dma_start(td.ap(), xqb[:]),
                      waits=tile_quant_done, tag="tap_fxqb")
                td2 = mktap("f_xq2T", [128, 8192], BF16)
                sch.q(lambda e, td2=td2: e.dma_start(td2.ap(), xq2T[:]),
                      waits=tile_quant_done, tag="tap_fxq2T")
                td3 = mktap("f_y1t", [128, 2 * D])
                sch.q(lambda e, td3=td3: e.dma_start(td3.ap(), y1t[:]),
                      waits=tile_quant_done, tag="tap_fy1t")
                td4 = mktap("f_stat", [128, 64])
                sch.q(lambda e, td4=td4: e.dma_start(td4.ap(), stat[:]),
                      waits=tile_quant_done, tag="tap_fstat")
            dbanks = [(banks[0], banks[1]), (banks[2], banks[3])]
            for ttn in range(2):
                for half in range(2):
                    b = dbanks[ttn][half]
            dmms = {0: [], 1: []}
            for oc in range(32):
                wsl = wdslot[oc % 2]
                ldw = sch.emit("gpsimd", lambda e, oc=oc: e.dma_start(
                    wdb[:, (oc % 2) * D:(oc % 2 + 1) * D],
                    wdT_d[oc * 128:(oc + 1) * 128, :]),
                    waits=wsl.pwaits() + (tile_quant_done if oc < 2 else []),
                    dma=True, tag=f"Fwd{s}_{grp}_{oc}", semkey=f"wd{oc % 2}")
                wsl.produced(ldw)
                for ttn in range(2):
                    for half in range(2):
                        b = dbanks[ttn][half]
                        mm = sch.t(lambda e, oc=oc, ttn=ttn, half=half, b=b: e.matmul(
                            b.ap[:, 0:512],
                            xq2T[:, ttn * 4096 + oc * 128: ttn * 4096 + (oc + 1) * 128],
                            wdb[:, (oc % 2) * D + half * 512:(oc % 2) * D + (half + 1) * 512],
                            start=(oc == 0), stop=(oc == 31)),
                            waits=[ldw] + tile_quant_done +
                                  (b.wwaits() if oc == 0 else []),
                            tag=f"Fd{s}_{grp}_{oc}_{ttn}_{half}")
                        if oc == 0:
                            b.set_writer(mm)
                        else:
                            b.writer = mm
                        dmms[ttn].append(mm)
                        wsl.consumed(mm)
                        xq2Ts.consumed(mm)
            for ttn in range(2):
                ti = grp * 2 + ttn
                ya = y1t[:, ttn * D:(ttn + 1) * D]
                ev_ops = []
                for half in range(2):
                    b = dbanks[ttn][half]
                    e1 = sch.v(lambda e, b=b, half=half, ttn=ttn: e.tensor_scalar(
                        sq_scr[:, half * 512:(half + 1) * 512], b.ap[:, 0:512],
                        stat[:, 49 + 4 * ttn:50 + 4 * ttn], None, op0=ALU.mult),
                        waits=[dmms[ttn][-1]] + rsq.pwaits() if False else
                              [dmms[ttn][-1]], tag=f"Fe{s}_{ti}_{half}")
                    b.add_reader(e1)
                    e2 = sch.v(lambda e, half=half, ya=ya: e.tensor_tensor(
                        sq_scr[:, half * 512:(half + 1) * 512],
                        sq_scr[:, half * 512:(half + 1) * 512],
                        ya[:, half * 512:(half + 1) * 512], op=ALU.add),
                        waits=[e1], tag=f"Fr{s}_{ti}_{half}")
                    y1s[ttn].consumed(e2)
                    ev_ops.append(e2)
                osl = oslot[0]
                oa = o_stage[:, 0:D]
                nrm2 = emit_ln2(ev_ops + osl.pwaits(), oa, f"FL{s}_{ti}")
                osl.produced(nrm2)
                sto = sch.q(lambda e, s=s, ti=ti, oa=oa: e.dma_start(
                    out_d[s, ti * 128:(ti + 1) * 128, :], oa),
                    waits=[nrm2], tag=f"out{s}_{ti}")
                osl.consumed(sto)
                ffn_done.append(sto)
        prev_seq_done = ffn_done
        return ffn_done

    for s in range(SEQ_PER_CORE if stop_after != "A2_seq0" else 1):
        if stop_after in ("consts", "prep1", "prep2", "prepall"):
            break
        phase_attn(s)
        if stop_after in ("A2", "A2_seq0"):
            break
        phase_ffn(s)

    if stop_after is not None:
        zfill = sch.g(lambda e: e.memset(sq_scr[:], 0.0), dma=False,
                      waits=prev_seq_done, tag="zf")
        for s in range(SEQ_PER_CORE):
            for ti in range(NT):
                sch.q(lambda e, s=s, ti=ti: e.dma_start(
                    out_d[s, ti * 128:(ti + 1) * 128, :], sq_scr[:]),
                    waits=[zfill], tag=f"zero{s}_{ti}")

    sch.finalize(nc)
    return nc, tap_d


def kernel(L0_attn_tokens, L0_attn_wq, L0_attn_wk, L0_attn_wv, L0_attn_wo,
           L0_attn_w_lr, L0_attn_w_target, L0_attn_w_gates, L0_attn_rms_w,
           L0_ffn_w_gate, L0_ffn_w_up, L0_ffn_w_down):
    x = np.ascontiguousarray(np.asarray(L0_attn_tokens, dtype=np.float32))
    consts = _host_consts()
    nc, tap_d = build_program(taps=TAPS, serial=SERIAL, stop_after=STOP_AFTER)
    in_maps = []
    for c in range(NCORES):
        m = {
            "x": x[c * SEQ_PER_CORE:(c + 1) * SEQ_PER_CORE],
            "wq": np.ascontiguousarray(np.asarray(L0_attn_wq, np.float32)),
            "wk": np.ascontiguousarray(np.asarray(L0_attn_wk, np.float32)),
            "wv": np.ascontiguousarray(np.asarray(L0_attn_wv, np.float32)),
            "wo": np.ascontiguousarray(np.asarray(L0_attn_wo, np.float32)),
            "wgates": np.ascontiguousarray(np.asarray(L0_attn_w_gates, np.float32)),
            "rmsw": np.ascontiguousarray(np.asarray(L0_attn_rms_w, np.float32)).reshape(1, D),
            "wgate": np.ascontiguousarray(np.asarray(L0_ffn_w_gate, np.float32)),
            "wup": np.ascontiguousarray(np.asarray(L0_ffn_w_up, np.float32)),
            "wdown": np.ascontiguousarray(np.asarray(L0_ffn_w_down, np.float32)),
        }
        m.update(consts)
        in_maps.append(m)
    res = run_bass_kernel_spmd(nc, in_maps, core_ids=list(range(NCORES)))
    kernel._last_results = res
    out = np.concatenate([np.asarray(r["out"]) for r in res.results], axis=0)
    return out
